# version 13
# speedup vs baseline: 1.1221x; 1.0445x over previous
import numpy as np
import ml_dtypes
from contextlib import ExitStack

import concourse.bass as bass
import concourse.mybir as mybir
from concourse.bass_utils import run_bass_kernel_spmd

F32 = mybir.dt.float32
BF16 = mybir.dt.bfloat16
I32 = mybir.dt.int32
AF = mybir.ActivationFunctionType
OP = mybir.AluOpType
AX = mybir.AxisListType
BF = ml_dtypes.bfloat16

NCORES = 8
D = 2048
S = 8192
B = 2
H = 16
DH = 128
TPC = 2048
NTT = TPC // 128
NDC = D // 128

EPS = 1e-5
CEXP = 16.0


class Sem:
    __slots__ = ("h", "n")

    def __init__(self, h):
        self.h = h
        self.n = 0

    def bump(self, k=1):
        self.n += k
        return self.n


class Chain:

    def __init__(self, p, name=None):
        p._semi += 1
        self.h = p.ctx.enter_context(p.nc.semaphore(name or f"ch{p._semi}"))
        self.n = 0

    def step(self, e, instr):
        self.n += 1
        instr.then_inc(self.h, 1)
        e.wait_ge(self.h, self.n)
        return instr


class Prog:

    def __init__(self, name="prog", pool_sems=False):
        self.nc = bass.Bass("TRN2", target_bir_lowering=False, debug=False)
        self.ctx = ExitStack()
        self.streams = {k: [] for k in ("sync", "vector", "tensor", "scalar", "gpsimd")}
        self._semi = 0
        self._namei = 0
        self._scopes = []
        self._pool_sems = pool_sems
        self._gen_pool = []
        self._gen_next = 0
        self._swdge_pool = []
        self._swdge_next = 0
        self._pb = None
        self._pb_n = 0

    def _newsem(self, name=None):
        self._semi += 1
        return Sem(self.ctx.enter_context(self.nc.semaphore(name or f"s{self._semi}")))

    GEN_CAP = 60
    SWDGE_CAP = 24

    def sem(self, name=None):
        if not self._pool_sems:
            return self._newsem(name)
        if len(self._gen_pool) < self.GEN_CAP:
            s = self._newsem(name)
            self._gen_pool.append(s)
            return s
        s = self._gen_pool[self._gen_next % self.GEN_CAP]
        self._gen_next += 1
        return s

    def swdge_sem(self, name=None):
        if not self._pool_sems:
            return self._newsem(name)
        if len(self._swdge_pool) < self.SWDGE_CAP:
            s = self._newsem(name)
            self._swdge_pool.append(s)
            return s
        s = self._swdge_pool[self._swdge_next % self.SWDGE_CAP]
        self._swdge_next += 1
        return s

    def _uname(self, name):
        self._namei += 1
        return f"{name}_{self._namei}"

    def push_scope(self):
        s = ExitStack()
        self._scopes.append(s)
        return s

    def pop_scope(self):
        self._scopes.pop().close()

    def _tgt(self):
        return self._scopes[-1] if self._scopes else self.ctx

    def sb(self, name, shape, dt=F32):
        return self._tgt().enter_context(self.nc.sbuf_tensor(self._uname(name), shape, dt))

    def scope(self):
        return ExitStack()

    def sbs(self, scope, name, shape, dt=F32):
        return scope.enter_context(self.nc.sbuf_tensor(self._uname(name), shape, dt))

    def ps(self, name, shape, dt=F32):
        return self._tgt().enter_context(self.nc.psum_tensor(self._uname(name), shape, dt))

    def din(self, name, shape, dt):
        return self.nc.dram_tensor(name, shape, dt, kind="ExternalInput").ap()

    def dout(self, name, shape, dt):
        return self.nc.dram_tensor(name, shape, dt, kind="ExternalOutput").ap()

    def dtmp(self, name, shape, dt):
        return self.nc.dram_tensor(name, shape, dt, kind="Internal").ap()

    def phase_barrier(self):
        if self._pb is None:
            self._pb = self._newsem("pb")
        self._pb_n += 5
        tgt = self._pb_n
        pb = self._pb
        for st in ("sync", "vector", "tensor", "scalar", "gpsimd"):
            self.streams[st].append(
                lambda e, tgt=tgt: (e.nop().then_inc(pb.h, 1),
                                    e.wait_ge(pb.h, tgt)))

    def sy(self, f):
        self.streams["sync"].append(f)

    def ve(self, f):
        self.streams["vector"].append(f)

    def te(self, f):
        self.streams["tensor"].append(f)

    def sc(self, f):
        self.streams["scalar"].append(f)

    def gp(self, f):
        self.streams["gpsimd"].append(f)

    def finalize(self):
        nc = self.nc
        with self.ctx, nc.Block() as block:
            block.sync(lambda e: [f(e) for f in self.streams["sync"]])
            block.vector(lambda e: [f(e) for f in self.streams["vector"]])
            block.tensor(lambda e: [f(e) for f in self.streams["tensor"]])
            block.scalar(lambda e: [f(e) for f in self.streams["scalar"]])
            block.gpsimd(lambda e: [f(e) for f in self.streams["gpsimd"]])
        return nc


def emit_preln(p, x_in, lnT_sb, idn_sb, out_dram=None, tag="ln", st_sem=None,
               tp_pool=None, xt_buf=None, lnt_buf=None, alloc=None):
    nc = p.nc
    alloc = alloc or p.sb
    xt = xt_buf if xt_buf is not None else alloc(tag + "_xt", [128, 2 * D], F32)
    lnt = lnt_buf if lnt_buf is not None else alloc(tag + "_lnt", [128, 2 * D], BF16)
    bn24 = alloc(tag + "_bn", [128, 24], F32)
    mv = alloc(tag + "_mv", [128, 2], F32)
    t1 = alloc(tag + "_t1", [128, 1], F32)
    t2 = alloc(tag + "_t2", [128, 1], F32)
    rstd = alloc(tag + "_rs", [128, 1], F32)
    nmr = alloc(tag + "_nm", [128, 1], F32)
    tp = tp_pool or [p.ps(tag + f"_tp{i}", [128, 1024], BF16) for i in range(2)]

    ld = p.sem()
    vs = p.sem()
    ss = p.sem()
    tt = p.sem()
    ev = p.sem()
    done = st_sem or p.sem()
    ss0, tt0, ev0 = ss.n, tt.n, ev.n

    for j in range(NTT):
        buf = j % 2
        nld = ld.bump(16)
        war = ss0 + 2 * (j - 2) + 2
        p.sy(lambda e, j=j, buf=buf, nld=nld, war=war: (
            (e.wait_ge(ss.h, war) if war > ss0 else None),
            e.dma_start(out=xt[:, buf * D:(buf + 1) * D],
                        in_=x_in[128 * j:128 * (j + 1), :]).then_inc(ld.h, 16)))

        nvs1 = vs.bump()
        p.ve(lambda e, buf=buf, nld=nld, nvs1=nvs1: (
            e.wait_ge(ld.h, nld),
            e.bn_stats(bn24[:, 0:6], xt[:, buf * D + 0:buf * D + 512]),
            e.bn_stats(bn24[:, 6:12], xt[:, buf * D + 512:buf * D + 1024]),
            e.bn_stats(bn24[:, 12:18], xt[:, buf * D + 1024:buf * D + 1536]),
            e.bn_stats(bn24[:, 18:24], xt[:, buf * D + 1536:buf * D + 2048]).then_inc(vs.h, 1)))
        nvs2 = vs.bump()
        p.ve(lambda e, nvs1=nvs1: (
            e.wait_ge(vs.h, nvs1),
            e.bn_aggr(mv[:], bn24[:]).then_inc(vs.h, 1)))
        nvs3 = vs.bump()
        p.ve(lambda e, nvs2=nvs2: (
            e.wait_ge(vs.h, nvs2),
            e.tensor_scalar(t1[:], mv[:, 1:2], EPS, None, OP.add).then_inc(vs.h, 1)))
        nvs4 = vs.bump()
        p.ve(lambda e, nvs3=nvs3: (
            e.wait_ge(vs.h, nvs3),
            e.reciprocal(t2[:], t1[:]).then_inc(vs.h, 1)))
        nss1 = ss.bump()
        p.sc(lambda e, nvs4=nvs4: (
            e.wait_ge(vs.h, nvs4),
            e.sqrt(rstd[:], t2[:]).then_inc(ss.h, 1)))
        nvs5 = vs.bump()
        p.ve(lambda e, nss1=nss1: (
            e.wait_ge(ss.h, nss1),
            e.scalar_tensor_tensor(nmr[:], mv[:, 0:1], -1.0, rstd[:],
                                   OP.mult, OP.mult).then_inc(vs.h, 1)))
        ntt_war = tt0 + 2 * (j - 1)
        nss2 = ss.bump()
        p.sc(lambda e, buf=buf, nvs5=nvs5, w=ntt_war: (
            e.wait_ge(vs.h, nvs5),
            (e.wait_ge(tt.h, w) if w > tt0 else None),
            e.activation(lnt[:, buf * D:(buf + 1) * D], xt[:, buf * D:(buf + 1) * D],
                         AF.Identity, bias=nmr[:, 0:1], scale=rstd[:, 0:1]).then_inc(ss.h, 1)))
        for half in range(2):
            ntt1 = tt.bump()
            p.te(lambda e, j=j, buf=buf, half=half, nss2=nss2, ntt1=ntt1: (
                e.wait_ge(ss.h, nss2),
                e.wait_ge(ev.h, ev0 + 2 * j),
                [e.transpose(tp[half][:, 128 * i:128 * (i + 1)],
                             lnt[:, buf * D + (8 * half + i) * 128: buf * D + (8 * half + i + 1) * 128],
                             idn_sb[:])
                 for i in range(8)][-1].then_inc(tt.h, 1)))
            nev = ev.bump()
            dst = lnT_sb.ap().rearrange("p (c t) -> p c t", c=NDC)[:, 8 * half:8 * half + 8, 128 * j:128 * (j + 1)]
            src = tp[half].ap().rearrange("p (c t) -> p c t", c=8)
            p.ve(lambda e, dst=dst, src=src, ntt1=ntt1: (
                e.wait_ge(tt.h, ntt1),
                e.tensor_copy(dst, src).then_inc(ev.h, 1)))

    if out_dram is not None:
        for i in range(NDC):
            nd = done.bump(16)
            p.sy(lambda e, i=i, ev_n=ev.n: (
                e.wait_ge(ev.h, ev_n),
                e.dma_start(out=out_dram[128 * i:128 * (i + 1), :],
                            in_=lnT_sb[:, i * TPC:(i + 1) * TPC]).then_inc(done.h, 16)))
    else:
        nd = done.bump()
        p.ve(lambda e, ev_n=ev.n: (e.wait_ge(ev.h, ev_n), e.engine_nop().then_inc(done.h, 1)))
    return done


def build_preln_prog():
    p = Prog("preln")
    x_in = p.din("x", [TPC, D], F32)
    lnT_out = p.dout("lnT", [D, TPC], BF16)
    idn = p.din("idn", [128, 128], BF16)
    idn_sb = p.sb("idn_sb", [128, 128], BF16)
    lnT_sb = p.sb("lnT_sb", [128, NDC * TPC], BF16)
    ld0 = p.sem()
    n0 = ld0.bump(16)
    p.sy(lambda e, n0=n0: e.dma_start(out=idn_sb[:], in_=idn).then_inc(ld0.h, 16))
    p.te(lambda e, n0=n0: e.wait_ge(ld0.h, n0))
    done = emit_preln(p, x_in, lnT_sb, idn_sb, out_dram=lnT_out)
    fin = done.n
    p.sy(lambda e, fin=fin: e.wait_ge(done.h, fin))
    return p.finalize()


class LNStats:
    def __init__(self, p, tag):
        self.p = p
        self.bn24 = p.sb(tag + "_bn", [128, 24], F32)
        self.mv = p.sb(tag + "_mv", [128, 2], F32)
        self.t1 = p.sb(tag + "_t1", [128, 1], F32)
        self.t2 = p.sb(tag + "_t2", [128, 1], F32)
        self.rstd = p.sb(tag + "_rs", [128, 1], F32)
        self.nmr = p.sb(tag + "_nm", [128, 1], F32)
        self.vs = p.sem()
        self.ss = p.sem()

    def emit(self, srcs, wait_v=None):
        p, bn24, mv = self.p, self.bn24, self.mv
        vs, ss = self.vs, self.ss
        n1 = vs.bump()
        p.ve(lambda e, srcs=srcs, wait_v=wait_v: (
            (e.wait_ge(wait_v[0].h, wait_v[1]) if wait_v else None),
            e.bn_stats(bn24[:, 0:6], srcs[0]),
            e.bn_stats(bn24[:, 6:12], srcs[1]),
            e.bn_stats(bn24[:, 12:18], srcs[2]),
            e.bn_stats(bn24[:, 18:24], srcs[3]).then_inc(vs.h, 1)))
        n2 = vs.bump()
        p.ve(lambda e, n1=n1: (
            e.wait_ge(vs.h, n1),
            e.bn_aggr(mv[:], bn24[:]).then_inc(vs.h, 1)))
        n3 = vs.bump()
        p.ve(lambda e, n2=n2: (
            e.wait_ge(vs.h, n2),
            e.tensor_scalar(self.t1[:], mv[:, 1:2], EPS, None, OP.add).then_inc(vs.h, 1)))
        n4 = vs.bump()
        p.ve(lambda e, n3=n3: (
            e.wait_ge(vs.h, n3),
            e.reciprocal(self.t2[:], self.t1[:]).then_inc(vs.h, 1)))
        ns1 = ss.bump()
        p.sc(lambda e, n4=n4: (
            e.wait_ge(vs.h, n4),
            e.sqrt(self.rstd[:], self.t2[:]).then_inc(ss.h, 1)))
        n5 = vs.bump()
        p.ve(lambda e, ns1=ns1: (
            e.wait_ge(ss.h, ns1),
            e.scalar_tensor_tensor(self.nmr[:], mv[:, 0:1], -1.0, self.rstd[:],
                                   OP.mult, OP.mult).then_inc(vs.h, 1)))
        return n5


def emit_wt_prep(p, w_in, wt_sb, O, I, idn_sb, tag, idn_wait=None, tp_pool=None):
    nc = p.nc
    noc = (O + 127) // 128
    nic = (I + 127) // 128
    wrow = p.sb(tag + "_wr", [128, 2 * I], BF16)
    tp = tp_pool or [p.ps(tag + f"_tp{i}", [128, 1024], BF16) for i in range(2)]
    glds = [p.swdge_sem(), p.swdge_sem()]
    tt = p.sem()
    ev = p.sem()
    tt0 = tt.n

    for oc in range(noc):
        buf = oc % 2
        ro = min(128, O - 128 * oc)
        gld = glds[buf]
        nld = gld.bump(16)
        nbanks = (nic + 7) // 8
        war = tt0 + (oc - 1) * nbanks
        p.gp(lambda e, oc=oc, buf=buf, ro=ro, war=war, nld=nld, tt0=tt0, gld=gld: (
            (e.wait_ge(tt.h, war) if war > tt0 else None),
            e.dma_start(out=wrow[0:ro, buf * I:buf * I + I],
                        in_=w_in[128 * oc:128 * oc + ro, :]).then_inc(gld.h, 16)))
        for half in range((nic + 7) // 8):
            ics = list(range(8 * half, min(8 * half + 8, nic)))
            tpb = (oc * nbanks + half) % 2
            ntt = tt.bump()
            evwar = ev.n - 1
            p.te(lambda e, oc=oc, buf=buf, ro=ro, ics=ics, half=half, nld=nld,
                        ntt=ntt, evwar=evwar, idn_wait=idn_wait, gld=gld, tpb=tpb: (
                e.wait_ge(gld.h, nld),
                (e.wait_ge(ev.h, evwar) if evwar > 0 else None),
                (e.wait_ge(idn_wait[0].h, idn_wait[1]) if (idn_wait and oc == 0 and half == 0) else None),
                [e.transpose(tp[tpb][0:min(128, I - 128 * ic), 128 * t:128 * t + ro],
                             wrow[0:ro, buf * I + 128 * ic:buf * I + min(128 * ic + 128, I)],
                             idn_sb[0:ro, 0:ro])
                 for t, ic in enumerate(ics)][-1].then_inc(tt.h, 1)))
            nev = ev.bump()
            p.ve(lambda e, oc=oc, ro=ro, ics=ics, half=half, ntt=ntt, tpb=tpb: (
                e.wait_ge(tt.h, ntt),
                [e.tensor_copy(
                    wt_sb[0:min(128, I - 128 * ic), ic * O + 128 * oc: ic * O + 128 * oc + ro],
                    tp[tpb][0:min(128, I - 128 * ic), 128 * t:128 * t + ro])
                 for t, ic in enumerate(ics)][-1].then_inc(ev.h, 1)))
    return ev


class PostLN:
    def __init__(self, p, x_in, x_out, tag):
        self.p = p
        self.x_in = x_in
        self.x_out = x_out
        self.lnst = LNStats(p, tag + "_lnst")
        self.xt = p.sb(tag + "_xt", [128, 2 * D], F32)
        self.lny = p.sb(tag + "_lny", [128, D], F32)
        self.ld = p.sem()
        self.ad = p.sem()
        self.st = p.swdge_sem()
        self.st0 = self.st.n

    def emit_ld(self, j, gate=None):
        p = self.p
        nld = self.ld.bump(16)
        stwar = self.st0 + 16 * (j - 1)
        p.sy(lambda e, j=j, nld=nld, stwar=stwar, gate=gate: (
            ([e.wait_ge(s.h, n) for (s, n) in gate]
             if (gate and j == 0) else None),
            (e.wait_ge(self.st.h, stwar) if stwar > self.st0 else None),
            e.dma_start(out=self.xt[:, (j % 2) * D:(j % 2 + 1) * D],
                        in_=self.x_in[128 * j:128 * (j + 1), :]).then_inc(self.ld.h, 16)))
        return nld

    def emit(self, j, srcs4, act_list, wait_v, nld=None):
        p = self.p
        if nld is None:
            nld = self.emit_ld(j)
        nvs = self.lnst.emit(srcs4, wait_v=wait_v)
        nss = self.lnst.ss.bump()
        p.sc(lambda e, act_list=act_list, nvs=nvs: (
            e.wait_ge(self.lnst.vs.h, nvs),
            [e.activation(self.lny[:, off:off + w], src, AF.Identity,
                          bias=self.lnst.nmr[:, 0:1], scale=self.lnst.rstd[:, 0:1])
             for (src, off, w) in act_list][-1].then_inc(self.lnst.ss.h, 1)))
        nad = self.ad.bump()
        p.ve(lambda e, j=j, nss=nss, nld=nld, nad=nad: (
            e.wait_ge(self.lnst.ss.h, nss),
            e.wait_ge(self.ld.h, nld),
            e.tensor_add(self.xt[:, (j % 2) * D:(j % 2 + 1) * D],
                         self.xt[:, (j % 2) * D:(j % 2 + 1) * D],
                         self.lny[:]).then_inc(self.ad.h, 1)))
        nst = self.st.bump(16)
        p.gp(lambda e, j=j, nad=nad, nst=nst: (
            e.wait_ge(self.ad.h, nad),
            e.dma_start(out=self.x_out[128 * j:128 * (j + 1), :],
                        in_=self.xt[:, (j % 2) * D:(j % 2 + 1) * D]).then_inc(self.st.h, 16)))
        return (self.lnst.ss, nss)

    def finish(self):
        p = self.p
        n = self.st.n
        p.sy(lambda e, n=n: e.wait_ge(self.st.h, n))


def build_outproj_prog():
    p = Prog("outproj")
    ct_in = p.din("ct", [D, TPC], BF16)
    wo_in = p.din("wo", [D, D], F32)
    x_in = p.din("x", [TPC, D], F32)
    idn = p.din("idn", [128, 128], BF16)
    x_out = p.dout("xn", [TPC, D], F32)

    idn_sb = p.sb("idn_sb", [128, 128], BF16)
    ld0 = p.sem()
    n0 = ld0.bump(16)
    p.sy(lambda e, n0=n0: e.dma_start(out=idn_sb[:], in_=idn).then_inc(ld0.h, 16))

    wt_sb = p.sb("wt", [128, NDC * D], BF16)
    wdone = emit_wt_prep(p, wo_in, wt_sb, D, D, idn_sb, "wp", idn_wait=(ld0, n0))
    nw = wdone.n

    ct_sb = p.sb("ct_sb", [128, 2 * NDC * 128], BF16)
    y_ps = [p.ps(f"y{n}", [128, 512], F32) for n in range(4)]
    cld = p.sem()
    mm = p.sem()
    pl = PostLN(p, x_in, x_out, "pl")

    free = None
    for j in range(NTT):
        buf = j % 2
        ncld = cld.bump(16)
        war = (j - 1)
        p.sy(lambda e, j=j, buf=buf, ncld=ncld, war=war: (
            (e.wait_ge(mm.h, war) if war > 0 else None),
            e.dma_start(out=ct_sb[:, buf * D:(buf + 1) * D]
                        .rearrange("p (c t) -> p c t", c=NDC),
                        in_=ct_in[:, 128 * j:128 * (j + 1)]
                        .rearrange("(c p) t -> p c t", p=128)).then_inc(cld.h, 16)))
        nmm = mm.bump()
        p.te(lambda e, j=j, buf=buf, ncld=ncld, nmm=nmm, nw=nw, free=free: (
            e.wait_ge(cld.h, ncld),
            (e.wait_ge(wdone.h, nw) if j == 0 else None),
            (e.wait_ge(free[0].h, free[1]) if free else None),
            [e.matmul(y_ps[n][:], ct_sb[:, buf * D + 128 * c:buf * D + 128 * (c + 1)],
                      wt_sb[:, c * D + 512 * n:c * D + 512 * (n + 1)],
                      start=(c == 0), stop=(c == NDC - 1))
             for c in range(NDC) for n in range(4)][-1].then_inc(mm.h, 1)))
        free = pl.emit(j, [yp[:] for yp in y_ps],
                       [(y_ps[n][:], 512 * n, 512) for n in range(4)], (mm, nmm))
    pl.finish()
    return p.finalize()


def build_ffn_prog(kind):
    NH = 5461 if kind == "tern" else 5632
    NHC = (NH + 127) // 128
    p = Prog("ffn_" + kind)
    wdt = I32 if kind == "l3" else F32
    x_in = p.din("x", [TPC, D], F32)
    w1 = p.din("w1", [NH, D], wdt)
    w3 = p.din("w3", [NH, D], wdt)
    w2 = p.din("w2", [D, NH], wdt)
    idn = p.din("idn", [128, 128], BF16)
    gs_in = us_in = ds_in = None
    if kind != "tern":
        gs_in = p.din("gs", [128, NHC], F32)
        us_in = p.din("us", [128, NHC], F32)
        ds_in = p.din("ds", [128, D], F32)
    x_out = p.dout("xn", [TPC, D], F32)
    emit_ffn(p, kind, x_in, x_out, w1, w3, w2, idn, gs_in, us_in, ds_in)
    return p.finalize()


def emit_ffn(p, kind, x_in, x_out, w1, w3, w2, idn, gs_in, us_in, ds_in):
    NH = 5461 if kind == "tern" else 5632
    NHC = (NH + 127) // 128
    NHPAD = NHC * 128
    NHB = (NHC + 7) // 8
    NQ = 4
    w_f32 = kind != "l2"
    g_scr = p.dtmp(p._uname("gscr"), [NHPAD, TPC], BF16)
    y_scr = p.dtmp(p._uname("yscr"), [TPC, 3 * 512], F32)

    tp = [p.ps(f"tp{i}", [128, 1024], BF16) for i in range(2)]
    psA = p.ps("psA", [128, 512], F32)
    psB = p.ps("psB", [128, 512], F32)
    psC = p.ps("psC", [128, 512], F32)
    psD = p.ps("psD", [128, 512], F32)
    ps_misc = p.ps("ps_misc", [128, 16], F32)

    idn_sb = p.sb("idn_sb", [128, 128], BF16)
    if kind == "tern":
        sgA_sb = p.sb("sgA", [128, NHPAD], BF16)
        sgB_sb = p.sb("sgB", [128, NHPAD], BF16)
        thbc = p.sb("thbc", [128, 4], F32)
        nthbc = p.sb("nthbc", [128, 4], F32)
    sg_sb = p.sb("sg_sb", [128, 512], BF16)
    hu_sb = p.sb("hu_sb", [128, 512], BF16)
    gt_sb = p.sb("gt_sb", [128, 2 * 512], BF16)
    y0_sb = p.sb("y0_sb", [128, 512], F32)
    y0r_sb = p.sb("y0r_sb", [128, 2 * 1536], F32)
    ysc_sb = p.sb("ysc_sb", [128, 512], F32)
    if kind != "tern":
        gs_sb = p.sb("gs_sb", [128, NHC], F32)
        us_sb = p.sb("us_sb", [128, NHC], F32)
        ds_sb = p.sb("ds_sb", [128, D], F32)

    ld0 = p.sem()
    n0 = ld0.bump(16)
    p.sy(lambda e: e.dma_start(out=idn_sb[:], in_=idn).then_inc(ld0.h, 16))
    if kind != "tern":
        n0 = ld0.bump(48)
        p.sy(lambda e: (e.dma_start(out=gs_sb[:], in_=gs_in).then_inc(ld0.h, 16),
                        e.dma_start(out=us_sb[:], in_=us_in).then_inc(ld0.h, 16),
                        e.dma_start(out=ds_sb[:], in_=ds_in).then_inc(ld0.h, 16)))
    p.te(lambda e, n0=n0: e.wait_ge(ld0.h, n0))
    p.ve(lambda e, n0=n0: e.wait_ge(ld0.h, n0))
    p.sc(lambda e, n0=n0: e.wait_ge(ld0.h, n0))

    s12 = p.scope()
    lnT_sb = p.sbs(s12, "lnT_sb", [128, NDC * TPC], BF16)
    lnt_buf = p.sbs(s12, "lnt_buf", [128, 2 * D], BF16)
    wrA = p.sbs(s12, "wrA", [128, 2 * D], F32 if w_f32 else BF16)
    wrB = p.sbs(s12, "wrB", [128, 2 * D], F32 if w_f32 else BF16)
    if kind != "l2":
        tr1 = p.sbs(s12, "tr1", [128, 2 * D], BF16)
        tr3 = p.sbs(s12, "tr3", [128, 2 * D], BF16)
    w1T = p.sbs(s12, "w1T", [128, 2 * D], BF16)
    w3T = p.sbs(s12, "w3T", [128, 2 * D], BF16)
    if kind == "tern":
        acc = p.sbs(s12, "acc", [128, 4], F32)
        red = p.sbs(s12, "red", [128, 1], F32)
        ones128 = p.sbs(s12, "ones128", [128, 1], F32)
        ones1 = p.sbs(s12, "ones1", [1, 128], F32)
        row13 = p.sbs(s12, "row13", [1, 4], F32)
        thw = p.sbs(s12, "thw", [128, 2 * D], F32)

    if w_f32:
        preln_xt = wrA
    else:
        preln_xt = p.sbs(s12, "preln_xt", [128, 2 * D], F32)
    lndone = emit_preln(p, x_in, lnT_sb, idn_sb, tp_pool=tp, xt_buf=preln_xt,
                        lnt_buf=lnt_buf,
                        alloc=lambda n, s, d=F32: p.sbs(s12, n, s, d))
    nln = lndone.n

    if kind == "tern":
        thlds = [p.sem(), p.sem()]
        thv = p.sem()
        tht = p.sem()
        p.ve(lambda e: (e.memset(acc[:], 0.0), e.memset(ones128[:], 1.0),
                        e.memset(ones1[:], 1.0).then_inc(thv.h, 1)))
        thv.bump()
        jobs = []
        for wi, w in enumerate((w1, w3, w2)):
            R, C = (NH, D) if wi < 2 else (D, NH)
            for r0 in range(0, R, 128):
                rr = min(128, R - r0)
                for c0 in range(0, C, 2048):
                    cc = min(2048, C - c0)
                    jobs.append((wi, w, r0, rr, c0, cc))
        for ji, (wi, w, r0, rr, c0, cc) in enumerate(jobs):
            buf = ji % 2
            thld = thlds[buf]
            nld = thld.bump(16)
            war = thv.n - 2
            p.sc(lambda e, w=w, r0=r0, rr=rr, c0=c0, cc=cc, buf=buf, nld=nld,
                        war=war, thld=thld: (
                (e.wait_ge(thv.h, war) if war > 1 else None),
                e.dma_start(out=thw[0:rr, buf * D:buf * D + cc],
                            in_=w[r0:r0 + rr, c0:c0 + cc]).then_inc(thld.h, 16)))
            nv1 = thv.bump()
            p.ve(lambda e, rr=rr, cc=cc, buf=buf, nld=nld, thld=thld: (
                e.wait_ge(thld.h, nld),
                e.tensor_reduce(red[0:rr, :], thw[0:rr, buf * D:buf * D + cc],
                                AX.X, OP.add, True).then_inc(thv.h, 1)))
            nv2 = thv.bump()
            p.ve(lambda e, wi=wi, rr=rr, nv1=nv1: (
                e.wait_ge(thv.h, nv1),
                e.tensor_add(acc[0:rr, wi:wi + 1], acc[0:rr, wi:wi + 1],
                             red[0:rr, :]).then_inc(thv.h, 1)))
        nred_done = thv.n
        ntt1 = tht.bump()
        p.te(lambda e, nv=thv.n: (
            e.wait_ge(thv.h, nv),
            e.matmul(ps_misc[0:1, 0:4], ones128[:], acc[:],
                     start=True, stop=True).then_inc(tht.h, 1)))
        nv3 = thv.bump()
        p.ve(lambda e, ntt1=ntt1: (
            e.wait_ge(tht.h, ntt1),
            e.tensor_scalar(row13[:], ps_misc[0:1, 0:4], 0.5 / (NH * D), None,
                            OP.mult).then_inc(thv.h, 1)))
        ntt2 = tht.bump()
        p.te(lambda e, nv3=nv3: (
            e.wait_ge(thv.h, nv3),
            e.matmul(ps_misc[:, 8:12], ones1[:], row13[:],
                     start=True, stop=True).then_inc(tht.h, 1)))
        nv5 = thv.bump()
        p.ve(lambda e, ntt2=ntt2: (
            e.wait_ge(tht.h, ntt2),
            e.tensor_copy(thbc[:], ps_misc[:, 8:12]),
            e.tensor_scalar(nthbc[:], ps_misc[:, 8:12], -1.0, None,
                            OP.mult).then_inc(thv.h, 1)))
        nth_ready = nv5
    else:
        thv = None
        nth_ready = 0
        nred_done = 0

    def scl1(hc, ro):
        return (thbc[0:ro, 0:1] if kind == "tern" else gs_sb[0:ro, hc:hc + 1])

    def scl3(hc, ro):
        return (thbc[0:ro, 1:2] if kind == "tern" else us_sb[0:ro, hc:hc + 1])

    upld = p.sem() if kind == "tern" else p.swdge_sem()
    qv = p.sem()
    qs = p.sem()
    utt = p.sem()
    uev = p.sem()
    umm = p.sem()
    ue = p.sem()
    gst = p.swdge_sem()

    emit_wld = p.sy if kind == "tern" else p.gp
    qs0, qv0, utt0 = qs.n, qv.n, utt.n
    for hc in range(NHC):
        ro = min(128, NH - 128 * hc)
        buf = hc % 2
        nld = upld.bump(32)
        if kind == "tern":
            war = (qs, qs0 + 4 * (hc - 1), qs0)
        elif kind == "l3":
            war = (qv, qv0 + 2 * (hc - 1), qv0)
        else:
            war = (utt, utt0 + 4 * (hc - 1), utt0)
        emit_wld(lambda e, hc=hc, ro=ro, buf=buf, nld=nld, war=war: (
            (e.wait_ge(lndone.h, nln) if (hc < 2 and w_f32) else None),
            (e.wait_ge(war[0].h, war[1]) if war[1] > war[2] else None),
            e.dma_start(out=wrA[0:ro, buf * D:buf * D + D],
                        in_=w1[128 * hc:128 * hc + ro, :]).then_inc(upld.h, 16),
            e.dma_start(out=wrB[0:ro, buf * D:buf * D + D],
                        in_=w3[128 * hc:128 * hc + ro, :]).then_inc(upld.h, 16)))
        if kind == "tern":
            for wrx, trx, wi in ((wrA, tr1, 0), (wrB, tr3, 1)):
                nqs = qs.bump(2)
                p.sc(lambda e, wrx=wrx, ro=ro, buf=buf, nld=nld, nqs=nqs,
                            wi=wi, hc=hc, war_v=qv.n: (
                    e.wait_ge(upld.h, nld),
                    (e.wait_ge(thv.h, nth_ready) if (hc == 0 and wi == 0) else None),
                    (e.wait_ge(qv.h, war_v) if war_v > 0 else None),
                    e.activation(sgA_sb[0:ro, 0:D], wrx[0:ro, buf * D:buf * D + D],
                                 AF.Sign, bias=nthbc[0:ro, wi:wi + 1]),
                    e.activation(sgB_sb[0:ro, 0:D], wrx[0:ro, buf * D:buf * D + D],
                                 AF.Sign, bias=thbc[0:ro, wi:wi + 1]).then_inc(qs.h, 2)))
                nqv = qv.bump()
                p.ve(lambda e, trx=trx, ro=ro, buf=buf, nqs=nqs,
                            war_t=utt.n - 2: (
                    e.wait_ge(qs.h, nqs),
                    (e.wait_ge(utt.h, war_t) if war_t > 0 else None),
                    e.tensor_add(trx[0:ro, buf * D:buf * D + D], sgA_sb[0:ro, 0:D],
                                 sgB_sb[0:ro, 0:D]).then_inc(qv.h, 1)))
            qsrc1, qsrc3 = tr1, tr3
            qwait = (qv, qv.n)
        elif kind == "l3":
            nqv = qv.bump(2)
            p.ve(lambda e, ro=ro, buf=buf, nld=nld, war_t=utt.n - 2: (
                e.wait_ge(upld.h, nld),
                (e.wait_ge(utt.h, war_t) if war_t > 0 else None),
                e.tensor_scalar(tr1[0:ro, buf * D:buf * D + D],
                                wrA[0:ro, buf * D:buf * D + D], 128.0, None,
                                OP.subtract),
                e.tensor_scalar(tr3[0:ro, buf * D:buf * D + D],
                                wrB[0:ro, buf * D:buf * D + D], 128.0, None,
                                OP.subtract).then_inc(qv.h, 2)))
            qsrc1, qsrc3 = tr1, tr3
            qwait = (qv, qv.n)
        else:
            qsrc1, qsrc3 = wrA, wrB
            qwait = (upld, upld.n)
        for gi, (src, dstT, half) in enumerate(((qsrc1, w1T, 0), (qsrc1, w1T, 1),
                                                (qsrc3, w3T, 0), (qsrc3, w3T, 1))):
            ntt = utt.bump()
            evwar = uev.n - 1
            p.te(lambda e, src=src, dstT=dstT, half=half, ro=ro, buf=buf, gi=gi,
                        qwait=qwait, ntt=ntt, evwar=evwar, hc=hc: (
                (e.wait_ge(qwait[0].h, qwait[1]) if gi == 0 else None),
                (e.wait_ge(lndone.h, nln) if (hc == 0 and gi == 0) else None),
                (e.wait_ge(uev.h, evwar) if evwar > 0 else None),
                [e.transpose(tp[gi % 2][:, 128 * t:128 * t + ro],
                             src[0:ro, buf * D + 128 * (8 * half + t):buf * D + 128 * (8 * half + t + 1)],
                             idn_sb[0:ro, 0:ro]) for t in range(8)][-1].then_inc(utt.h, 1)))
            nev = uev.bump()
            p.ve(lambda e, dstT=dstT, half=half, ro=ro, buf=buf, gi=gi, ntt=ntt: (
                e.wait_ge(utt.h, ntt),
                [e.tensor_copy(dstT[:, buf * D + 128 * (8 * half + t):buf * D + 128 * (8 * half + t) + ro],
                               tp[gi % 2][:, 128 * t:128 * t + ro])
                 for t in range(8)][-1].then_inc(uev.h, 1)))
        for tq in range(4):
            nmm = umm.bump(2)
            pw = ue.n - 1
            p.te(lambda e, ro=ro, buf=buf, tq=tq, nmm=nmm, nev=uev.n, pw=pw: (
                e.wait_ge(uev.h, nev),
                (e.wait_ge(ue.h, pw) if pw > 0 else None),
                [e.matmul(psA[0:ro, :],
                          w1T[:, buf * D + 128 * i:buf * D + 128 * i + ro],
                          lnT_sb[:, i * TPC + 512 * tq:i * TPC + 512 * (tq + 1)],
                          start=(i == 0), stop=(i == NDC - 1)) for i in range(NDC)][-1],
                [e.matmul(psB[0:ro, :],
                          w3T[:, buf * D + 128 * i:buf * D + 128 * i + ro],
                          lnT_sb[:, i * TPC + 512 * tq:i * TPC + 512 * (tq + 1)],
                          start=(i == 0), stop=(i == NDC - 1)) for i in range(NDC)][-1]
                .then_inc(umm.h, 2)))
            ne1 = ue.bump()
            p.sc(lambda e, hc=hc, ro=ro, nmm=nmm: (
                e.wait_ge(umm.h, nmm),
                e.activation(sg_sb[0:ro, :], psA[0:ro, :], AF.Silu,
                             scale=scl1(hc, ro)).then_inc(ue.h, 1)))
            ne2 = ue.bump()
            p.ve(lambda e, hc=hc, ro=ro, nmm=nmm: (
                e.wait_ge(umm.h, nmm),
                e.tensor_scalar(hu_sb[0:ro, :], psB[0:ro, :], scl3(hc, ro), None,
                                OP.mult).then_inc(ue.h, 1)))
            ne3 = ue.bump()
            gbuf = tq % 2
            war_g = gst.n - 16
            p.ve(lambda e, ro=ro, ne1=ne1, ne2=ne2, gbuf=gbuf, war_g=war_g: (
                e.wait_ge(ue.h, ne1),
                e.wait_ge(ue.h, ne2),
                (e.wait_ge(gst.h, war_g) if war_g > 0 else None),
                e.tensor_mul(gt_sb[0:ro, gbuf * 512:(gbuf + 1) * 512],
                             sg_sb[0:ro, :], hu_sb[0:ro, :]).then_inc(ue.h, 1)))
            ngs = gst.bump(16)
            p.gp(lambda e, hc=hc, ro=ro, tq=tq, ne3=ne3, gbuf=gbuf: (
                e.wait_ge(ue.h, ne3),
                e.dma_start(out=g_scr[128 * hc:128 * hc + ro, 512 * tq:512 * (tq + 1)],
                            in_=gt_sb[0:ro, gbuf * 512:(gbuf + 1) * 512])
                .then_inc(gst.h, 16)))
    ngst_all = gst.n
    numm_all = umm.n

    s12.close()
    pl = PostLN(p, x_in, x_out, "pl")
    s3 = p.scope()
    w2T = p.sbs(s3, "w2T", [128, 2 * NHC * 512], BF16)
    g_sb = p.sbs(s3, "g_sb", [128, 2 * NHPAD], BF16)
    wdr = p.sbs(s3, "wdr", [128, NHPAD], F32 if w_f32 else BF16)
    if kind != "l2":
        tdr = p.sbs(s3, "tdr", [128, NHPAD], BF16)

    dld = p.sem() if kind == "tern" else p.swdge_sem()
    dqv = p.sem()
    dqs = p.sem()
    dtt = p.sem()
    dev = p.sem()
    gld = p.sem()
    dmm = p.sem()
    yev = p.sem()
    yst = p.swdge_sem()
    yld = p.sem()
    plfree = None

    emit_dld = p.sy if kind == "tern" else p.gp
    dqs0, dqv0, dtt0, dmm0 = dqs.n, dqv.n, dtt.n, dmm.n
    for q in range(NQ):
        for dc in range(4):
            dg = 4 * q + dc
            nld = dld.bump(16)
            if kind == "tern":
                war = (dqs, dqs0 + 2 * dg, dqs0)
            elif kind == "l3":
                war = (dqv, dqv0 + dg, dqv0)
            else:
                war = (dtt, dtt0 + NHB * dg, dtt0)
            emit_dld(lambda e, dg=dg, nld=nld, war=war: (
                (e.wait_ge(gst.h, ngst_all) if dg == 0 else None),
                (e.wait_ge(umm.h, numm_all) if dg == 0 else None),
                (e.wait_ge(war[0].h, war[1]) if war[1] > war[2] else None),
                e.dma_start(out=wdr[:, 0:NH],
                            in_=w2[128 * dg:128 * (dg + 1), :]).then_inc(dld.h, 16)))
            if kind == "tern":
                ndqs = dqs.bump(2)
                p.sc(lambda e, nld=nld, war_v=dqv.n: (
                    e.wait_ge(dld.h, nld),
                    (e.wait_ge(dqv.h, war_v) if war_v > 0 else None),
                    e.activation(sgA_sb[:, 0:NH], wdr[:, 0:NH],
                                 AF.Sign, bias=nthbc[:, 2:3]),
                    e.activation(sgB_sb[:, 0:NH], wdr[:, 0:NH],
                                 AF.Sign, bias=thbc[:, 2:3]).then_inc(dqs.h, 2)))
                ndqv = dqv.bump()
                p.ve(lambda e, ndqs=ndqs, war_t=dtt.n: (
                    e.wait_ge(dqs.h, ndqs),
                    (e.wait_ge(dtt.h, war_t) if war_t > 0 else None),
                    e.tensor_add(tdr[:, 0:NH], sgA_sb[:, 0:NH],
                                 sgB_sb[:, 0:NH]).then_inc(dqv.h, 1)))
                dq_src, dq_wait = tdr, (dqv, dqv.n)
            elif kind == "l3":
                ndqv = dqv.bump()
                p.ve(lambda e, nld=nld, war_t=dtt.n: (
                    e.wait_ge(dld.h, nld),
                    (e.wait_ge(dtt.h, war_t) if war_t > 0 else None),
                    e.tensor_scalar(tdr[:, 0:NH], wdr[:, 0:NH],
                                    128.0, None, OP.subtract).then_inc(dqv.h, 1)))
                dq_src, dq_wait = tdr, (dqv, dqv.n)
            else:
                dq_src, dq_wait = wdr, (dld, nld)
            for hb in range(NHB):
                hcs = list(range(8 * hb, min(8 * hb + 8, NHC)))
                ntt = dtt.bump()
                evwar = dev.n - 1
                p.te(lambda e, dq_src=dq_src, dq_wait=dq_wait, hcs=hcs,
                            hb=hb, ntt=ntt, evwar=evwar: (
                    (e.wait_ge(dq_wait[0].h, dq_wait[1]) if hb == 0 else None),
                    (e.wait_ge(dev.h, evwar) if evwar > 0 else None),
                    [e.transpose(tp[hb % 2][0:min(128, NH - 128 * hcx), 128 * t:128 * (t + 1)],
                                 dq_src[:, 128 * hcx:min(128 * (hcx + 1), NH)],
                                 idn_sb[:]) for t, hcx in enumerate(hcs)][-1]
                    .then_inc(dtt.h, 1)))
                ndev = dev.bump()
                w2war = dmm0 + 16 * (q - 1) if q > 1 else 0
                qb = (q % 2) * NHC * 512
                p.ve(lambda e, hcs=hcs, hb=hb, dc=dc, ntt=ntt, w2war=w2war,
                            qb=qb: (
                    e.wait_ge(dtt.h, ntt),
                    (e.wait_ge(dmm.h, w2war) if w2war > 0 else None),
                    [e.tensor_copy(w2T[0:min(128, NH - 128 * hcx),
                                       qb + hcx * 512 + 128 * dc:qb + hcx * 512 + 128 * (dc + 1)],
                                   tp[hb % 2][0:min(128, NH - 128 * hcx), 128 * t:128 * (t + 1)])
                     for t, hcx in enumerate(hcs)][-1].then_inc(dev.h, 1)))
        ndev_q = dev.n
        for tj in range(NTT):
            gti = 16 * q + tj
            buf = tj % 2
            psn = psC if tj % 2 == 0 else psD
            ngl = gld.bump(16)
            mmwar = dmm.n - 1
            p.sy(lambda e, tj=tj, buf=buf, ngl=ngl, mmwar=mmwar, q=q: (
                (e.wait_ge(gst.h, ngst_all) if (q == 0 and tj == 0) else None),
                (e.wait_ge(dmm.h, mmwar) if mmwar > 0 else None),
                e.dma_start(out=g_sb[:, buf * NHPAD:(buf + 1) * NHPAD]
                            .rearrange("p (c t) -> p c t", c=NHC),
                            in_=g_scr[:, 128 * tj:128 * (tj + 1)]
                            .rearrange("(c p) t -> p c t", p=128)).then_inc(gld.h, 16)))
            nmm = dmm.bump()
            if q < 3:
                psfree = (yev, yev.n - 1)
            else:
                psfree = plfree if plfree else (yev, yev.n - 1)
            qb = (q % 2) * NHC * 512
            p.te(lambda e, buf=buf, ngl=ngl, nmm=nmm, ndev=ndev_q, tj=tj,
                        psn=psn, psfree=psfree, qb=qb: (
                e.wait_ge(gld.h, ngl),
                (e.wait_ge(dev.h, ndev) if tj == 0 else None),
                (e.wait_ge(psfree[0].h, psfree[1]) if psfree[1] > 0 else None),
                [e.matmul(psn[:],
                          g_sb[0:min(128, NH - 128 * hcx), buf * NHPAD + 128 * hcx:buf * NHPAD + 128 * (hcx + 1)],
                          w2T[0:min(128, NH - 128 * hcx), qb + hcx * 512:qb + hcx * 512 + 512],
                          start=(hcx == 0), stop=(hcx == NHC - 1))
                 for hcx in range(NHC)][-1].then_inc(dmm.h, 1)))
            if q < 3:
                nye = yev.bump()
                stwar = yst.n - 16
                p.ve(lambda e, nmm=nmm, stwar=stwar, psn=psn: (
                    e.wait_ge(dmm.h, nmm),
                    (e.wait_ge(yst.h, stwar) if stwar > 0 else None),
                    e.tensor_copy(y0_sb[:], psn[:]).then_inc(yev.h, 1)))
                nys = yst.bump(16)
                p.gp(lambda e, tj=tj, q=q, nye=nye: (
                    e.wait_ge(yev.h, nye),
                    e.dma_start(out=y_scr[128 * tj:128 * (tj + 1), 512 * q:512 * (q + 1)],
                                in_=y0_sb[:]).then_inc(yst.h, 16)))
            else:
                nyl = yld.bump(16)
                ldwar = pl.lnst.ss.n - 2
                p.sy(lambda e, tj=tj, buf=buf, nyl=nyl, ldwar=ldwar: (
                    (e.wait_ge(pl.lnst.ss.h, ldwar) if ldwar > 0 else None),
                    e.dma_start(out=y0r_sb[:, buf * 1536:(buf + 1) * 1536],
                                in_=y_scr[128 * tj:128 * (tj + 1), :]).then_inc(yld.h, 16)))
                if kind != "tern":
                    nsc = yev.bump()
                    p.ve(lambda e, buf=buf, nmm=nmm, nyl=nyl, psn=psn: (
                        e.wait_ge(dmm.h, nmm),
                        e.wait_ge(yld.h, nyl),
                        e.tensor_mul(y0r_sb[:, buf * 1536:(buf + 1) * 1536],
                                     y0r_sb[:, buf * 1536:(buf + 1) * 1536],
                                     ds_sb[:, 0:1536]),
                        e.tensor_mul(ysc_sb[:], psn[:],
                                     ds_sb[:, 1536:2048]).then_inc(yev.h, 1)))
                    srcs = [y0r_sb[:, buf * 1536 + 512 * i:buf * 1536 + 512 * (i + 1)]
                            for i in range(3)] + [ysc_sb[:]]
                    acts = [(y0r_sb[:, buf * 1536:(buf + 1) * 1536], 0, 1536),
                            (ysc_sb[:], 1536, 512)]
                    plfree = pl.emit(tj, srcs, acts, (yev, nsc))
                else:
                    nsc = yev.bump()
                    p.ve(lambda e, nmm=nmm, nyl=nyl: (
                        e.wait_ge(dmm.h, nmm),
                        e.wait_ge(yld.h, nyl),
                        e.engine_nop().then_inc(yev.h, 1)))
                    srcs = [y0r_sb[:, buf * 1536 + 512 * i:buf * 1536 + 512 * (i + 1)]
                            for i in range(3)] + [psn[:]]
                    acts = [(y0r_sb[:, buf * 1536:(buf + 1) * 1536], 0, 1536),
                            (psn[:], 1536, 512)]
                    plfree = pl.emit(tj, srcs, acts, (yev, nsc))
    s3.close()
    pl.finish()


def build_attn3_prog():
    p = Prog("attn3")
    lnt_in = p.din("lnt", [D, S], BF16)
    wq_in = p.din("wq", [512, D], F32)
    wk_in = p.din("wk", [512, D], F32)
    wv_in = p.din("wv", [512, D], F32)
    idn = p.din("idn", [128, 128], BF16)
    dmask = p.din("dmask", [128, 4 * 512], BF16)
    ct_out = p.dout("ct", [512, S], BF16)
    emit_attn3(p, lambda ts: lnt_in[:, 512 * ts:512 * (ts + 1)],
               wq_in, wk_in, wv_in, idn, dmask, ct_out)
    return p.finalize()


def emit_attn3(p, lnt_tile, wq_in, wk_in, wv_in, idn, dmask, ct_out,
               w_wait=None, lnt_wait=None):

    tp = [p.ps(f"tp{i}", [128, 1024], BF16) for i in range(2)]
    ps0 = p.ps("ps0", [128, 512], F32)
    ps1 = p.ps("ps1", [128, 512], F32)
    ps2 = p.ps("ps2", [128, 512], F32)
    ps3 = p.ps("ps3", [128, 512], F32)
    psDn = p.ps("psDn", [1, 512], F32)
    psDn2 = p.ps("psDn2", [1, 512], F32)

    idn_sb = p.sb("idn_sb", [128, 128], BF16)
    dmask_sb = p.sb("dmask_sb", [128, 4 * 512], BF16)
    negc = p.sb("negc", [128, 1], F32)
    ones128 = p.sb("ones128", [128, 1], BF16)
    one1 = p.sb("one1", [1, 128], F32)
    wrow = p.sb("wrow", [128, D], BF16)
    wqT = p.sb("wqT", [128, D], BF16)
    wkT = p.sb("wkT", [128, D], BF16)
    wvT = p.sb("wvT", [128, D], BF16)
    lnt_sb = p.sb("lnt_sb", [128, 2 * 16 * 512], BF16)
    qT_h = p.sb("qT_h", [128, S], BF16)
    kT_h = p.sb("kT_h", [128, S], BF16)
    vT_h = p.sb("vT_h", [128, S], BF16)
    vsm = p.sb("vsm", [128, S], BF16)
    pt_sb = p.sb("pt_sb", [128, 32 * 512], BF16)
    rd_sb = p.sb("rd_sb", [1, 512], F32)
    rdd_sb = p.sb("rdd_sb", [1, 512], F32)
    os_sb = p.sb("os_sb", [128, 512], F32)
    bc_sb = p.sb("bc_sb", [128, 512], F32)
    ct_sb = p.sb("ct_sb", [128, 2 * 512], BF16)

    ld0 = p.sem()
    n0 = ld0.bump(32)
    p.sy(lambda e, w_wait=w_wait: (
        (e.wait_ge(w_wait[0].h, w_wait[1]) if w_wait else None),
        e.dma_start(out=idn_sb[:], in_=idn).then_inc(ld0.h, 16),
        e.dma_start(out=dmask_sb[:], in_=dmask).then_inc(ld0.h, 16)))
    vc0 = p.sem()
    nvc0 = vc0.bump()
    p.ve(lambda e, n0=n0: (
        e.wait_ge(ld0.h, n0),
        e.memset(negc[:], -CEXP), e.memset(ones128[:], 1.0),
        e.memset(one1[:], 1.0).then_inc(vc0.h, 1)))
    p.te(lambda e, n0=n0: e.wait_ge(ld0.h, n0))
    p.sc(lambda e, nvc0=nvc0: e.wait_ge(vc0.h, nvc0))
    p.ve(lambda e, n0=n0: e.wait_ge(ld0.h, n0))

    gld = p.swdge_sem()
    wtt = p.sem()
    wev = p.sem()
    lnld = p.sem()
    pmm = p.sem()
    pev = p.sem()
    vtt = p.sem()
    vev = p.sem()
    fmm = p.sem()
    fex = p.sem()
    fpt = p.sem()
    fac = p.sem()
    fden = p.sem()
    fpo = p.sem()
    cst = p.swdge_sem()
    hbar = p.sem()
    hbar0 = hbar.n

    SCL = 1.0 / float(np.sqrt(128.0))
    bank_last_exp = [None, None]
    slot_last_den = [0] * 32

    for hi in range(4):
        for wi, (w_in, wT) in enumerate(((wq_in, wqT), (wk_in, wkT), (wv_in, wvT))):
            nld = gld.bump(16)
            p.gp(lambda e, hi=hi, wi=wi, w_in=w_in, nld=nld, wtw=wtt.n,
                        w_wait=w_wait: (
                (e.wait_ge(w_wait[0].h, w_wait[1])
                 if (w_wait and hi == 0 and wi == 0) else None),
                (e.wait_ge(wtt.h, wtw) if wtw > 0 else None),
                e.dma_start(out=wrow[:], in_=w_in[128 * hi:128 * (hi + 1), :])
                .then_inc(gld.h, 16)))
            for half in range(2):
                ntt = wtt.bump()
                p.te(lambda e, half=half, nld=nld, ntt=ntt, evw=wev.n - 1, hi=hi: (
                    e.wait_ge(gld.h, nld),
                    (e.wait_ge(wev.h, evw) if evw > 0 else None),
                    [e.transpose(tp[half][:, 128 * t:128 * (t + 1)],
                                 wrow[:, 128 * (8 * half + t):128 * (8 * half + t + 1)],
                                 idn_sb[:]) for t in range(8)][-1].then_inc(wtt.h, 1)))
                nev = wev.bump()
                p.ve(lambda e, wT=wT, half=half, ntt=ntt, hb=hi - 1, hi=hi: (
                    e.wait_ge(wtt.h, ntt),
                    (e.wait_ge(hbar.h, hbar0 + hb + 1) if (hb >= 0) else None),
                    [e.tensor_copy(wT[:, 128 * (8 * half + t):128 * (8 * half + t) + 128],
                                   tp[half][:, 128 * t:128 * (t + 1)])
                     for t in range(8)][-1].then_inc(wev.h, 1)))
        nwev_h = wev.n
        for ts in range(16):
            buf = ts % 2
            nll = lnld.bump(16)
            pmwar = pmm.n - 1
            p.sy(lambda e, ts=ts, buf=buf, nll=nll, pmwar=pmwar, hi=hi,
                        lnt_wait=lnt_wait: (
                (e.wait_ge(lnt_wait[0].h, lnt_wait[1])
                 if (lnt_wait and hi == 0 and ts == 0) else None),
                (e.wait_ge(pmm.h, pmwar) if pmwar > 0 else None),
                e.dma_start(out=lnt_sb[:, buf * 8192:(buf + 1) * 8192]
                            .rearrange("p (c t) -> p c t", c=16),
                            in_=lnt_tile(ts)
                            .rearrange("(c p) t -> p c t", p=128)).then_inc(lnld.h, 16)))
            nmm = pmm.bump()
            p.te(lambda e, ts=ts, buf=buf, nll=nll, nmm=nmm, nwev=nwev_h,
                        pvw=pev.n: (
                e.wait_ge(lnld.h, nll),
                (e.wait_ge(wev.h, nwev) if ts == 0 else None),
                (e.wait_ge(pev.h, pvw) if pvw > 0 else None),
                [e.matmul(psm[:], wT[:, 128 * i:128 * (i + 1)],
                          lnt_sb[:, buf * 8192 + 512 * i:buf * 8192 + 512 * (i + 1)],
                          start=(i == 0), stop=(i == NDC - 1))
                 for wT, psm in ((wqT, ps0), (wkT, ps1), (wvT, ps2))
                 for i in range(NDC)][-1].then_inc(pmm.h, 1)))
            npe = pev.bump(3)
            p.ve(lambda e, ts=ts, nmm=nmm, hb=hi - 1: (
                e.wait_ge(pmm.h, nmm),
                (e.wait_ge(hbar.h, hbar0 + hb + 1) if hb >= 0 else None),
                e.tensor_copy(qT_h[:, 512 * ts:512 * (ts + 1)], ps0[:]),
                e.tensor_copy(kT_h[:, 512 * ts:512 * (ts + 1)], ps1[:]),
                e.tensor_copy(vT_h[:, 512 * ts:512 * (ts + 1)], ps2[:]).then_inc(pev.h, 3)))
        npev_h = pev.n
        for g in range(8):
            ntt = vtt.bump()
            p.te(lambda e, g=g, npev=npev_h, evw=vev.n - 1: (
                (e.wait_ge(pev.h, npev) if g == 0 else None),
                (e.wait_ge(vev.h, evw) if evw > 0 else None),
                [e.transpose(tp[g % 2][:, 128 * t:128 * (t + 1)],
                             vT_h[:, 128 * (8 * g + t):128 * (8 * g + t + 1)],
                             idn_sb[:]) for t in range(8)][-1].then_inc(vtt.h, 1)))
            nev = vev.bump()
            p.ve(lambda e, g=g, ntt=ntt: (
                e.wait_ge(vtt.h, ntt),
                [e.tensor_copy(vsm[:, 128 * (8 * g + t):128 * (8 * g + t) + 128],
                               tp[g % 2][:, 128 * t:128 * (t + 1)])
                 for t in range(8)][-1].then_inc(vev.h, 1)))
        nvev_h = vev.n
        for qt in range(16):
            nkc = 4 * qt + 4
            nhalves = 1 if nkc <= 32 else 2
            psS = (ps0, ps2)
            for H in range(nhalves):
                lo, hi_kc = 32 * H, min(32 * (H + 1), nkc)
                this_half_pt = {}
                for kc in range(lo, hi_kc):
                    s = kc - lo
                    diag_r = kc - 4 * qt
                    war = bank_last_exp[kc % 2]
                    nfm = fmm.bump()
                    fw = fpo.n if (kc == 0) else 0
                    p.te(lambda e, qt=qt, kc=kc, nfm=nfm, war=war,
                                nvev=nvev_h, fw=fw: (
                        (e.wait_ge(vev.h, nvev) if (qt == 0 and kc == 0) else None),
                        (e.wait_ge(war[0].h, war[1]) if war else None),
                        (e.wait_ge(fpo.h, fw) if fw > 0 else None),
                        e.matmul(psS[kc % 2][:], kT_h[:, 128 * kc:128 * (kc + 1)],
                                 qT_h[:, 512 * qt:512 * (qt + 1)],
                                 start=True, stop=True).then_inc(fmm.h, 1)))
                    sw = slot_last_den[s]
                    if diag_r < 0:
                        nfp = fpt.bump()
                        p.sc(lambda e, s=s, kcp=kc % 2, nfm=nfm, sw=sw: (
                            e.wait_ge(fmm.h, nfm),
                            (e.wait_ge(fden.h, sw) if sw > 0 else None),
                            e.activation(pt_sb[:, 512 * s:512 * (s + 1)],
                                         psS[kcp][:], AF.Exp, bias=negc[:, 0:1],
                                         scale=SCL).then_inc(fpt.h, 1)))
                        bank_last_exp[kc % 2] = (fpt, fpt.n)
                    else:
                        nfe = fex.bump()
                        p.sc(lambda e, s=s, kcp=kc % 2, nfm=nfm, sw=sw: (
                            e.wait_ge(fmm.h, nfm),
                            (e.wait_ge(fden.h, sw) if sw > 0 else None),
                            e.activation(pt_sb[:, 512 * s:512 * (s + 1)],
                                         psS[kcp][:], AF.Exp, bias=negc[:, 0:1],
                                         scale=SCL).then_inc(fex.h, 1)))
                        bank_last_exp[kc % 2] = (fex, fex.n)
                        nfp = fpt.bump()
                        p.ve(lambda e, s=s, nfe=nfe, r=diag_r: (
                            e.wait_ge(fex.h, nfe),
                            e.tensor_mul(pt_sb[:, 512 * s:512 * (s + 1)],
                                         pt_sb[:, 512 * s:512 * (s + 1)],
                                         dmask_sb[:, 512 * r:512 * (r + 1)])
                            .then_inc(fpt.h, 1)))
                    this_half_pt[s] = fpt.n
                psO = ps1 if H == 0 else ps3
                psD = psDn if H == 0 else psDn2
                for kc in range(lo, hi_kc):
                    s = kc - lo
                    xw = this_half_pt[s]
                    last = (kc == hi_kc - 1)
                    if last:
                        nfa = fac.bump()
                        p.te(lambda e, kc=kc, s=s, xw=xw, psO=psO, nfa=nfa,
                                    st=(s == 0): (
                            e.wait_ge(fpt.h, xw),
                            e.matmul(psO[:], vsm[:, 128 * kc:128 * (kc + 1)],
                                     pt_sb[:, 512 * s:512 * (s + 1)],
                                     start=st, stop=True).then_inc(fac.h, 1)))
                    else:
                        p.te(lambda e, kc=kc, s=s, xw=xw, psO=psO,
                                    st=(s == 0): (
                            e.wait_ge(fpt.h, xw),
                            e.matmul(psO[:], vsm[:, 128 * kc:128 * (kc + 1)],
                                     pt_sb[:, 512 * s:512 * (s + 1)],
                                     start=st, stop=False)))
                for kc in range(lo, hi_kc):
                    s = kc - lo
                    nfd = fden.bump()
                    p.te(lambda e, s=s, psD=psD, nfd=nfd, st=(s == 0),
                                last=(kc == hi_kc - 1): (
                        e.matmul(psD[:], ones128[:],
                                 pt_sb[:, 512 * s:512 * (s + 1)],
                                 start=st, stop=last).then_inc(fden.h, 1)))
                    slot_last_den[s] = fden.n
            nfo1 = fpo.bump()
            if nhalves == 2:
                p.ve(lambda e, nfa=fac.n, nfd=fden.n: (
                    e.wait_ge(fac.h, nfa),
                    e.wait_ge(fden.h, nfd),
                    e.tensor_copy(os_sb[:], ps1[:]),
                    e.tensor_add(os_sb[:], os_sb[:], ps3[:]),
                    e.tensor_copy(rdd_sb[:], psDn[0:1, :]),
                    e.tensor_add(rdd_sb[:], rdd_sb[:], psDn2[0:1, :]),
                    e.reciprocal(rd_sb[:], rdd_sb[:]).then_inc(fpo.h, 1)))
            else:
                p.ve(lambda e, nfa=fac.n, nfd=fden.n: (
                    e.wait_ge(fac.h, nfa),
                    e.wait_ge(fden.h, nfd),
                    e.tensor_copy(os_sb[:], ps1[:]),
                    e.reciprocal(rd_sb[:], psDn[0:1, :]).then_inc(fpo.h, 1)))
            nfo2 = fpo.bump()
            p.te(lambda e, nfo1=nfo1, xw1=bank_last_exp[0], xw2=bank_last_exp[1]: (
                e.wait_ge(fpo.h, nfo1),
                (e.wait_ge(xw1[0].h, xw1[1]) if xw1 else None),
                (e.wait_ge(xw2[0].h, xw2[1]) if xw2 else None),
                e.matmul(ps2[:], one1[:], rd_sb[0:1, :],
                         start=True, stop=True).then_inc(fpo.h, 1)))
            nfo3 = fpo.bump()
            p.ve(lambda e, nfo2=nfo2: (
                e.wait_ge(fpo.h, nfo2),
                e.tensor_copy(bc_sb[:], ps2[:]).then_inc(fpo.h, 1)))
            nfo4 = fpo.bump()
            cbuf = qt % 2
            p.ve(lambda e, nfo3=nfo3, cbuf=cbuf, stw=cst.n - 16: (
                e.wait_ge(fpo.h, nfo3),
                (e.wait_ge(cst.h, stw) if stw > 0 else None),
                e.tensor_mul(ct_sb[:, cbuf * 512:(cbuf + 1) * 512], os_sb[:],
                             bc_sb[:]).then_inc(fpo.h, 1)))
            ncs = cst.bump(16)
            p.gp(lambda e, hi=hi, qt=qt, cbuf=cbuf, nfo4=nfo4: (
                e.wait_ge(fpo.h, nfo4),
                e.dma_start(out=ct_out[128 * hi:128 * (hi + 1), 512 * qt:512 * (qt + 1)],
                            in_=ct_sb[:, cbuf * 512:(cbuf + 1) * 512])
                .then_inc(cst.h, 16)))
        nhb = hbar.bump()
        p.ve(lambda e, nfo=fpo.n, nhb=nhb: (
            e.wait_ge(fpo.h, nfo), e.engine_nop().then_inc(hbar.h, 1)))
    p.sy(lambda e, n=cst.n: e.wait_ge(cst.h, n))


def build_l1qkv_prog():
    p = Prog("l1qkv")
    x_in = p.din("x", [TPC, D], F32)
    w_in = p.din("wqkv", [3 * D, D], F32)
    idn = p.din("idn", [128, 128], BF16)
    qkvT_out = p.dout("qkvT", [3 * D, TPC], BF16)
    emit_l1qkv(p, x_in, w_in, idn, qkvT_out)
    return p.finalize()


def emit_l1qkv(p, x_in, w_in, idn, qkvT_out):

    tp = [p.ps(f"tp{i}", [128, 1024], BF16) for i in range(2)]
    psm = [p.ps(f"psm{i}", [128, 512], F32) for i in range(4)]

    idn_sb = p.sb("idn_sb", [128, 128], BF16)
    ld0 = p.sem()
    n0 = ld0.bump(16)
    p.sy(lambda e: e.dma_start(out=idn_sb[:], in_=idn).then_inc(ld0.h, 16))
    p.te(lambda e, n0=n0: e.wait_ge(ld0.h, n0))
    p.ve(lambda e, n0=n0: e.wait_ge(ld0.h, n0))

    lnT_sb = p.sb("lnT_sb", [128, NDC * TPC], BF16)
    lndone = emit_preln(p, x_in, lnT_sb, idn_sb, tp_pool=tp)
    nln = lndone.n

    wrow = p.sb("wrow", [128, 2 * D], BF16)
    wTc = p.sb("wTc", [128, 2 * D], BF16)
    ost = p.sb("ost", [128, 2 * D], BF16)
    gld = p.swdge_sem()
    wtt = p.sem()
    wev = p.sem()
    mm = p.sem()
    oev = p.sem()
    ost_st = p.swdge_sem()

    for oc in range(48):
        buf = oc % 2
        nld = gld.bump(16)
        p.gp(lambda e, oc=oc, buf=buf, nld=nld, ttw=wtt.n - 2: (
            (e.wait_ge(wtt.h, ttw) if ttw > 0 else None),
            e.dma_start(out=wrow[:, buf * D:(buf + 1) * D],
                        in_=w_in[128 * oc:128 * (oc + 1), :]).then_inc(gld.h, 16)))
        for half in range(2):
            ntt = wtt.bump()
            p.te(lambda e, oc=oc, buf=buf, half=half, nld=nld, ntt=ntt,
                        evw=wev.n - 1: (
                e.wait_ge(gld.h, nld),
                (e.wait_ge(lndone.h, nln) if (oc == 0 and half == 0) else None),
                (e.wait_ge(wev.h, evw) if evw > 0 else None),
                [e.transpose(tp[half][:, 128 * t:128 * (t + 1)],
                             wrow[:, buf * D + 128 * (8 * half + t):buf * D + 128 * (8 * half + t + 1)],
                             idn_sb[:]) for t in range(8)][-1].then_inc(wtt.h, 1)))
            nev = wev.bump()
            p.ve(lambda e, buf=buf, half=half, ntt=ntt, mmw=mm.n - 1: (
                e.wait_ge(wtt.h, ntt),
                (e.wait_ge(mm.h, mmw) if mmw > 0 else None),
                [e.tensor_copy(wTc[:, buf * D + 128 * (8 * half + t):buf * D + 128 * (8 * half + t) + 128],
                               tp[half][:, 128 * t:128 * (t + 1)])
                 for t in range(8)][-1].then_inc(wev.h, 1)))
        nmm = mm.bump()
        p.te(lambda e, oc=oc, buf=buf, nmm=nmm, nev=wev.n, oevw=oev.n - 1: (
            e.wait_ge(wev.h, nev),
            (e.wait_ge(oev.h, oevw) if oevw > 0 else None),
            [e.matmul(psm[tq][:], wTc[:, buf * D + 128 * i:buf * D + 128 * (i + 1)],
                      lnT_sb[:, i * TPC + 512 * tq:i * TPC + 512 * (tq + 1)],
                      start=(i == 0), stop=(i == NDC - 1))
             for tq in range(4) for i in range(NDC)][-1].then_inc(mm.h, 1)))
        nov = oev.bump()
        p.ve(lambda e, buf=buf, nmm=nmm, stw=ost_st.n - 16: (
            e.wait_ge(mm.h, nmm),
            (e.wait_ge(ost_st.h, stw) if stw > 0 else None),
            [e.tensor_copy(ost[:, buf * D + 512 * tq:buf * D + 512 * (tq + 1)],
                           psm[tq][:]) for tq in range(4)][-1].then_inc(oev.h, 1)))
        nst = ost_st.bump(16)
        p.gp(lambda e, oc=oc, buf=buf, nov=nov: (
            e.wait_ge(oev.h, nov),
            e.dma_start(out=qkvT_out[128 * oc:128 * (oc + 1), :],
                        in_=ost[:, buf * D:(buf + 1) * D]).then_inc(ost_st.h, 16)))
    p.sy(lambda e, n=ost_st.n: e.wait_ge(ost_st.h, n))


def build_l1comb_prog():
    p = Prog("l1comb")
    qT_in = p.din("qT", [D, TPC], BF16)
    koT_in = p.din("koT", [D, TPC], BF16)
    voT_in = p.din("voT", [D, TPC], BF16)
    kfT_in = p.din("kfT", [D, TPC], BF16)
    vfT_in = p.din("vfT", [D, TPC], BF16)
    msk_in = p.din("msk", [128, 1], F32)
    selh_in = p.din("selh", [16, 16 * 128], BF16)
    wo_in = p.din("wo", [D, D], F32)
    x_in = p.din("x", [TPC, D], F32)
    idn = p.din("idn", [128, 128], BF16)
    x_out = p.dout("xn", [TPC, D], F32)
    emit_l1comb(p, qT_in, koT_in, voT_in, kfT_in, vfT_in, msk_in, selh_in,
                wo_in, x_in, idn, x_out)
    return p.finalize()


def emit_l1comb(p, qT_in, koT_in, voT_in, kfT_in, vfT_in, msk_in, selh_in,
                wo_in, x_in, idn, x_out):

    tp = [p.ps(f"tp{i}", [128, 1024], BF16) for i in range(2)]
    y_ps = [p.ps(f"y{n}", [128, 512], F32) for n in range(4)]
    psa = tp[0].ap()[:, 0:64].bitcast(F32)
    psb = tp[0].ap()[:, 256:512].bitcast(F32)
    psdt = tp[1].ap()[0:16, 0:128]

    idn_sb = p.sb("idn_sb", [128, 128], BF16)
    msk_sb = p.sb("msk_sb", [128, 1], F32)
    selh_sb = p.sb("selh_sb", [16, 16 * 128], BF16)
    ones128 = p.sb("ones128b", [128, 1], BF16)
    ld0 = p.sem()
    n0 = ld0.bump(48)
    p.sy(lambda e: (e.dma_start(out=idn_sb[:], in_=idn).then_inc(ld0.h, 16),
                    e.dma_start(out=msk_sb[:], in_=msk_in).then_inc(ld0.h, 16),
                    e.dma_start(out=selh_sb[:], in_=selh_in).then_inc(ld0.h, 16)))
    vc0 = p.sem()
    nvc0 = vc0.bump()
    p.ve(lambda e, n0=n0: (e.wait_ge(ld0.h, n0),
                           e.memset(ones128[:], 1.0).then_inc(vc0.h, 1)))
    p.te(lambda e, n0=n0: e.wait_ge(ld0.h, n0))
    p.sc(lambda e, n0=n0: e.wait_ge(ld0.h, n0))

    wt_sb = p.sb("wt", [128, NDC * D], BF16)
    wdone = emit_wt_prep(p, wo_in, wt_sb, D, D, idn_sb, "wp", idn_wait=(ld0, n0),
                         tp_pool=tp)
    nw = wdone.n

    qc = p.sb("qc", [128, 2 * D], BF16)
    kfc = p.sb("kfc", [128, 2 * D], BF16)
    koc = p.sb("koc", [128, 2 * D], BF16)
    vfc = p.sb("vfc", [128, 2 * D], BF16)
    voc = p.sb("voc", [128, 2 * D], BF16)
    prod = p.sb("prod", [128, D], BF16)
    a_sb = p.sb("a_sb", [128, 32], F32)
    dsg = p.sb("dsg", [128, 16], F32)
    dsgb = p.sb("dsgb", [128, 16], BF16)
    dT_sb = p.sb("dT_sb", [16, 128], BF16)
    dbc = p.sb("dbc", [128, 128], BF16)
    vd_sb = p.sb("vd_sb", [128, 128], BF16)
    outT = p.sb("outT", [128, 2 * D], BF16)

    cld = p.sem()
    av = p.sem()
    at = p.sem()
    asg = p.sem()
    cmb = p.sem()
    cmb0 = cmb.n
    mm = p.sem()
    pl = PostLN(p, x_in, x_out, "pl")
    free = None

    for j in range(NTT):
        buf = j % 2
        ncl = cld.bump(5 * 16)
        cw = cmb0 + 16 * (j - 1)
        p.sy(lambda e, j=j, buf=buf, ncl=ncl, cw=cw, cmb0=cmb0: (
            (e.wait_ge(cmb.h, cw) if cw > cmb0 else None),
            [e.dma_start(out=dst[:, buf * D:(buf + 1) * D]
                         .rearrange("p (c t) -> p c t", c=NDC),
                         in_=src[:, 128 * j:128 * (j + 1)]
                         .rearrange("(c p) t -> p c t", p=128)).then_inc(cld.h, 16)
             for dst, src in ((qc, qT_in), (kfc, kfT_in), (koc, koT_in),
                              (vfc, vfT_in), (voc, voT_in))][-1]))
        for h in range(16):
            for which, ksrc, col in ((0, kfc, h), (1, koc, 16 + h)):
                nav = av.bump()
                p.ve(lambda e, h=h, buf=buf, ksrc=ksrc, ncl=ncl, nav=nav,
                            atw=at.n: (
                    e.wait_ge(cld.h, ncl),
                    (e.wait_ge(at.h, atw) if atw > 0 else None),
                    e.tensor_mul(prod[:, 0:128],
                                 qc[:, buf * D + 128 * h:buf * D + 128 * (h + 1)],
                                 ksrc[:, buf * D + 128 * h:buf * D + 128 * (h + 1)])
                    .then_inc(av.h, 1)))
                nat = at.bump()
                p.te(lambda e, col=col, nav=nav, nat=nat, j=j, nw=nw: (
                    e.wait_ge(av.h, nav),
                    (e.wait_ge(vc0.h, nvc0) if (j == 0 and col == 0) else None),
                    (e.wait_ge(wdone.h, nw) if (j == 0 and col == 0) else None),
                    e.matmul(psa[:, col:col + 1], prod[:, 0:128], ones128[:],
                             start=True, stop=True).then_inc(at.h, 1)))
        nav1 = av.bump()
        p.ve(lambda e, nat=at.n, nav1=nav1: (
            e.wait_ge(at.h, nat),
            e.tensor_copy(a_sb[:], psa[:]).then_inc(av.h, 1)))
        nav2 = av.bump()
        p.ve(lambda e, nav1=nav1: (
            e.wait_ge(av.h, nav1),
            e.tensor_sub(dsg[:], a_sb[:, 0:16], a_sb[:, 16:32]).then_inc(av.h, 1)))
        nsg = asg.bump()
        p.sc(lambda e, nav2=nav2: (
            e.wait_ge(av.h, nav2),
            e.activation(dsg[:], dsg[:], AF.Sigmoid,
                         scale=float(DH ** -0.5)).then_inc(asg.h, 1)))
        nav3 = av.bump()
        p.ve(lambda e, nsg=nsg, nav3=nav3: (
            e.wait_ge(asg.h, nsg),
            e.tensor_scalar(dsgb[:], dsg[:], msk_sb[:, 0:1], None,
                            OP.mult).then_inc(av.h, 1)))
        nat2 = at.bump()
        p.te(lambda e, nav3=nav3, nat2=nat2: (
            e.wait_ge(av.h, nav3),
            e.transpose(psdt[:], dsgb[:], idn_sb[:]).then_inc(at.h, 1)))
        nav4 = av.bump()
        p.ve(lambda e, nat2=nat2, nav4=nav4: (
            e.wait_ge(at.h, nat2),
            e.tensor_copy(dT_sb[:], psdt[:]).then_inc(av.h, 1)))
        for h in range(16):
            nat3 = at.bump()
            p.te(lambda e, h=h, nav4=nav4, nat3=nat3, cmw=cmb.n: (
                e.wait_ge(av.h, nav4),
                (e.wait_ge(cmb.h, cmw) if cmw > 0 else None),
                e.matmul(psb[:], selh_sb[:, 128 * h:128 * (h + 1)], dT_sb[:],
                         start=True, stop=True).then_inc(at.h, 1)))
            ncm = cmb.bump()
            p.ve(lambda e, h=h, buf=buf, nat3=nat3, ncm=ncm: (
                e.wait_ge(at.h, nat3),
                e.tensor_copy(dbc[:], psb[:]),
                e.tensor_sub(vd_sb[:],
                             vfc[:, buf * D + 128 * h:buf * D + 128 * (h + 1)],
                             voc[:, buf * D + 128 * h:buf * D + 128 * (h + 1)]),
                e.tensor_mul(vd_sb[:], vd_sb[:], dbc[:]),
                e.tensor_add(outT[:, buf * D + 128 * h:buf * D + 128 * (h + 1)],
                             vd_sb[:],
                             voc[:, buf * D + 128 * h:buf * D + 128 * (h + 1)])
                .then_inc(cmb.h, 1)))
        nmm = mm.bump()
        p.te(lambda e, j=j, buf=buf, nmm=nmm, ncmb=cmb.n, nw=nw, free=free: (
            e.wait_ge(cmb.h, ncmb),
            (e.wait_ge(wdone.h, nw) if j == 0 else None),
            (e.wait_ge(free[0].h, free[1]) if free else None),
            [e.matmul(y_ps[n][:], outT[:, buf * D + 128 * c:buf * D + 128 * (c + 1)],
                      wt_sb[:, c * D + 512 * n:c * D + 512 * (n + 1)],
                      start=(c == 0), stop=(c == NDC - 1))
             for c in range(NDC) for n in range(4)][-1].then_inc(mm.h, 1)))
        free = pl.emit(j, [yp[:] for yp in y_ps],
                       [(y_ps[n][:], 512 * n, 512) for n in range(4)], (mm, nmm))
    pl.finish()


def build_attn0_prog(debug=False):
    p = Prog("attn0")
    lnt_in = p.din("lnt", [D, S], BF16)
    wq_in = p.din("wq", [512, D], F32)
    wk_in = p.din("wk", [512, D], F32)
    wv_in = p.din("wv", [512, D], F32)
    cosp_in = p.din("cosp", [128, 2 * S], BF16)
    sinp_in = p.din("sinp", [128, 2 * S], BF16)
    gate_in = p.din("gate", [1, 4], F32)
    idn = p.din("idn", [128, 128], BF16)
    ct_out = p.dout("ct", [512, S], BF16)
    emit_attn0(p, lambda ts: lnt_in[:, 512 * ts:512 * (ts + 1)],
               wq_in, wk_in, wv_in, cosp_in, sinp_in, gate_in, idn, ct_out)
    return p.finalize()


def emit_attn0(p, lnt_tile, wq_in, wk_in, wv_in, cosp_in, sinp_in, gate_in,
               idn, ct_out, debug=False, w_wait=None, lnt_wait=None):
    if debug:
        dbgq = p.dout("dbgq", [128, 2 * S], BF16)
        dbgk = p.dout("dbgk", [128, 2 * S], BF16)
        dbgv = p.dout("dbgv", [128, 2 * S], BF16)

    tp = [p.ps(f"tp{i}", [128, 1024], BF16) for i in range(2)]
    ps0 = p.ps("ps0", [128, 512], F32)
    ps1 = p.ps("ps1", [128, 512], F32)
    ps2 = p.ps("ps2", [128, 512], F32)
    ps3 = p.ps("ps3", [128, 512], F32)
    ps4 = p.ps("ps4", [128, 512], F32)
    ps5 = p.ps("ps5", [128, 512], F32)
    psDn = ps4[0:1, 0:512]
    psLd = ps2[0:1, 0:512]

    idn_sb = p.sb("idn_sb", [128, 128], BF16)
    negc = p.sb("negc", [128, 1], F32)
    ones128 = p.sb("ones128", [128, 1], BF16)
    one1 = p.sb("one1", [1, 128], F32)
    g_sb = p.sb("g_sb", [1, 4], F32)
    omg_sb = p.sb("omg_sb", [1, 4], F32)
    qTr = p.sb("qTr", [128, 2 * S], BF16)
    kTr = p.sb("kTr", [128, 2 * S], BF16)
    vT2 = p.sb("vT2", [128, 2 * S], BF16)
    mem_sb = p.sb("mem_sb", [128, 129], BF16)
    qf_sb = p.sb("qf_sb", [128, 512], BF16)
    qmin_sb = p.sb("qmin_sb", [128, 512], F32)
    rd1 = p.sb("rd1", [1, 512], F32)
    rd2 = p.sb("rd2", [1, 512], F32)
    rdd = p.sb("rdd", [1, 512], F32)
    ab_sb = p.sb("ab_sb", [128, 512], F32)
    bb_sb = p.sb("bb_sb", [128, 512], F32)
    ct_sb = p.sb("ct_sb", [128, 2 * 512], BF16)

    ld0 = p.sem()
    n0 = ld0.bump(16 * 2)
    p.sy(lambda e, w_wait=w_wait: (
        (e.wait_ge(w_wait[0].h, w_wait[1]) if w_wait else None),
        e.dma_start(out=idn_sb[:], in_=idn).then_inc(ld0.h, 16),
        e.dma_start(out=g_sb[:], in_=gate_in).then_inc(ld0.h, 16)))
    csld = p.sem()
    vc0 = p.sem()
    sc0 = p.sem()
    nsc0 = sc0.bump()
    p.sc(lambda e, n0=n0: (e.wait_ge(ld0.h, n0),
                           e.activation(g_sb[:], g_sb[:], AF.Sigmoid).then_inc(sc0.h, 1)))
    nvc0 = vc0.bump()
    p.ve(lambda e, nsc0=nsc0: (
        e.wait_ge(sc0.h, nsc0),
        e.memset(negc[:], -CEXP), e.memset(ones128[:], 1.0),
        e.memset(one1[:], 1.0),
        e.tensor_scalar(omg_sb[:], g_sb[:], -1.0, 1.0, OP.mult, OP.add)
        .then_inc(vc0.h, 1)))
    p.te(lambda e, n0=n0: e.wait_ge(ld0.h, n0))

    gld = p.swdge_sem()
    wtt = p.sem()
    wev = p.sem()
    lnld = p.sem()
    pmm = p.sem()
    pro = p.sem()
    kfs = p.sem()
    kfe = p.sem()
    vtt = p.sem()
    vev = p.sem()
    mmm = p.sem()
    mev = p.sem()
    qfv = p.sem()
    qfe = p.sem()
    fmm = p.sem()
    fpt = p.sem()
    fac = p.sem()
    fpo = p.sem()
    fden = p.sem()
    fmo = p.sem()
    cst = p.swdge_sem()
    pbar = p.sem()
    pbar0 = pbar.n

    SCL = 1.0 / float(np.sqrt(128.0))
    bank_last_exp = [0, 0]
    slot_last_den = [0] * 32

    for pi in range(2):
        sp = p.scope()
        lnt_sb = p.sbs(sp, "lnt_sb%d" % pi, [128, 2 * 8192], BF16)
        wT6 = p.sbs(sp, "wT6%d" % pi, [128, 6 * D], BF16)
        wrow = p.sbs(sp, "wrow%d" % pi, [128, D], BF16)
        rtmp1 = p.sbs(sp, "rtmp1%d" % pi, [128, 512], BF16)
        rtmp2 = p.sbs(sp, "rtmp2%d" % pi, [128, 512], BF16)
        cos_sb = p.sbs(sp, "cos_sb%d" % pi, [128, S], BF16)
        sin_sb = p.sbs(sp, "sin_sb%d" % pi, [128, S], BF16)

        ncs_ld = csld.bump(32)
        p.sy(lambda e, pi=pi, ncs_ld=ncs_ld, pb=pi, cos_sb=cos_sb,
                    sin_sb=sin_sb: (
            (e.wait_ge(pbar.h, pbar0 + pb) if pb > 0 else None),
            e.dma_start(out=cos_sb[:], in_=cosp_in[:, pi * S:(pi + 1) * S]).then_inc(csld.h, 16),
            e.dma_start(out=sin_sb[:], in_=sinp_in[:, pi * S:(pi + 1) * S]).then_inc(csld.h, 16)))

        for si, (w_in, ri) in enumerate(((wq_in, 0), (wq_in, 1), (wk_in, 0),
                                         (wk_in, 1), (wv_in, 0), (wv_in, 1))):
            rowbase = 128 * (2 * pi + ri)
            nld = gld.bump(16)
            p.gp(lambda e, w_in=w_in, rowbase=rowbase, nld=nld, ttw=wtt.n, pb=pi,
                        si=si, w_wait=w_wait: (
                (e.wait_ge(w_wait[0].h, w_wait[1])
                 if (w_wait and pb == 0 and si == 0) else None),
                (e.wait_ge(pbar.h, pbar0 + pb) if (pb > 0 and si == 0) else None),
                (e.wait_ge(wtt.h, ttw) if ttw > 0 else None),
                e.dma_start(out=wrow[:], in_=w_in[rowbase:rowbase + 128, :])
                .then_inc(gld.h, 16)))
            for half in range(2):
                ntt = wtt.bump()
                p.te(lambda e, half=half, nld=nld, ntt=ntt, evw=wev.n - 1: (
                    e.wait_ge(gld.h, nld),
                    (e.wait_ge(wev.h, evw) if evw > 0 else None),
                    [e.transpose(tp[half][:, 128 * t:128 * (t + 1)],
                                 wrow[:, 128 * (8 * half + t):128 * (8 * half + t + 1)],
                                 idn_sb[:]) for t in range(8)][-1].then_inc(wtt.h, 1)))
                nev = wev.bump()
                p.ve(lambda e, si=si, half=half, ntt=ntt: (
                    e.wait_ge(wtt.h, ntt),
                    [e.tensor_copy(wT6[:, si * D + 128 * (8 * half + t):si * D + 128 * (8 * half + t) + 128],
                                   tp[half][:, 128 * t:128 * (t + 1)])
                     for t in range(8)][-1].then_inc(wev.h, 1)))
        nwev_p = wev.n
        psl = (ps0, ps1, ps2, ps3, ps4, ps5)
        for ts in range(16):
            nll = lnld.bump(16)
            pmwar = pmm.n - 1
            p.sy(lambda e, ts=ts, nll=nll, pmwar=pmwar, pb=pi, lnt_wait=lnt_wait: (
                (e.wait_ge(lnt_wait[0].h, lnt_wait[1])
                 if (lnt_wait and pb == 0 and ts == 0) else None),
                (e.wait_ge(pbar.h, pbar0 + pb) if (pb > 0 and ts == 0) else None),
                (e.wait_ge(pmm.h, pmwar) if pmwar > 0 else None),
                e.dma_start(out=lnt_sb.ap()[:, (ts % 2) * 8192:(ts % 2 + 1) * 8192]
                            .rearrange("p (c t) -> p c t", c=16),
                            in_=lnt_tile(ts)
                            .rearrange("(c p) t -> p c t", p=128)).then_inc(lnld.h, 16)))
            nmm = pmm.bump()
            p.te(lambda e, ts=ts, nll=nll, nmm=nmm, nwev=nwev_p, prw=pro.n: (
                e.wait_ge(lnld.h, nll),
                (e.wait_ge(wev.h, nwev) if ts == 0 else None),
                (e.wait_ge(pro.h, prw) if prw > 0 else None),
                [e.matmul(psl[si][0:128, 0:512],
                          wT6[:, si * D + 128 * i:si * D + 128 * (i + 1)],
                          lnt_sb[:, (ts % 2) * 8192 + 512 * i:(ts % 2) * 8192 + 512 * (i + 1)],
                          start=(i == 0), stop=(i == NDC - 1))
                 for si in range(6) for i in range(NDC)][-1].then_inc(pmm.h, 1)))
            npr = pro.bump()
            tsl = slice(512 * ts, 512 * (ts + 1))
            ts2 = slice(S + 512 * ts, S + 512 * (ts + 1))
            p.ve(lambda e, nmm=nmm, npr=npr, tsl=tsl, ts2=ts2, ncs_ld=ncs_ld,
                        cos_sb=cos_sb, sin_sb=sin_sb: (
                e.wait_ge(pmm.h, nmm),
                e.wait_ge(csld.h, ncs_ld),
                e.tensor_mul(rtmp1[:], ps0[0:128, 0:512], cos_sb[:, tsl]),
                e.tensor_mul(rtmp2[:], ps1[0:128, 0:512], sin_sb[:, tsl]),
                e.tensor_sub(qTr.ap()[:, tsl], rtmp1[:], rtmp2[:]),
                e.tensor_mul(rtmp1[:], ps1[0:128, 0:512], cos_sb[:, tsl]),
                e.tensor_mul(rtmp2[:], ps0[0:128, 0:512], sin_sb[:, tsl]),
                e.tensor_add(qTr.ap()[:, ts2], rtmp1[:], rtmp2[:]),
                e.tensor_mul(rtmp1[:], ps2[0:128, 0:512], cos_sb[:, tsl]),
                e.tensor_mul(rtmp2[:], ps3[0:128, 0:512], sin_sb[:, tsl]),
                e.tensor_sub(kTr.ap()[:, tsl], rtmp1[:], rtmp2[:]),
                e.tensor_mul(rtmp1[:], ps3[0:128, 0:512], cos_sb[:, tsl]),
                e.tensor_mul(rtmp2[:], ps2[0:128, 0:512], sin_sb[:, tsl]),
                e.tensor_add(kTr.ap()[:, ts2], rtmp1[:], rtmp2[:]),
                e.tensor_copy(vT2.ap()[:, tsl], ps4[0:128, 0:512]),
                e.tensor_copy(vT2.ap()[:, ts2], ps5[0:128, 0:512]).then_inc(pro.h, 1)))
        npro_p = pro.n
        if debug and pi == 0:
            dbst = p.sem()
            nd = dbst.bump(48)
            p.gp(lambda e, npro=npro_p: (
                e.wait_ge(pro.h, npro),
                e.dma_start(out=dbgq, in_=qTr[:]).then_inc(dbst.h, 16),
                e.dma_start(out=dbgk, in_=kTr[:]).then_inc(dbst.h, 16),
                e.dma_start(out=dbgv, in_=vT2[:]).then_inc(dbst.h, 16)))
            p.sy(lambda e, nd=nd: e.wait_ge(dbst.h, nd))
        sp.close()
        sf = p.scope()
        vaug = p.sbs(sf, "vaug%d" % pi, [128, 64 * 129], BF16)
        kfsm = p.sbs(sf, "kfsm%d" % pi, [128, S], BF16)
        kfblk = p.sbs(sf, "kfblk%d" % pi, [128, 2048], BF16)
        ftmp = p.sbs(sf, "ftmp%d" % pi, [128, 2048], F32)
        pt_sb = p.sbs(sf, "pt%d" % pi, [128, 32 * 512], BF16)
        tpf0 = tp[0].ap()[:, 0:1024].bitcast(F32)
        tpf1 = tp[1].ap()[:, 0:1024].bitcast(F32)

        nvo = vc0.n
        for hh in range(2):
            off = hh * S
            hidx = pi + 2 * hh
            nvz = vev.bump()
            p.ve(lambda e, npro=npro_p, nvz=nvz, facw=fac.n: (
                e.wait_ge(pro.h, npro),
                (e.wait_ge(fac.h, facw) if facw > 0 else None),
                e.memset(vaug.ap().rearrange("p (c w) -> p c w", w=129)[:, :, 128:129],
                         1.0).then_inc(vev.h, 1)))
            for g in range(8):
                ntt = vtt.bump()
                fpw = fpo.n if g == 0 else 0
                p.te(lambda e, g=g, off=off, ntt=ntt, npro=npro_p, evw=vev.n - 1,
                            nvz=nvz, fpw=fpw: (
                    (e.wait_ge(pro.h, npro) if g == 0 else None),
                    (e.wait_ge(fpo.h, fpw) if fpw > 0 else None),
                    (e.wait_ge(vev.h, evw) if evw > 0 else None),
                    [e.transpose(tp[g % 2][:, 128 * t:128 * (t + 1)],
                                 vT2.ap()[:, off + 128 * (8 * g + t):off + 128 * (8 * g + t + 1)],
                                 idn_sb[:]) for t in range(8)][-1].then_inc(vtt.h, 1)))
                nev = vev.bump()
                p.ve(lambda e, g=g, ntt=ntt, nvz=nvz: (
                    e.wait_ge(vtt.h, ntt),
                    e.wait_ge(vev.h, nvz),
                    [e.tensor_copy(vaug[:, 129 * (8 * g + t):129 * (8 * g + t) + 128],
                                   tp[g % 2][:, 128 * t:128 * (t + 1)])
                     for t in range(8)][-1].then_inc(vev.h, 1)))
            for blk in range(4):
                bo = off + 2048 * blk
                nkf1 = kfs.bump()
                p.ve(lambda e, bo=bo, nkf1=nkf1, npro=npro_p, kew=kfe.n: (
                    (e.wait_ge(pro.h, npro) if nkf1 == 1 else None),
                    (e.wait_ge(kfe.h, kew) if kew > 0 else None),
                    e.tensor_scalar(ftmp[:], kTr.ap()[:, bo:bo + 2048], 0.0, None,
                                    OP.min).then_inc(kfs.h, 1)))
                nke = kfe.bump()
                p.sc(lambda e, nkf1=nkf1, nke=nke, kfw=kfs.n: (
                    e.wait_ge(kfs.h, nkf1),
                    (e.wait_ge(kfs.h, kfw) if kfw > 0 else None),
                    e.activation(ftmp[:], ftmp[:], AF.Exp).then_inc(kfe.h, 1)))
                nkf2 = kfs.bump()
                p.ve(lambda e, bo=bo, nke=nke, nkf2=nkf2, ttw=vtt.n: (
                    e.wait_ge(kfe.h, nke),
                    (e.wait_ge(vtt.h, ttw) if ttw > 0 else None),
                    e.scalar_tensor_tensor(kfblk[:], kTr.ap()[:, bo:bo + 2048], 1.0,
                                           ftmp[:], OP.add, OP.max).then_inc(kfs.h, 1)))
                for g in range(2):
                    ntt = vtt.bump()
                    p.te(lambda e, g=g, blk=blk, ntt=ntt, nkf2=nkf2, evw=vev.n - 1: (
                        e.wait_ge(kfs.h, nkf2),
                        (e.wait_ge(vev.h, evw) if evw > 0 else None),
                        [e.transpose(tp[g % 2][:, 128 * t:128 * (t + 1)],
                                     kfblk[:, 128 * (8 * g + t):128 * (8 * g + t + 1)],
                                     idn_sb[:]) for t in range(8)][-1].then_inc(vtt.h, 1)))
                    nev = vev.bump()
                    p.ve(lambda e, g=g, blk=blk, ntt=ntt, mw=mmm.n: (
                        e.wait_ge(vtt.h, ntt),
                        (e.wait_ge(mmm.h, mw) if (mw > 0 and blk == 0 and g == 0) else None),
                        [e.tensor_copy(kfsm[:, 2048 * blk + 128 * (8 * g + t):2048 * blk + 128 * (8 * g + t) + 128],
                                       tp[g % 2][:, 128 * t:128 * (t + 1)])
                         for t in range(8)][-1].then_inc(vev.h, 1)))
            nvev_h = vev.n
            nme = mmm.bump()
            p.te(lambda e, nvev=nvev_h, nme=nme, mevw=mev.n, fpw=fpo.n,
                        vaug=vaug: (
                e.wait_ge(vev.h, nvev),
                (e.wait_ge(mev.h, mevw) if mevw > 0 else None),
                (e.wait_ge(fpo.h, fpw) if fpw > 0 else None),
                [e.matmul(ps5.ap()[0:128, 0:129], kfsm[:, 128 * c:128 * (c + 1)],
                          vaug[:, 129 * c:129 * (c + 1)],
                          start=(c == 0), stop=(c == 63)) for c in range(64)][-1]
                .then_inc(mmm.h, 1)))
            nmv = mev.bump()
            p.ve(lambda e, nme=nme, nmv=nmv: (
                e.wait_ge(mmm.h, nme),
                e.tensor_copy(mem_sb[:], ps5.ap()[0:128, 0:129]).then_inc(mev.h, 1)))
            for qt in range(16):
                nqf1 = qfv.bump()
                tsl = slice(off + 512 * qt, off + 512 * (qt + 1))
                p.ve(lambda e, tsl=tsl, nqf1=nqf1, qew=qfe.n: (
                    (e.wait_ge(qfe.h, qew) if qew > 0 else None),
                    e.tensor_scalar(qmin_sb[:], qTr.ap()[:, tsl], 0.0, None,
                                    OP.min).then_inc(qfv.h, 1)))
                nqe = qfe.bump()
                p.sc(lambda e, nqf1=nqf1, nqe=nqe: (
                    e.wait_ge(qfv.h, nqf1),
                    e.activation(qmin_sb[:], qmin_sb[:], AF.Exp).then_inc(qfe.h, 1)))
                nqf2 = qfv.bump()
                p.ve(lambda e, tsl=tsl, nqe=nqe, nqf2=nqf2, fw=fmo.n: (
                    e.wait_ge(qfe.h, nqe),
                    (e.wait_ge(fmo.h, fw) if fw > 0 else None),
                    e.scalar_tensor_tensor(qf_sb[:], qTr.ap()[:, tsl], 1.0,
                                           qmin_sb[:], OP.add, OP.max)
                    .then_inc(qfv.h, 1)))
                psS = (ps0, ps2)
                for H in range(2):
                    this_half_exp = [0] * 32
                    for s in range(32):
                        kc = 32 * H + s
                        war = bank_last_exp[kc % 2]
                        nfm = fmm.bump()
                        fw = fpo.n if (H == 0 and s == 0) else 0
                        p.te(lambda e, kc=kc, qt=qt, off=off, war=war, nfm=nfm,
                                    fw=fw: (
                            (e.wait_ge(fpo.h, fw) if fw > 0 else None),
                            (e.wait_ge(fpt.h, war) if war > 0 else None),
                            e.matmul(psS[kc % 2][0:128, 0:512],
                                     kTr.ap()[:, off + 128 * kc:off + 128 * (kc + 1)],
                                     qTr.ap()[:, off + 512 * qt:off + 512 * (qt + 1)],
                                     start=True, stop=True).then_inc(fmm.h, 1)))
                        sw = slot_last_den[s]
                        nfp = fpt.bump()
                        p.sc(lambda e, s=s, kcp=kc % 2, nfm=nfm, sw=sw,
                                    pt_sb=pt_sb: (
                            e.wait_ge(fmm.h, nfm),
                            (e.wait_ge(fden.h, sw) if sw > 0 else None),
                            e.activation(pt_sb[:, 512 * s:512 * (s + 1)],
                                         psS[kcp][0:128, 0:512], AF.Exp,
                                         bias=negc[:, 0:1], scale=SCL)
                            .then_inc(fpt.h, 1)))
                        bank_last_exp[kc % 2] = fpt.n
                        this_half_exp[s] = fpt.n
                    psO = ps1 if H == 0 else ps3
                    psD = ps4 if H == 0 else ps5
                    for s in range(32):
                        kc = 32 * H + s
                        xw = this_half_exp[s]
                        if s == 31:
                            nfa = fac.bump()
                            p.te(lambda e, kc=kc, s=s, xw=xw, psO=psO, nfa=nfa,
                                        vaug=vaug, pt_sb=pt_sb: (
                                e.wait_ge(fpt.h, xw),
                                e.matmul(psO[0:128, 0:512],
                                         vaug[:, 129 * kc:129 * kc + 128],
                                         pt_sb[:, 512 * s:512 * (s + 1)],
                                         start=False, stop=True)
                                .then_inc(fac.h, 1)))
                        else:
                            p.te(lambda e, kc=kc, s=s, xw=xw, psO=psO,
                                        vaug=vaug, pt_sb=pt_sb: (
                                e.wait_ge(fpt.h, xw),
                                e.matmul(psO[0:128, 0:512],
                                         vaug[:, 129 * kc:129 * kc + 128],
                                         pt_sb[:, 512 * s:512 * (s + 1)],
                                         start=(s == 0), stop=False)))
                    for s in range(32):
                        nfd = fden.bump()
                        mw = nmv if (qt == 0 and H == 1 and s == 0) else 0
                        p.te(lambda e, s=s, psD=psD, nfd=nfd, mw=mw,
                                    pt_sb=pt_sb: (
                            (e.wait_ge(mev.h, mw) if mw > 0 else None),
                            e.matmul(psD[0:1, 0:512], ones128[:],
                                     pt_sb[:, 512 * s:512 * (s + 1)],
                                     start=(s == 0), stop=(s == 31))
                            .then_inc(fden.h, 1)))
                        slot_last_den[s] = fden.n
                nmo = fmo.bump()
                p.te(lambda e, nmo=nmo, qv=qfv.n, mw=nmv, q0=(qt == 0),
                            vw=nvev_h: (
                    e.wait_ge(qfv.h, qv),
                    (e.wait_ge(mev.h, mw) if q0 else None),
                    (e.wait_ge(vev.h, vw) if q0 else None),
                    e.matmul(tpf0[0:128, 0:512], mem_sb[:, 0:128], qf_sb[:],
                             start=True, stop=True),
                    e.matmul(tpf1[0:1, 0:512], mem_sb[:, 128:129], qf_sb[:],
                             start=True, stop=True).then_inc(fmo.h, 1)))
                nfo1 = fpo.bump()
                p.ve(lambda e, hidx=hidx, nfa=fac.n, nfd=fden.n, nmo=fmo.n,
                            nvo=nvo: (
                    e.wait_ge(fac.h, nfa),
                    e.wait_ge(fden.h, nfd),
                    e.wait_ge(fmo.h, nmo),
                    e.wait_ge(vc0.h, nvo),
                    e.tensor_copy(ab_sb[:], ps1[0:128, 0:512]),
                    e.tensor_add(ab_sb[:], ab_sb[:], ps3[0:128, 0:512]),
                    e.tensor_copy(bb_sb[:], tpf0[0:128, 0:512]),
                    e.tensor_copy(rdd[:], ps4[0:1, 0:512]),
                    e.tensor_add(rdd[:], rdd[:], ps5[0:1, 0:512]),
                    e.reciprocal(rd1[:], tpf1[0:1, 0:512]),
                    e.reciprocal(rd2[:], rdd[:]),
                    e.tensor_scalar(rd1[:], rd1[:], g_sb[0:1, hidx:hidx + 1], None,
                                    OP.mult),
                    e.tensor_scalar(rd2[:], rd2[:], omg_sb[0:1, hidx:hidx + 1], None,
                                    OP.mult).then_inc(fpo.h, 1)))
                nfo2 = fpo.bump()
                p.te(lambda e, nfo1=nfo1, xw=fpt.n: (
                    e.wait_ge(fpo.h, nfo1),
                    e.wait_ge(fpt.h, xw),
                    e.matmul(ps2[0:128, 0:512], one1[:], rd1[:], start=True,
                             stop=True).then_inc(fpo.h, 1)))
                nfo3 = fpo.bump()
                p.ve(lambda e, nfo2=nfo2: (
                    e.wait_ge(fpo.h, nfo2),
                    e.tensor_mul(bb_sb[:], bb_sb[:], ps2[0:128, 0:512])
                    .then_inc(fpo.h, 1)))
                nfo4 = fpo.bump()
                p.te(lambda e, nfo3=nfo3: (
                    e.wait_ge(fpo.h, nfo3),
                    e.matmul(ps2[0:128, 0:512], one1[:], rd2[:], start=True,
                             stop=True).then_inc(fpo.h, 1)))
                nfo5 = fpo.bump()
                cbuf = qt % 2
                p.ve(lambda e, nfo4=nfo4, cbuf=cbuf, stw=cst.n - 16: (
                    e.wait_ge(fpo.h, nfo4),
                    (e.wait_ge(cst.h, stw) if stw > 0 else None),
                    e.tensor_mul(ab_sb[:], ab_sb[:], ps2[0:128, 0:512]),
                    e.tensor_add(ct_sb[:, cbuf * 512:(cbuf + 1) * 512], bb_sb[:],
                                 ab_sb[:]).then_inc(fpo.h, 1)))
                ncs = cst.bump(16)
                p.gp(lambda e, hidx=hidx, qt=qt, cbuf=cbuf, nfo5=nfo5: (
                    e.wait_ge(fpo.h, nfo5),
                    e.dma_start(out=ct_out[128 * hidx:128 * (hidx + 1),
                                           512 * qt:512 * (qt + 1)],
                                in_=ct_sb[:, cbuf * 512:(cbuf + 1) * 512])
                    .then_inc(cst.h, 16)))
        sf.close()
        npb = pbar.bump()
        p.ve(lambda e, nfo=fpo.n, npb=npb: (
            e.wait_ge(fpo.h, nfo), e.engine_nop().then_inc(pbar.h, 1)))
    p.sy(lambda e, n=cst.n: e.wait_ge(cst.h, n))


def build_attn2_prog(debug=False, stages=4, nheads=4, nchunks=64, flash_stages=7):
    p = Prog("attn2")
    lnt_in = p.din("lnt", [D, S], BF16)
    wq_in = p.din("wq", [512, D], F32)
    wk_in = p.din("wk", [512, D], F32)
    wv_in = p.din("wv", [512, D], F32)
    wg_in = p.din("wg", [512, D], F32)
    ws1_in = p.din("ws1", [16, D], F32)
    ws2_in = p.din("ws2", [256, 16], F32)
    bs2_in = p.din("bs2p", [128, 2], F32)
    gnw_in = p.din("gnw", [128, 1], F32)
    cos_in = p.din("cosr", [128, S], BF16)
    sin_in = p.din("sinr", [128, S], BF16)
    pmat_in = p.din("pmat", [128, 128], BF16)
    cmaskA_in = p.din("cmaskA", [128, 128], BF16)
    cmaskW_in = p.din("cmaskW", [128, 128], BF16)
    idn = p.din("idn", [128, 128], BF16)
    idn32_in = p.din("idn32", [128, 128], F32)
    ct_out = p.dout("ct", [512, S], BF16)
    emit_attn2(p, lambda ts: lnt_in[:, 512 * ts:512 * (ts + 1)],
               wq_in, wk_in, wv_in, wg_in, ws1_in, ws2_in, bs2_in, gnw_in,
               cos_in, sin_in, pmat_in, cmaskA_in, cmaskW_in, idn, idn32_in,
               ct_out, debug=debug, stages=stages, nheads=nheads,
               nchunks=nchunks, flash_stages=flash_stages)
    return p.finalize()


def emit_attn2(p, lnt_tile, wq_in, wk_in, wv_in, wg_in, ws1_in, ws2_in,
               bs2_in, gnw_in, cos_in, sin_in, pmat_in, cmaskA_in, cmaskW_in,
               idn, idn32_in, ct_out, debug=False, stages=4, nheads=4,
               nchunks=64, w_wait=None, lnt_wait=None, flash_stages=7):
    if debug:
        dbg_es = p.dout("dbg_es", [128, 2 * S], BF16)
        dbg_enz = p.dout("dbg_enz", [128, 2 * S], BF16)
        dbg_q = p.dout("dbg_q", [128, S], BF16)
        dbg_k = p.dout("dbg_k", [128, S], BF16)
        dbg_cb = p.dout("dbg_cb", [128, 512], BF16)
        dbg_cf = p.dout("dbg_cf", [128, 66], F32)
        dbg_e = p.dout("dbg_e", [128, 2], F32)
        dbg_cf2 = p.dout("dbg_cf2", [128, 128], BF16)

    tp = [p.ps(f"tp{i}", [128, 1024], BF16) for i in range(2)]
    f = [p.ps(f"f{i}", [128, 512], F32) for i in range(6)]

    idn_sb = p.sb("idn_sb", [128, 128], BF16)
    idn32_sb = p.sb("idn32_sb", [128, 128], F32)
    pmat_sb = p.sb("pmat_sb[:]", [128, 128], BF16)
    cmA_sb = p.sb("cmA_sb", [128, 128], BF16)
    cmW_sb = p.sb("cmW_sb", [128, 128], BF16)
    gnw_sb = p.sb("gnw_sb", [128, 1], F32)
    bs2_sb = p.sb("bs2_sb", [128, 2], F32)
    negc = p.sb("negc", [128, 1], F32)
    ones128 = p.sb("ones128", [128, 1], BF16)
    one1 = p.sb("one1", [1, 128], F32)
    cos_sb = p.sb("cos_sb", [128, S], BF16)
    sin_sb = p.sb("sin_sb", [128, S], BF16)
    esT2 = p.sb("esT2", [128, 2 * S], BF16)
    enzT2 = p.sb("enzT2", [128, 2 * S], BF16)
    qT_h = p.sb("qT_h", [128, S], BF16)
    kT_h = p.sb("kT_h", [128, S], BF16)
    vT_h = p.sb("vT_h", [128, S], BF16)
    gT_h = p.sb("gT_h", [128, S], BF16)
    wT4 = p.sb("wT4", [128, 4 * D], BF16)
    wrow = p.sb("wrow", [128, 1024], BF16)
    lnt_sb = p.sb("lnt_sb", [128, 8192], BF16)
    ws1T = p.sb("ws1T", [128, 256], BF16)
    ws2T = p.sb("ws2T", [16, 256], BF16)
    z1t = p.sb("z1t", [16, 512], BF16)
    U = p.sb("U", [128, 2048], BF16)
    qraw = U.ap()[:, 1024:1536]
    rtmp1 = U.ap()[:, 0:512]
    rtmp2 = U.ap()[:, 512:1024]
    ok_sb = p.sb("ok_sb", [128, 2 * 64], F32)
    eq_sb = p.sb("eq_sb", [128, 2 * 64], BF16)
    den_sb = p.sb("den_sb", [128, 2], F32)
    p2_sb = p.sb("p2_sb", [128, 2 * 64], BF16)
    Mst_f = p.sb("Mst_f", [128, 64], F32)
    MstB2 = p.sb("MstB2", [128, 3 * 64], BF16)
    Sst_f = p.sb("Sst_f", [64, 128], F32)
    SstB2 = p.sb("SstB2", [64, 3 * 128], BF16)
    sq_sb = p.sb("sq_sb", [128, 2 * 128], BF16)
    sg2_sb = p.sb("sg2_sb", [128, 2 * 128], BF16)
    f3c_sb = p.sb("f3c_sb", [128, 2 * 128], BF16)
    e1_sb = p.sb("e1_sb", [128, 1], F32)
    e2_sb = p.sb("e2_sb", [128, 2], F32)
    e3_sb = p.sb("e3_sb", [128, 1], F32)
    rstd_sb = p.sb("rstd_sb", [128, 1], F32)
    rbc_sb = U.ap()[:, 1792:1920]
    ct_sb = p.sb("ct_sb", [128, 256], BF16)
    dbge_sb = ct_sb.ap().bitcast(F32)[:, 0:2] if debug else None

    erow_sb = z1t.ap()[0:1, 0:256].bitcast(F32)

    ld0 = p.sem()
    n0 = ld0.bump(16 * 9)
    p.sy(lambda e, w_wait=w_wait: (
        (e.wait_ge(w_wait[0].h, w_wait[1]) if w_wait else None),
        e.dma_start(out=idn_sb[:], in_=idn).then_inc(ld0.h, 16),
        e.dma_start(out=idn32_sb[:], in_=idn32_in).then_inc(ld0.h, 16),
        e.dma_start(out=pmat_sb[:], in_=pmat_in).then_inc(ld0.h, 16),
        e.dma_start(out=cmA_sb[:], in_=cmaskA_in).then_inc(ld0.h, 16),
        e.dma_start(out=cmW_sb[:], in_=cmaskW_in).then_inc(ld0.h, 16),
        e.dma_start(out=gnw_sb[:], in_=gnw_in).then_inc(ld0.h, 16),
        e.dma_start(out=bs2_sb[:], in_=bs2_in).then_inc(ld0.h, 16),
        e.dma_start(out=cos_sb[:], in_=cos_in).then_inc(ld0.h, 16),
        e.dma_start(out=sin_sb[:], in_=sin_in).then_inc(ld0.h, 16)))
    ld0g = p.swdge_sem()
    ng1 = ld0g.bump(16)
    p.gp(lambda e, w_wait=w_wait: (
        (e.wait_ge(w_wait[0].h, w_wait[1]) if w_wait else None),
        e.dma_start(out=lnt_sb[0:16, 0:D], in_=ws1_in).then_inc(ld0g.h, 16)))
    vc0 = p.sem()
    nvc0 = vc0.bump()
    p.ve(lambda e, n0=n0: (
        e.wait_ge(ld0.h, n0),
        e.memset(negc[:], -CEXP), e.memset(ones128[:], 1.0),
        e.memset(one1[:], 1.0), e.memset(qT_h[:], 1.0).then_inc(vc0.h, 1)))
    p.te(lambda e, n0=n0, nvc0=nvc0: (e.wait_ge(ld0.h, n0),
                                      e.wait_ge(vc0.h, nvc0)))
    p.sc(lambda e, nvc0=nvc0: e.wait_ge(vc0.h, nvc0))

    stt = p.sem()
    sve = p.sem()
    ssc = p.sem()
    lnld = p.sem()
    vch = Chain(p, "vch")

    n_t1 = stt.bump()
    p.te(lambda e, n_t1=n_t1, ng1=ng1: (
        e.wait_ge(ld0g.h, ng1),
        [e.transpose(tp[0].ap()[0:128, 16 * c:16 * (c + 1)],
                     lnt_sb[0:16, 128 * c:128 * (c + 1)], idn_sb[0:16, 0:16])
         for c in range(16)][-1].then_inc(stt.h, 1)))
    nws2 = ld0g.bump(32)
    p.gp(lambda e, n_t1=n_t1: (
        e.wait_ge(stt.h, n_t1),
        e.dma_start(out=wrow[0:128, 0:16], in_=ws2_in[0:128, :]).then_inc(ld0g.h, 16),
        e.dma_start(out=wrow[0:128, 16:32], in_=ws2_in[128:256, :]).then_inc(ld0g.h, 16)))
    n_v1 = sve.bump()
    p.ve(lambda e, n_t1=n_t1: (
        e.wait_ge(stt.h, n_t1),
        e.tensor_copy(ws1T[:, 0:256], tp[0].ap()[0:128, 0:256]).then_inc(sve.h, 1)))
    n_t2 = stt.bump()
    p.te(lambda e, n_v1=n_v1, n_t2=n_t2, nws2=nws2: (
        e.wait_ge(sve.h, n_v1),
        e.wait_ge(ld0g.h, nws2),
        e.transpose(tp[0].ap()[0:16, 0:128], wrow[0:128, 0:16], idn_sb[:]),
        e.transpose(tp[0].ap()[0:16, 128:256], wrow[0:128, 16:32], idn_sb[:])
        .then_inc(stt.h, 1)))
    n_v2 = sve.bump()
    p.ve(lambda e, n_t2=n_t2: (
        e.wait_ge(stt.h, n_t2),
        e.tensor_copy(ws2T[0:16, 0:256], tp[0].ap()[0:16, 0:256]).then_inc(sve.h, 1)))

    for ts in range(16):
        nll = lnld.bump(16)
        mmwar = stt.n - 1
        p.sy(lambda e, ts=ts, nll=nll, mmwar=mmwar, lnt_wait=lnt_wait,
                    nws1t=n_t1: (
            (e.wait_ge(lnt_wait[0].h, lnt_wait[1])
             if (lnt_wait and ts == 0) else None),
            (e.wait_ge(stt.h, nws1t) if ts == 0 else None),
            (e.wait_ge(stt.h, mmwar) if ts > 0 else None),
            e.dma_start(out=lnt_sb.ap().rearrange("p (c t) -> p c t", c=16),
                        in_=lnt_tile(ts)
                        .rearrange("(c p) t -> p c t", p=128)).then_inc(lnld.h, 16)))
        n_tz = stt.bump()
        p.te(lambda e, ts=ts, nll=nll, n_tz=n_tz, n_v2=n_v2, vw=sve.n,
                    scw=ssc.n: (
            e.wait_ge(lnld.h, nll),
            (e.wait_ge(sve.h, n_v2) if ts == 0 else None),
            (e.wait_ge(sve.h, vw) if ts > 0 else None),
            (e.wait_ge(ssc.h, scw) if scw > 0 else None),
            [e.matmul(f[5][0:16, 0:512], ws1T[:, 16 * c:16 * (c + 1)],
                      lnt_sb[:, 512 * c:512 * (c + 1)],
                      start=(c == 0), stop=(c == 15)) for c in range(16)][-1]
            .then_inc(stt.h, 1)))
        n_vz = sve.bump()
        p.ve(lambda e, n_tz=n_tz: (
            e.wait_ge(stt.h, n_tz),
            e.tensor_copy(z1t[0:16, :], f[5][0:16, 0:512]).then_inc(sve.h, 1)))
        for mt in range(2):
            n_ts = stt.bump()
            p.te(lambda e, mt=mt, n_vz=n_vz, n_ts=n_ts, vw=sve.n, scw=ssc.n: (
                e.wait_ge(sve.h, vw),
                (e.wait_ge(ssc.h, scw) if scw > 0 else None),
                e.matmul(f[5][0:128, 0:512], ws2T[0:16, 128 * mt:128 * (mt + 1)],
                         z1t[0:16, :], start=True, stop=True).then_inc(stt.h, 1)))
            n_vs = sve.bump()
            p.ve(lambda e, mt=mt, n_ts=n_ts, scw=ssc.n: (
                e.wait_ge(stt.h, n_ts),
                (e.wait_ge(ssc.h, scw) if scw > 0 else None),
                vch.step(e, e.tensor_scalar(f[5][0:128, 0:512], f[5][0:128, 0:512],
                                            bs2_sb[:, mt:mt + 1], 32.0, OP.add, OP.min)),
                e.tensor_scalar(f[5][0:128, 0:512], f[5][0:128, 0:512], -32.0,
                                None, OP.max).then_inc(sve.h, 1)))
            n_ss = ssc.bump()
            p.sc(lambda e, mt=mt, ts=ts, vw=sve.n: (
                e.wait_ge(sve.h, vw),
                e.activation(esT2.ap()[:, mt * S + 512 * ts:mt * S + 512 * (ts + 1)],
                             f[5][0:128, 0:512], AF.Exp).then_inc(ssc.h, 1)))
    nes = ssc.n
    nzv = 0
    for mt in range(2):
        n_v = sve.bump()
        p.ve(lambda e, mt=mt, nes=nes, vw=sve.n - 1: (
            (e.wait_ge(ssc.h, nes) if mt == 0 else None),
            (e.wait_ge(sve.h, vw) if mt == 1 else None),
            e.tensor_tensor_scan(kT_h[:], qT_h[:],
                                 esT2.ap()[:, mt * S:(mt + 1) * S], 0.0,
                                 OP.mult, OP.add).then_inc(sve.h, 1)))
        n_v2b = sve.bump()

        def _recip(e, mt=mt, n_v=n_v):
            e.wait_ge(sve.h, n_v)
            with p.nc.allow_low_precision(reason="enz = 1/Z in bf16 (multiplicative)"):
                e.reciprocal(enzT2.ap()[:, mt * S:(mt + 1) * S],
                             kT_h[:]).then_inc(sve.h, 1)

        p.ve(_recip)
        nzv = n_v2b
    if debug:
        dbst = p.sem()
        nd = dbst.bump(32)
        p.gp(lambda e, nzv=nzv: (
            e.wait_ge(sve.h, nzv),
            e.dma_start(out=dbg_es, in_=esT2[:]).then_inc(dbst.h, 16),
            e.dma_start(out=dbg_enz, in_=enzT2[:]).then_inc(dbst.h, 16)))
        p.sy(lambda e, nd=nd: e.wait_ge(dbst.h, nd))

    gld = p.swdge_sem()
    wtt = p.sem()
    wev = p.sem()
    pmm = p.sem()
    pro = p.sem()
    ctt = p.sem()
    cve = p.sem()
    csc = p.sem()
    cst = p.swdge_sem()
    hbar = p.sem()
    hbar0 = hbar.n
    SCL = 1.0 / float(np.sqrt(128.0))

    if stages <= 1:
        p.sy(lambda e, n=sve.n: e.wait_ge(sve.h, n))
        return
    ep_state = [0, 0]
    last_s2 = 0
    n_stt_sph = stt.n
    for hi in range(nheads):
        mt, half = hi % 2, hi // 2
        for wi, w_in in enumerate((wq_in, wk_in, wv_in, wg_in)):
            for hf in range(2):
                nld = gld.bump(16)
                p.gp(lambda e, w_in=w_in, hi=hi, wi=wi, hf=hf, nld=nld,
                            ttw=wtt.n, n_t2=n_t2: (
                    (e.wait_ge(stt.h, n_t2) if (hi == 0 and wi == 0 and hf == 0) else None),
                    (e.wait_ge(wtt.h, ttw) if ttw > 0 else None),
                    e.dma_start(out=wrow[:],
                                in_=w_in[128 * hi:128 * (hi + 1),
                                         1024 * hf:1024 * (hf + 1)])
                    .then_inc(gld.h, 16)))
                ntt = wtt.bump()
                p.te(lambda e, hf=hf, nld=nld, ntt=ntt, evw=wev.n - 1: (
                    e.wait_ge(gld.h, nld),
                    (e.wait_ge(wev.h, evw) if evw > 0 else None),
                    [e.transpose(tp[hf][:, 128 * t:128 * (t + 1)],
                                 wrow[:, 128 * t:128 * (t + 1)],
                                 idn_sb[:]) for t in range(8)][-1].then_inc(wtt.h, 1)))
                nev = wev.bump()
                p.ve(lambda e, wi=wi, hf=hf, ntt=ntt, hb=hi, pmw=pmm.n,
                            nzv=nzv: (
                    e.wait_ge(wtt.h, ntt),
                    (e.wait_ge(hbar.h, hbar0 + hb) if (hb > 0 and wi == 0 and hf == 0) else None),
                    (e.wait_ge(sve.h, nzv) if (hb == 0 and wi == 0 and hf == 0) else None),
                    (e.wait_ge(pmm.h, pmw) if pmw > 0 else None),
                    [e.tensor_copy(wT4[:, wi * D + 128 * (8 * hf + t):wi * D + 128 * (8 * hf + t) + 128],
                                   tp[hf][:, 128 * t:128 * (t + 1)])
                     for t in range(8)][-1].then_inc(wev.h, 1)))
        nwev_h = wev.n
        for ts in range(16):
            nll = lnld.bump(16)
            pmwar = pmm.n
            p.sy(lambda e, ts=ts, nll=nll, pmwar=pmwar, hi=hi,
                        n_stt_sph=n_stt_sph: (
                (e.wait_ge(stt.h, n_stt_sph)
                 if (hi == 0 and ts == 0 and n_stt_sph > 0) else None),
                (e.wait_ge(pmm.h, pmwar) if pmwar > 0 else None),
                e.dma_start(out=lnt_sb.ap().rearrange("p (c t) -> p c t", c=16),
                            in_=lnt_tile(ts)
                            .rearrange("(c p) t -> p c t", p=128)).then_inc(lnld.h, 16)))
            nmm = pmm.bump()
            p.te(lambda e, ts=ts, nll=nll, nmm=nmm, nwev=nwev_h, prw=pro.n,
                        nzv=nzv, hi=hi: (
                e.wait_ge(lnld.h, nll),
                (e.wait_ge(wev.h, nwev) if ts == 0 else None),
                (e.wait_ge(sve.h, nzv) if (hi == 0 and ts == 0) else None),
                (e.wait_ge(pro.h, prw) if prw > 0 else None),
                [e.matmul(f[wi][0:128, 0:512],
                          wT4[:, wi * D + 128 * i:wi * D + 128 * (i + 1)],
                          lnt_sb[:, 512 * i:512 * (i + 1)],
                          start=(i == 0), stop=(i == NDC - 1))
                 for wi in range(4) for i in range(NDC)][-1].then_inc(pmm.h, 1)))
            npr1 = pro.bump()
            p.ve(lambda e, nmm=nmm, npr1=npr1: (
                e.wait_ge(pmm.h, nmm),
                e.tensor_copy(qraw[:], f[0][0:128, 0:512]).then_inc(pro.h, 1)))
            ntsw1 = pro.bump()
            p.te(lambda e, npr1=npr1: (
                e.wait_ge(pro.h, npr1),
                e.matmul(f[4][0:128, 0:512], pmat_sb[:], qraw[:],
                         start=True, stop=True).then_inc(pro.h, 1)))
            npr2 = pro.bump()
            tsl = slice(512 * ts, 512 * (ts + 1))
            p.ve(lambda e, ntsw1=ntsw1, npr2=npr2, tsl=tsl: (
                e.wait_ge(pro.h, ntsw1),
                e.tensor_mul(rtmp1, f[0][0:128, 0:512], cos_sb[:, tsl]),
                vch.step(e, e.tensor_mul(rtmp2, f[4][0:128, 0:512], sin_sb[:, tsl])),
                e.tensor_add(qT_h.ap()[:, tsl], rtmp1, rtmp2),
                e.tensor_copy(qraw[:], f[1][0:128, 0:512]).then_inc(pro.h, 1)))
            ntsw2 = pro.bump()
            p.te(lambda e, npr2=npr2: (
                e.wait_ge(pro.h, npr2),
                e.matmul(f[4][0:128, 0:512], pmat_sb[:], qraw[:],
                         start=True, stop=True).then_inc(pro.h, 1)))
            npr3 = pro.bump()
            p.ve(lambda e, ntsw2=ntsw2, npr3=npr3, tsl=tsl: (
                e.wait_ge(pro.h, ntsw2),
                e.tensor_mul(rtmp1, f[1][0:128, 0:512], cos_sb[:, tsl]),
                vch.step(e, e.tensor_mul(rtmp2, f[4][0:128, 0:512], sin_sb[:, tsl])),
                e.tensor_add(kT_h.ap()[:, tsl], rtmp1, rtmp2),
                e.tensor_copy(vT_h.ap()[:, tsl], f[2][0:128, 0:512]),
                e.tensor_copy(gT_h.ap()[:, tsl], f[3][0:128, 0:512]).then_inc(pro.h, 1)))
        npro_h = pro.n
        if debug and hi == 0:
            dbs2 = p.sem()
            nd2 = dbs2.bump(32)
            p.gp(lambda e, npro=npro_h: (
                e.wait_ge(pro.h, npro),
                e.dma_start(out=dbg_q, in_=qT_h[:]).then_inc(dbs2.h, 16),
                e.dma_start(out=dbg_k, in_=kT_h[:]).then_inc(dbs2.h, 16)))
            p.sy(lambda e, nd2=nd2: e.wait_ge(dbs2.h, nd2))
        if stages <= 2:
            p.sy(lambda e, n=pro.n: e.wait_ge(pro.h, n))
            return
        nst0 = cve.bump()
        p.ve(lambda e, npro=npro_h, nst0=nst0: (
            e.wait_ge(pro.h, npro),
            e.memset(Mst_f[:], 0.0), e.memset(Sst_f[:], 0.0),
            e.memset(MstB2[:], 0.0), e.memset(SstB2[:], 0.0).then_inc(cve.h, 1)))
        stx = {k: {} for k in ("v1", "v2", "v4", "v5", "v7", "f3c", "s2",
                               "s1", "t4", "t6b")}

        def _reg(P):
            o = 896 * P
            return {"at": U.ap()[:, o:o + 128],
                    "esc": U.ap()[:, o + 128:o + 192],
                    "enzc": U.ap()[:, o + 192:o + 256],
                    "vc": U.ap()[:, o + 256:o + 384],
                    "kc": U.ap()[:, o + 384:o + 512],
                    "esm": U.ap()[0:64, o + 512:o + 640],
                    "p2T": U.ap()[0:64, o + 640:o + 768],
                    "w2t": U.ap()[:, o + 768:o + 896]}

        def _bank(P):
            return (f[P][0:128, 0:128], f[P][0:128, 256:320],
                    f[2 + P][0:128, 0:128], f[2 + P][0:128, 256:384])

        def stage_P(c):
            P = c % 2
            r = _reg(P)
            bA = _bank(P)[0]
            csl = slice(128 * c, 128 * (c + 1))
            msl = slice(mt * S + 128 * c, mt * S + 128 * (c + 1))
            prow = slice(64 * half, 64 * (half + 1))
            idn64 = idn_sb[64:128, 64:128] if half else idn_sb[0:64, 0:64]
            tpP = tp[P]
            n_t1 = ctt.bump()
            vw = stx["v5"].get(c - 2, 0)
            p.te(lambda e, csl=csl, msl=msl, prow=prow, vw=vw, idn64=idn64,
                        tpP=tpP: (
                (e.wait_ge(cve.h, vw) if vw > 0 else None),
                e.transpose(tpP.ap()[0:128, 0:64], esT2.ap()[prow, msl], idn64),
                e.transpose(tpP.ap()[0:128, 64:128], enzT2.ap()[prow, msl], idn64),
                e.transpose(tpP.ap()[0:128, 128:256], vT_h.ap()[:, csl], idn_sb[:]),
                e.transpose(tpP.ap()[0:128, 256:384], kT_h.ap()[:, csl], idn_sb[:])
                .then_inc(ctt.h, 1)))
            n_v1 = cve.bump()
            p.ve(lambda e, n_t1=ctt.n, r=r, tpP=tpP: (
                e.wait_ge(ctt.h, n_t1),
                e.tensor_copy(r["esc"], tpP.ap()[0:128, 0:64]),
                e.tensor_copy(r["enzc"], tpP.ap()[0:128, 64:128]),
                e.tensor_copy(r["vc"], tpP.ap()[0:128, 128:256]),
                e.tensor_copy(r["kc"], tpP.ap()[0:128, 256:384]).then_inc(cve.h, 1)))
            stx["v1"][c] = cve.n
            n_t1b = ctt.bump()
            p.te(lambda e, n_v1=n_v1, r=r, tpP=tpP: (
                e.wait_ge(cve.h, n_v1),
                e.transpose(tpP.ap()[0:64, 384:512], r["esc"], idn_sb[:])
                .then_inc(ctt.h, 1)))
            n_v1b = cve.bump()
            p.ve(lambda e, n_t1b=n_t1b, r=r, tpP=tpP: (
                e.wait_ge(ctt.h, n_t1b),
                e.tensor_copy(r["esm"], tpP.ap()[0:64, 384:512]).then_inc(cve.h, 1)))
            n_t2 = ctt.bump()
            wf = stx["f3c"].get(c - 2, 0)
            ws = stx["s2"].get(c - 2, 0)
            p.te(lambda e, csl=csl, bA=bA, wf=wf, ws=ws: (
                (e.wait_ge(cve.h, wf) if wf > 0 else None),
                (e.wait_ge(csc.h, ws) if ws > 0 else None),
                e.matmul(bA, kT_h.ap()[:, csl], qT_h.ap()[:, csl],
                         start=True, stop=True).then_inc(ctt.h, 1)))
            n_v2 = cve.bump()
            p.ve(lambda e, n_t2=n_t2, r=r, bA=bA: (
                e.wait_ge(ctt.h, n_t2),
                e.tensor_mul(r["at"], bA, cmA_sb[:]).then_inc(cve.h, 1)))
            stx["v2"][c] = cve.n
            n_t6a = ctt.bump()
            p.te(lambda e, nv1=stx["v1"][c], r=r: (
                e.wait_ge(cve.h, nv1),
                e.matmul(f[4][0:128, 0:64], r["kc"], r["esc"],
                         start=True, stop=True),
                e.matmul(f[4].ap()[0:64, 128:256], r["esc"], r["vc"],
                         start=True, stop=True).then_inc(ctt.h, 1)))
            slot = (c + 1) % 3
            n_v7 = cve.bump()
            p.ve(lambda e, n_t6a=n_t6a, slot=slot: (
                e.wait_ge(ctt.h, n_t6a),
                vch.step(e, e.tensor_add(Mst_f[:], Mst_f[:], f[4][0:128, 0:64])),
                e.tensor_scalar(MstB2[:, 64 * slot:64 * (slot + 1)], Mst_f[:],
                                SCL, None, OP.mult),
                vch.step(e, e.tensor_add(Sst_f[:], Sst_f[:],
                                         f[4].ap()[0:64, 128:256])),
                e.tensor_copy(SstB2[0:64, 128 * slot:128 * (slot + 1)],
                              Sst_f[:]).then_inc(cve.h, 1)))
            stx["v7"][c] = cve.n

        def stage_X1(c):
            P = c % 2
            r = _reg(P)
            _, bX, _, _ = _bank(P)
            csl = slice(128 * c, 128 * (c + 1))
            slot = c % 3
            okP = ok_sb[:, 64 * P:64 * (P + 1)]
            eqP = eq_sb[:, 64 * P:64 * (P + 1)]
            denP = den_sb[:, P:P + 1]
            e2P = e2_sb[:, P:P + 1]
            p2P = p2_sb[:, 64 * P:64 * (P + 1)]
            n_t3 = ctt.bump()
            p.te(lambda e, csl=csl, bX=bX, r=r, nv2=stx["v2"][c], slot=slot: (
                e.wait_ge(cve.h, nv2),
                e.matmul(bX, r["at"], r["esc"], start=True, stop=False),
                e.matmul(bX, qT_h.ap()[:, csl],
                         MstB2[:, 64 * slot:64 * (slot + 1)],
                         start=False, stop=True).then_inc(ctt.h, 1)))
            n_v3 = cve.bump()
            ws1 = stx["s1"].get(c - 2, 0)
            p.ve(lambda e, n_t3=n_t3, okP=okP, bX=bX, r=r, ws1=ws1: (
                e.wait_ge(ctt.h, n_t3),
                (e.wait_ge(csc.h, ws1) if ws1 > 0 else None),
                e.tensor_mul(okP, bX, r["enzc"]).then_inc(cve.h, 1)))
            n_s1 = csc.bump()
            p.sc(lambda e, n_v3=n_v3, okP=okP, eqP=eqP: (
                e.wait_ge(cve.h, n_v3),
                e.activation(eqP, okP, AF.Exp, bias=negc[:, 0:1])
                .then_inc(csc.h, 1)))
            stx["s1"][c] = csc.n
            n_v4a = cve.bump()
            p.ve(lambda e, n_s1=n_s1, denP=denP, eqP=eqP: (
                e.wait_ge(csc.h, n_s1),
                e.reduce_sum(denP, eqP, AX.X).then_inc(cve.h, 1)))
            n_v4 = cve.bump()
            wt4 = stx["t4"].get(c - 2, 0)
            p.ve(lambda e, n_v4a=n_v4a, denP=denP, e2P=e2P, p2P=p2P, eqP=eqP,
                        r=r, wt4=wt4: (
                e.wait_ge(cve.h, n_v4a),
                (e.wait_ge(ctt.h, wt4) if wt4 > 0 else None),
                vch.step(e, e.reciprocal(e2P, denP)),
                vch.step(e, e.tensor_scalar(p2P, eqP, e2P[0:128, 0:1],
                                            None, OP.mult)),
                e.tensor_mul(p2P, p2P, r["enzc"]).then_inc(cve.h, 1)))
            stx["v4"][c] = cve.n

        def stage_X2(c, lvl=4):
            P = c % 2
            r = _reg(P)
            _, _, bW, bO = _bank(P)
            csl = slice(128 * c, 128 * (c + 1))
            slot = c % 3
            p2P = p2_sb[:, 64 * P:64 * (P + 1)]
            tpP = tp[P]
            n_t4 = ctt.bump()
            stx["t4"][c] = n_t4
            p.te(lambda e, nv4=stx["v4"][c], p2P=p2P, tpP=tpP: (
                e.wait_ge(cve.h, nv4),
                e.transpose(tpP.ap()[0:64, 512:640], p2P, idn_sb[:])
                .then_inc(ctt.h, 1)))
            n_v5 = cve.bump()
            p.ve(lambda e, n_t4=n_t4, r=r, tpP=tpP: (
                e.wait_ge(ctt.h, n_t4),
                e.tensor_copy(r["p2T"], tpP.ap()[0:64, 512:640]).then_inc(cve.h, 1)))
            stx["v5"][c] = cve.n
            if lvl < 2:
                stx["t6b"][c] = ctt.n
                stx["f3c"][c] = cve.n
                stx["s2"][c] = csc.n
                return csc.n
            n_t5 = ctt.bump()
            p.te(lambda e, n_v5=n_v5, r=r, bW=bW: (
                e.wait_ge(cve.h, n_v5),
                e.matmul(bW, r["esm"], r["p2T"], start=True, stop=True)
                .then_inc(ctt.h, 1)))
            n_v6 = cve.bump()
            p.ve(lambda e, n_t5=n_t5, r=r, bW=bW: (
                e.wait_ge(ctt.h, n_t5),
                e.tensor_mul(r["w2t"], bW, cmW_sb[:]).then_inc(cve.h, 1)))
            if lvl < 3:
                stx["t6b"][c] = ctt.n
                stx["f3c"][c] = cve.n
                stx["s2"][c] = csc.n
                return csc.n
            n_t6b = ctt.bump()
            p.te(lambda e, n_v6=n_v6, r=r, bO=bO, slot=slot: (
                e.wait_ge(cve.h, n_v6),
                e.matmul(bO, r["vc"], r["w2t"], start=True, stop=False),
                e.matmul(bO, SstB2[0:64, 128 * slot:128 * (slot + 1)],
                         r["p2T"], start=False, stop=True).then_inc(ctt.h, 1)))
            stx["t6b"][c] = ctt.n
            if lvl < 4:
                stx["f3c"][c] = cve.n
                stx["s2"][c] = csc.n
                return csc.n
            n_f3c = cve.bump()
            p.ve(lambda e, n_t6b=n_t6b, bO=bO, par=P: (
                e.wait_ge(ctt.h, n_t6b),
                e.tensor_copy(f3c_sb[:, 128 * par:128 * (par + 1)], bO)
                .then_inc(cve.h, 1)))
            stx["f3c"][c] = cve.n
            if lvl < 5:
                stx["s2"][c] = csc.n
                return csc.n
            n_s2 = csc.bump()
            p.sc(lambda e, n_f3c=n_f3c, csl=csl, par=P: (
                e.wait_ge(cve.h, n_f3c),
                e.activation(sq_sb[:, 128 * par:128 * (par + 1)],
                             f3c_sb[:, 128 * par:128 * (par + 1)], AF.Square),
                e.activation(sg2_sb[:, 128 * par:128 * (par + 1)],
                             gT_h.ap()[:, csl], AF.Silu).then_inc(csc.h, 1)))
            stx["s2"][c] = csc.n
            return csc.n

        def emit_epi(ec, s2cnt, hi=hi):
            par = ec % 2
            n_t7 = ctt.bump()
            p.te(lambda e, s2cnt=s2cnt, par=par, w8=ep_state[0]: (
                e.wait_ge(csc.h, s2cnt),
                (e.wait_ge(cve.h, w8) if w8 > 0 else None),
                e.matmul(f[5][0:128, 0:1], sq_sb[:, 128 * par:128 * (par + 1)],
                         ones128[:], start=True, stop=True).then_inc(ctt.h, 1)))
            n_v8a = cve.bump()
            p.ve(lambda e, n_t7=n_t7: (
                e.wait_ge(ctt.h, n_t7),
                e.tensor_scalar(e1_sb[:], f[5][0:128, 0:1], 1.0 / 128.0, EPS,
                                OP.mult, OP.add).then_inc(cve.h, 1)))
            ep_state[0] = cve.n
            n_v8 = cve.bump()
            p.ve(lambda e, n_v8a=n_v8a: (
                e.wait_ge(cve.h, n_v8a),
                e.reciprocal(e3_sb[:], e1_sb[:]).then_inc(cve.h, 1)))
            n_s3 = csc.bump()
            p.sc(lambda e, n_v8=n_v8: (
                e.wait_ge(cve.h, n_v8),
                e.sqrt(rstd_sb[:], e3_sb[:]).then_inc(csc.h, 1)))
            n_t8 = ctt.bump()
            p.te(lambda e, n_s3=n_s3, w9=ep_state[1]: (
                e.wait_ge(csc.h, n_s3),
                (e.wait_ge(cve.h, w9) if w9 > 0 else None),
                e.transpose(f[5].ap()[0:1, 128:256], rstd_sb[:], idn32_sb[:])
                .then_inc(ctt.h, 1)))
            n_v9 = cve.bump()
            p.ve(lambda e, n_t8=n_t8: (
                e.wait_ge(ctt.h, n_t8),
                e.tensor_copy(erow_sb[:], f[5].ap()[0:1, 128:256]).then_inc(cve.h, 1)))
            ep_state[1] = cve.n
            n_t9 = ctt.bump()
            p.te(lambda e, n_v9=n_v9: (
                e.wait_ge(cve.h, n_v9),
                e.matmul(f[5].ap()[0:128, 256:384], one1[:], erow_sb[:],
                         start=True, stop=True).then_inc(ctt.h, 1)))
            n_v10 = cve.bump()
            cpos = ec % 2
            n_stw = cst.n
            p.ve(lambda e, n_t9=n_t9, cpos=cpos, n_stw=n_stw, par=par: (
                e.wait_ge(ctt.h, n_t9),
                ((e.wait_ge(cst.h, n_stw) if n_stw > 0 else None) if cpos == 0 else None),
                vch.step(e, e.tensor_copy(rbc_sb[:], f[5].ap()[0:128, 256:384])),
                vch.step(e, e.tensor_mul(rbc_sb[:],
                                         f3c_sb[:, 128 * par:128 * (par + 1)],
                                         rbc_sb[:])),
                vch.step(e, e.tensor_scalar(rbc_sb[:], rbc_sb[:], gnw_sb[:, 0:1],
                                            None, OP.mult)),
                e.tensor_mul(ct_sb[:, 128 * cpos:128 * (cpos + 1)],
                             rbc_sb[:], sg2_sb[:, 128 * par:128 * (par + 1)])
                .then_inc(cve.h, 1)))
            if cpos == 1:
                ncs = cst.bump(16)
                p.gp(lambda e, hi=hi, ec=ec, nv=cve.n: (
                    e.wait_ge(cve.h, nv),
                    e.dma_start(out=ct_out[128 * hi:128 * (hi + 1),
                                           128 * (ec - 1):128 * (ec + 1)],
                                in_=ct_sb[:, 0:256])
                    .then_inc(cst.h, 16)))

        s2cnt = {}
        for it in range(nchunks + 3):
            if flash_stages >= 3 and 2 <= it and it - 2 < nchunks:
                s2cnt[it - 2] = stage_X2(it - 2, lvl=flash_stages - 2)
            if flash_stages >= 2 and 1 <= it and it - 1 < nchunks:
                stage_X1(it - 1)
            if it < nchunks:
                stage_P(it)
            if flash_stages >= 4 and 3 <= it and it - 3 < nchunks:
                emit_epi(it - 3, s2cnt[it - 3])
        nhb = hbar.bump()
        p.ve(lambda e, nv=cve.n, nhb=nhb: (
            e.wait_ge(cve.h, nv), e.engine_nop().then_inc(hbar.h, 1)))
    p.sy(lambda e, n=cst.n: e.wait_ge(cst.h, n))


def _idn128():
    return np.eye(128, dtype=BF)


def _heads(c):
    j = c % 4
    return [j, j + 4, j + 8, j + 12]


def _shard_x(x):
    return [np.ascontiguousarray(x[c // 4, 2048 * (c % 4):2048 * (c % 4 + 1), :])
            for c in range(8)]


def _scales_128(s):
    return np.ascontiguousarray(np.asarray(s, np.float32).reshape(-1, 128).T)


def _bc128(s):
    s = np.asarray(s, np.float32)
    return np.ascontiguousarray(np.broadcast_to(s[None, :], (128, len(s))))


def _dmask():
    m = np.zeros((128, 4 * 512), dtype=BF)
    for r in range(4):
        for pp in range(128):
            m[pp, 512 * r + pp + 128 * r:512 * (r + 1)] = 1.0
    return m


def _selh():
    s = np.zeros((16, 16 * 128), dtype=BF)
    for h in range(16):
        s[h, 128 * h:128 * (h + 1)] = 1.0
    return s


def _l0_tables(c):
    j = c % 4
    inv = 1.0 / 10000.0 ** (np.arange(0, 2048, 2, dtype=np.float64) / 2048)
    s = np.arange(S, dtype=np.float64)
    cosp = np.empty((128, 2 * S), dtype=BF)
    sinp = np.empty((128, 2 * S), dtype=BF)
    for pi, hA in enumerate((j, j + 4)):
        ang = np.outer(inv[128 * hA:128 * (hA + 1)], s)
        cosp[:, pi * S:(pi + 1) * S] = np.cos(ang).astype(np.float32)
        sinp[:, pi * S:(pi + 1) * S] = np.sin(ang).astype(np.float32)
    return cosp, sinp


def _l0_wrows(W, c):
    j = c % 4
    order = [j, j + 8, j + 4, j + 12]
    return np.ascontiguousarray(np.concatenate(
        [np.asarray(W)[128 * h:128 * (h + 1)] for h in order], axis=0))


def _hrows(W, c):
    return np.ascontiguousarray(np.concatenate(
        [np.asarray(W)[128 * h:128 * (h + 1)] for h in _heads(c)], axis=0))


def _l2_tables():
    inv = 1.0 / 10000.0 ** (np.arange(0, 128, 2, dtype=np.float64) / 128)
    s = np.arange(S, dtype=np.float64)
    ang = np.outer(inv, s)
    cos = np.concatenate([np.cos(ang), np.cos(ang)], axis=0).astype(np.float32)
    sin = np.concatenate([-np.sin(ang), np.sin(ang)], axis=0).astype(np.float32)
    return np.ascontiguousarray(cos.astype(BF)), np.ascontiguousarray(sin.astype(BF))


def _pmat():
    m = np.zeros((128, 128), dtype=BF)
    for col in range(128):
        m[(col + 64) % 128, col] = 1.0
    return m


def _wo_perm(Wo):
    idx = np.empty(2048, np.int64)
    for jc in range(4):
        for hi in range(4):
            h = jc + 4 * hi
            idx[512 * jc + 128 * hi:512 * jc + 128 * (hi + 1)] = \
                np.arange(128 * h, 128 * (h + 1))
    return np.ascontiguousarray(np.asarray(Wo)[:, idx])


_PROGS = {}


def _prog(name, builder):
    if name not in _PROGS:
        _PROGS[name] = builder()
    return _PROGS[name]


def _run(name, builder, in_maps):
    nc = _prog(name, builder)
    return run_bass_kernel_spmd(nc, in_maps, core_ids=list(range(NCORES))).results


def _assemble_lnt(outs):
    return [np.concatenate([np.asarray(outs[4 * b + j]["lnT"]) for j in range(4)],
                           axis=1) for b in range(B)]


def _assemble_ct(outs):
    return [np.concatenate([np.asarray(outs[4 * b + j]["ct"]) for j in range(4)],
                           axis=0) for b in range(B)]


def _attn_sandwich(xs, kind, d):
    idn = _idn128()
    r1 = _run("preln", build_preln_prog,
              [{"x": xs[c], "idn": idn} for c in range(8)])
    lnt_b = _assemble_lnt(r1)
    if kind == "l0":
        in_maps = []
        for c in range(8):
            cosp, sinp = _l0_tables(c)
            gate = np.asarray(d["L0_attn_gate"], np.float32).reshape(16)
            in_maps.append({
                "lnt": np.ascontiguousarray(lnt_b[c // 4]),
                "wq": _l0_wrows(d["L0_attn_Wq"], c),
                "wk": _l0_wrows(d["L0_attn_Wk"], c),
                "wv": _l0_wrows(d["L0_attn_Wv"], c),
                "cosp": cosp, "sinp": sinp,
                "gate": np.ascontiguousarray(gate[_heads(c)][None, :]),
                "idn": idn})
        r2 = _run("attn0", build_attn0_prog, in_maps)
        wo = _wo_perm(d["L0_attn_Wo"])
    elif kind == "l2":
        SCL = 1.0 / np.sqrt(np.float32(128))
        cos_t, sin_t = _l2_tables()
        ws2 = np.asarray(d["L2_attn_Ws2"])
        bs2 = np.asarray(d["L2_attn_bs2"])
        in_maps = []
        for c in range(8):
            hl = _heads(c)
            morder = [hl[0], hl[2], hl[1], hl[3]]
            ws2_st = np.concatenate([ws2[64 * h:64 * (h + 1)] for h in morder], 0)
            bs2p = np.stack(
                [np.concatenate([bs2[64 * morder[0]:64 * morder[0] + 64],
                                 bs2[64 * morder[1]:64 * morder[1] + 64]]),
                 np.concatenate([bs2[64 * morder[2]:64 * morder[2] + 64],
                                 bs2[64 * morder[3]:64 * morder[3] + 64]])],
                axis=1).astype(np.float32)
            in_maps.append({
                "lnt": np.ascontiguousarray(lnt_b[c // 4]),
                "wq": _hrows(d["L2_attn_Wq"], c), "wk": _hrows(d["L2_attn_Wk"], c),
                "wv": _hrows(d["L2_attn_Wv"], c), "wg": _hrows(d["L2_attn_Wg"], c),
                "ws1": np.asarray(d["L2_attn_Ws1"]),
                "ws2": np.ascontiguousarray(ws2_st),
                "bs2p": np.ascontiguousarray(bs2p),
                "gnw": np.asarray(d["L2_attn_g_norm_weight"]).reshape(128, 1).astype(np.float32),
                "cosr": cos_t, "sinr": sin_t, "pmat": _pmat(),
                "cmaskA": np.ascontiguousarray((np.triu(np.ones((128, 128), np.float32)) * SCL).astype(BF)),
                "cmaskW": np.ascontiguousarray(np.triu(np.ones((128, 128), np.float32)).astype(BF)),
                "idn": idn, "idn32": np.eye(128, dtype=np.float32)})
        r2 = _run("attn2", build_attn2_prog, in_maps)
        wo = _wo_perm(d["L2_attn_Wo"])
    else:
        dm = _dmask()
        in_maps = [{"lnt": np.ascontiguousarray(lnt_b[c // 4]),
                    "wq": _hrows(d["L3_attn_Wq"], c),
                    "wk": _hrows(d["L3_attn_Wk"], c),
                    "wv": _hrows(d["L3_attn_Wv"], c),
                    "idn": idn, "dmask": dm} for c in range(8)]
        r2 = _run("attn3", build_attn3_prog, in_maps)
        wo = _wo_perm(d["L3_attn_Wo"])
    ct_b = _assemble_ct(r2)
    r3 = _run("outproj", build_outproj_prog,
              [{"ct": np.ascontiguousarray(ct_b[c // 4][:, 2048 * (c % 4):2048 * (c % 4 + 1)]),
                "wo": wo, "x": xs[c], "idn": idn} for c in range(8)])
    return [np.asarray(r3[c]["xn"]) for c in range(8)]


def _ffn_sandwich(xs, kind, w1, w3, w2, scales=None):
    idn = _idn128()
    ins = {"w1": np.asarray(w1), "w3": np.asarray(w3), "w2": np.asarray(w2),
           "idn": idn}
    if scales is not None:
        ins["gs"] = _scales_128(scales[0])
        ins["us"] = _scales_128(scales[1])
        ins["ds"] = _bc128(scales[2])
    r = _run("ffn_" + kind, lambda: build_ffn_prog(kind),
             [dict(ins, x=xs[c]) for c in range(8)])
    return [np.asarray(r[c]["xn"]) for c in range(8)]


def _l1_sandwich(xs, d):
    idn = _idn128()
    r1 = _run("l1qkv", build_l1qkv_prog,
              [{"x": xs[c], "wqkv": np.asarray(d["L1_attn_weight_qkv"]),
                "idn": idn} for c in range(8)])
    qkvT = [np.asarray(r1[c]["qkvT"]) for c in range(8)]
    sel = _selh()
    wo = np.asarray(d["L1_attn_weight_out"])
    in_maps = []
    for c in range(8):
        part = c % 4
        in_maps.append({
            "qT": qkvT[c][0:2048], "koT": qkvT[c][2048:4096],
            "voT": qkvT[c][4096:6144], "kfT": qkvT[part][2048:4096],
            "vfT": qkvT[part][4096:6144],
            "msk": np.full((128, 1), 0.0 if c < 4 else 1.0, np.float32),
            "selh": sel, "wo": wo, "x": xs[c], "idn": idn})
    r2 = _run("l1comb", build_l1comb_prog, in_maps)
    return [np.asarray(r2[c]["xn"]) for c in range(8)]


def emit_partial_outproj(p, ct_in, wos_in, idn_sb, yp_out, idn_wait):
    wt_sb = p.sb("pow", [128, 4 * D], BF16)
    potp = [p.ps(p._uname("potp"), [128, 1024], BF16) for i in range(2)]
    wdone = emit_wt_prep(p, wos_in, wt_sb, D, 512, idn_sb, p._uname("pwp"),
                         idn_wait=idn_wait, tp_pool=potp)
    nw = wdone.n
    ct_sb = p.sb("poc", [128, 2 * 512], BF16)
    y_sb = p.sb("poy", [128, 2 * D], BF16)
    y_ps = [p.ps("poyp", [128, 512], F32) for _ in range(4)]
    y_ps2 = ([p.ps(p._uname("poyq"), [128, 512], F32) for _ in range(2)]
             + [potp[0].ap()[:, 0:1024].bitcast(F32),
                potp[1].ap()[:, 0:1024].bitcast(F32)])
    clds = [p.sem(), p.sem()]
    mm = p.sem()
    yev = p.sem()
    ysts = [p.swdge_sem(), p.swdge_sem()]
    for tt in range(64):
        buf = tt % 2
        cld = clds[buf]
        yst = ysts[buf]
        ncl = cld.bump(16)
        mmwar = mm.n - 1
        p.sy(lambda e, tt=tt, buf=buf, ncl=ncl, mmwar=mmwar, cld=cld: (
            (e.wait_ge(mm.h, mmwar) if mmwar > 0 else None),
            e.dma_start(out=ct_sb[:, buf * 512:(buf + 1) * 512]
                        .rearrange("p (c t) -> p c t", c=4),
                        in_=ct_in[:, 128 * tt:128 * (tt + 1)]
                        .rearrange("(c p) t -> p c t", p=128)).then_inc(cld.h, 16)))
        nmm = mm.bump()
        psn = y_ps if tt % 2 == 0 else y_ps2
        p.te(lambda e, buf=buf, ncl=ncl, nmm=nmm, nw=nw, tt=tt, yw=yev.n - 1,
                    cld=cld, psn=psn: (
            e.wait_ge(cld.h, ncl),
            (e.wait_ge(wdone.h, nw) if tt == 0 else None),
            (e.wait_ge(yev.h, yw) if yw > 0 else None),
            [e.matmul(psn[n][0:128, 0:512],
                      ct_sb[:, buf * 512 + 128 * c:buf * 512 + 128 * (c + 1)],
                      wt_sb[:, c * D + 512 * n:c * D + 512 * (n + 1)],
                      start=(c == 0), stop=(c == 3))
             for c in range(4) for n in range(4)][-1].then_inc(mm.h, 1)))
        nye = yev.bump()
        p.ve(lambda e, buf=buf, nmm=nmm, stw=yst.n, yst=yst, psn=psn: (
            e.wait_ge(mm.h, nmm),
            (e.wait_ge(yst.h, stw) if stw > 0 else None),
            [e.tensor_copy(y_sb[:, buf * D + 512 * n:buf * D + 512 * (n + 1)],
                           psn[n][0:128, 0:512]) for n in range(4)][-1].then_inc(yev.h, 1)))
        nys = yst.bump(16)
        p.gp(lambda e, tt=tt, buf=buf, nye=nye, yst=yst: (
            e.wait_ge(yev.h, nye),
            e.dma_start(out=yp_out[128 * tt:128 * (tt + 1), :],
                        in_=y_sb[:, buf * D:(buf + 1) * D]).then_inc(yst.h, 16)))
    fin = p.sem()
    nf = fin.bump()
    p.gp(lambda e, n0=ysts[0].n, n1=ysts[1].n, nf=nf: (
        e.wait_ge(ysts[0].h, n0), e.wait_ge(ysts[1].h, n1),
        e.nop().then_inc(fin.h, 1)))
    return fin


def emit_postln_rs(p, ys_in, x_in, x_out, y_wait=None):
    pl = PostLN(p, x_in, x_out, p._uname("plr"))
    y_sb = p.sb("plry", [128, 2 * D], BF16)
    yld = p.sem()
    for j in range(NTT):
        buf = j % 2
        nld = pl.emit_ld(j, gate=(y_wait[:1] if y_wait else None))
        nyl = yld.bump(16)
        ldwar = pl.lnst.ss.n - 2
        p.sy(lambda e, j=j, buf=buf, nyl=nyl, ldwar=ldwar, y_wait=y_wait: (
            ([e.wait_ge(s.h, n) for (s, n) in y_wait]
             if (y_wait and j == 0) else None),
            (e.wait_ge(pl.lnst.ss.h, ldwar) if ldwar > 0 else None),
            e.dma_start(out=y_sb[:, buf * D:(buf + 1) * D],
                        in_=ys_in[128 * j:128 * (j + 1), :]).then_inc(yld.h, 16)))
        srcs = [y_sb[:, buf * D + 512 * n:buf * D + 512 * (n + 1)] for n in range(4)]
        acts = [(y_sb[:, buf * D:(buf + 1) * D], 0, D)]
        pl.emit(j, srcs, acts, (yld, nyl), nld=nld)
    pl.finish()
    return pl.st


def emit_cc(p, kind, op, groups, in_ap, out_ap, waits, block=True):
    cc = p._newsem(p._uname("cc"))
    p.gp(lambda e, waits=waits, block=block: (
        [e.wait_ge(s.h, n) for (s, n) in waits],
        e.collective_compute(kind, op, replica_groups=groups,
                             ins=[in_ap], outs=[out_ap]).then_inc(cc.h),
        (e.wait_ge(cc.h, 1) if block else None)))
    return cc


GROUPS4 = [[0, 1, 2, 3], [4, 5, 6, 7]]
PAIRS = [[0, 4], [1, 5], [2, 6], [3, 7]]


def build_fused_prog(stages=8, sub=6):
    p = Prog("fused", pool_sems=True)
    NHC2 = 44
    xg = p.din("x", [TPC, D], F32)
    idn = p.din("idn", [128, 128], BF16)
    idn32 = p.din("idn32", [128, 128], F32)
    wq0 = p.din("wq0", [512, D], F32)
    wk0 = p.din("wk0", [512, D], F32)
    wv0 = p.din("wv0", [512, D], F32)
    cosp = p.din("cosp", [128, 2 * S], BF16)
    sinp = p.din("sinp", [128, 2 * S], BF16)
    gate0 = p.din("gate0", [1, 4], F32)
    wos0 = p.din("wos0", [D, 512], F32)
    w1_0 = p.din("w1_0", [5461, D], F32)
    w3_0 = p.din("w3_0", [5461, D], F32)
    w2_0 = p.din("w2_0", [D, 5461], F32)
    wqkv1 = p.din("wqkv1", [3 * D, D], F32)
    wo1 = p.din("wo1", [D, D], F32)
    msk1 = p.din("msk1", [128, 1], F32)
    selh1 = p.din("selh1", [16, 16 * 128], BF16)
    w1_1 = p.din("w1_1", [5461, D], F32)
    w3_1 = p.din("w3_1", [5461, D], F32)
    w2_1 = p.din("w2_1", [D, 5461], F32)
    wq2 = p.din("wq2", [512, D], F32)
    wk2 = p.din("wk2", [512, D], F32)
    wv2 = p.din("wv2", [512, D], F32)
    wg2 = p.din("wg2", [512, D], F32)
    ws1 = p.din("ws1", [16, D], F32)
    ws2 = p.din("ws2", [256, 16], F32)
    bs2p = p.din("bs2p", [128, 2], F32)
    gnw = p.din("gnw", [128, 1], F32)
    cosr = p.din("cosr", [128, S], BF16)
    sinr = p.din("sinr", [128, S], BF16)
    pmat = p.din("pmat", [128, 128], BF16)
    cmaskA = p.din("cmaskA", [128, 128], BF16)
    cmaskW = p.din("cmaskW", [128, 128], BF16)
    wos2 = p.din("wos2", [D, 512], F32)
    gw2 = p.din("gw2", [5632, D], F32)
    uw2 = p.din("uw2", [5632, D], F32)
    dw2 = p.din("dw2", [D, 5632], F32)
    gs2 = p.din("gs2", [128, NHC2], F32)
    us2 = p.din("us2", [128, NHC2], F32)
    ds2 = p.din("ds2", [128, D], F32)
    wq3 = p.din("wq3", [512, D], F32)
    wk3 = p.din("wk3", [512, D], F32)
    wv3 = p.din("wv3", [512, D], F32)
    dmask = p.din("dmask", [128, 4 * 512], BF16)
    wos3 = p.din("wos3", [D, 512], F32)
    g3 = p.din("g3", [5632, D], I32)
    u3 = p.din("u3", [5632, D], I32)
    d3 = p.din("d3", [D, 5632], I32)
    gs3 = p.din("gs3", [128, NHC2], F32)
    us3 = p.din("us3", [128, NHC2], F32)
    ds3 = p.din("ds3", [128, D], F32)
    x_out = p.dout("xn", [TPC, D], F32)

    x_int = p.dtmp("x_int", [TPC, D], F32)
    lnt_loc = p.dtmp("lnt_loc", [D, TPC], BF16)
    lnt_ag = p.dtmp("lnt_ag", [4 * D, TPC], BF16)
    ct_loc = p.dtmp("ct_loc", [512, S], BF16)
    yp = p.dtmp("yp", [S, D], BF16)
    ysum = p.dtmp("ysum", [TPC, D], BF16)
    qkv_loc = p.dtmp("qkv_loc", [3 * D, TPC], BF16)
    kv_ag = p.dtmp("kv_ag", [2 * 2 * D, TPC], BF16)

    def lnt_tile_ag(ts):
        r, lts = ts // 4, ts % 4
        return lnt_ag[D * r:D * (r + 1), 512 * lts:512 * (lts + 1)]

    idn_sb = p.sb("idn_sb", [128, 128], BF16)
    x0 = p.sem()
    n0 = x0.bump(16)
    p.sy(lambda e: e.dma_start(out=idn_sb[:], in_=idn).then_inc(x0.h, 16))
    for st in ("vector", "tensor", "scalar", "gpsimd"):
        p.streams[st].append(lambda e, n0=n0: e.wait_ge(x0.h, n0))
    p.phase_barrier()

    def attn_sandwich(emit_ln_attn, wos, x_src=x_int, sub=6):
        def s1():
            p.push_scope()
            lnT_sb = p.sb("lnT_f", [128, NDC * TPC], BF16)
            lndone = emit_preln(p, x_src, lnT_sb, idn_sb, out_dram=lnt_loc,
                                tag=p._uname("pln"))
            s1.nld = (lndone, lndone.n)
            p.pop_scope()

        def s2():
            s2.cc = emit_cc(p, "AllGather", OP.bypass, GROUPS4, lnt_loc,
                            lnt_ag, [s1.nld], block=False)

        def s3():
            p.push_scope()
            emit_ln_attn(w_wait=s1.nld, lnt_wait=(s2.cc, 1))
            p.pop_scope()
            p.phase_barrier()

        def s4():
            p.push_scope()
            fin = emit_partial_outproj(p, ct_loc, wos, idn_sb, yp, None)
            s4.nfin = (fin, fin.n)
            p.pop_scope()

        def s5():
            s5.cc = emit_cc(p, "ReduceScatter", OP.add, GROUPS4, yp, ysum,
                            [s4.nfin], block=False)

        def s6():
            p.push_scope()
            emit_postln_rs(p, ysum, x_src, x_int,
                           y_wait=[s4.nfin, (s5.cc, 1)])
            p.pop_scope()
            p.phase_barrier()

        for f in (s1, s2, s3, s4, s5, s6)[:sub]:
            f()

    def ffn_block(kind, a, b, c, gs=None, us=None, ds=None, out=None):
        p.push_scope()
        emit_ffn(p, kind, x_int, out if out is not None else x_int,
                 a, b, c, idn, gs, us, ds)
        p.pop_scope()
        p.phase_barrier()

    def l1_attn():
        p.push_scope()
        emit_l1qkv(p, x_int, wqkv1, idn, qkv_loc)
        p.pop_scope()
        p.phase_barrier()
        l1fin = p.sem()
        nl1 = l1fin.bump()
        p.sy(lambda e, nl1=nl1: e.nop().then_inc(l1fin.h, 1))
        emit_cc(p, "AllGather", OP.bypass, PAIRS, qkv_loc[D:3 * D, :], kv_ag,
                [(l1fin, nl1)])
        p.phase_barrier()
        p.push_scope()
        emit_l1comb(p, qkv_loc[0:D, :], qkv_loc[D:2 * D, :],
                    qkv_loc[2 * D:3 * D, :], kv_ag[0:D, :], kv_ag[D:2 * D, :],
                    msk1, selh1, wo1, x_int, idn, x_int)
        p.pop_scope()
        p.phase_barrier()

    phases = [
        lambda: attn_sandwich(
            lambda **kw: emit_attn0(p, lnt_tile_ag, wq0, wk0, wv0, cosp,
                                    sinp, gate0, idn, ct_loc, **kw),
            wos0, x_src=xg, sub=sub),
        lambda: ffn_block("tern", w1_0, w3_0, w2_0),
        l1_attn,
        lambda: ffn_block("tern", w1_1, w3_1, w2_1),
        lambda: attn_sandwich(
            lambda **kw: emit_attn2(p, lnt_tile_ag, wq2, wk2, wv2, wg2, ws1,
                                    ws2, bs2p, gnw, cosr, sinr, pmat, cmaskA,
                                    cmaskW, idn, idn32, ct_loc, **kw), wos2),
        lambda: ffn_block("l2", gw2, uw2, dw2, gs2, us2, ds2),
        lambda: attn_sandwich(
            lambda **kw: emit_attn3(p, lnt_tile_ag, wq3, wk3, wv3, idn,
                                    dmask, ct_loc, **kw), wos3),
        lambda: ffn_block("l3", g3, u3, d3, gs3, us3, ds3,
                          out=x_out if stages == 8 else None),
    ]
    for k in range(stages):
        phases[k]()
    if stages < 8:
        xo = p.sem()
        nxo = xo.bump(16)
        p.sy(lambda e, nxo=nxo: (e.dma_start(out=x_out, in_=x_int)
                                 .then_inc(xo.h, 16),
                                 e.wait_ge(xo.h, nxo)))
    return p.finalize()


def _fused_in_maps(d):
    idn = _idn128()
    idn32 = np.eye(128, dtype=np.float32)
    cos_t, sin_t = _l2_tables()
    SCL = 1.0 / np.sqrt(np.float32(128))
    ws2 = np.asarray(d["L2_attn_Ws2"])
    bs2 = np.asarray(d["L2_attn_bs2"])
    x = np.asarray(d["L0_attn_hidden_states"], np.float32)
    xs = _shard_x(x)
    maps = []
    for c in range(8):
        hl = _heads(c)
        cosp, sinp = _l0_tables(c)
        gate = np.asarray(d["L0_attn_gate"], np.float32).reshape(16)
        morder = [hl[0], hl[2], hl[1], hl[3]]
        ws2_st = np.concatenate([ws2[64 * h:64 * (h + 1)] for h in morder], 0)
        bs2p = np.stack(
            [np.concatenate([bs2[64 * morder[0]:64 * morder[0] + 64],
                             bs2[64 * morder[1]:64 * morder[1] + 64]]),
             np.concatenate([bs2[64 * morder[2]:64 * morder[2] + 64],
                             bs2[64 * morder[3]:64 * morder[3] + 64]])],
            axis=1).astype(np.float32)

        def wos(W):
            return np.ascontiguousarray(np.concatenate(
                [np.asarray(W)[:, 128 * h:128 * (h + 1)] for h in hl], axis=1))

        maps.append({
            "x": xs[c], "idn": idn, "idn32": idn32,
            "wq0": _l0_wrows(d["L0_attn_Wq"], c),
            "wk0": _l0_wrows(d["L0_attn_Wk"], c),
            "wv0": _l0_wrows(d["L0_attn_Wv"], c),
            "cosp": cosp, "sinp": sinp,
            "gate0": np.ascontiguousarray(gate[hl][None, :]),
            "wos0": wos(d["L0_attn_Wo"]),
            "w1_0": np.asarray(d["L0_ffn_w1"]), "w3_0": np.asarray(d["L0_ffn_w3"]),
            "w2_0": np.asarray(d["L0_ffn_w2"]),
            "wqkv1": np.asarray(d["L1_attn_weight_qkv"]),
            "wo1": np.asarray(d["L1_attn_weight_out"]),
            "msk1": np.full((128, 1), 0.0 if c < 4 else 1.0, np.float32),
            "selh1": _selh(),
            "w1_1": np.asarray(d["L1_ffn_w1"]), "w3_1": np.asarray(d["L1_ffn_w3"]),
            "w2_1": np.asarray(d["L1_ffn_w2"]),
            "wq2": _hrows(d["L2_attn_Wq"], c), "wk2": _hrows(d["L2_attn_Wk"], c),
            "wv2": _hrows(d["L2_attn_Wv"], c), "wg2": _hrows(d["L2_attn_Wg"], c),
            "ws1": np.asarray(d["L2_attn_Ws1"]),
            "ws2": np.ascontiguousarray(ws2_st),
            "bs2p": np.ascontiguousarray(bs2p),
            "gnw": np.asarray(d["L2_attn_g_norm_weight"]).reshape(128, 1).astype(np.float32),
            "cosr": cos_t, "sinr": sin_t, "pmat": _pmat(),
            "cmaskA": np.ascontiguousarray((np.triu(np.ones((128, 128), np.float32)) * SCL).astype(BF)),
            "cmaskW": np.ascontiguousarray(np.triu(np.ones((128, 128), np.float32)).astype(BF)),
            "wos2": wos(d["L2_attn_Wo"]),
            "gw2": np.asarray(d["L2_ffn_gate_w"]), "uw2": np.asarray(d["L2_ffn_up_w"]),
            "dw2": np.asarray(d["L2_ffn_down_w"]),
            "gs2": _scales_128(d["L2_ffn_gate_s"]), "us2": _scales_128(d["L2_ffn_up_s"]),
            "ds2": _bc128(d["L2_ffn_down_s"]),
            "wq3": _hrows(d["L3_attn_Wq"], c), "wk3": _hrows(d["L3_attn_Wk"], c),
            "wv3": _hrows(d["L3_attn_Wv"], c),
            "dmask": _dmask(), "wos3": wos(d["L3_attn_Wo"]),
            "g3": np.asarray(d["L3_ffn_wq_gate"]), "u3": np.asarray(d["L3_ffn_wq_up"]),
            "d3": np.asarray(d["L3_ffn_wq_down"]),
            "gs3": _scales_128(d["L3_ffn_s_gate"]), "us3": _scales_128(d["L3_ffn_s_up"]),
            "ds3": _bc128(d["L3_ffn_s_down"])})
    return maps


def kernel(**inputs):
    nc = _prog("fused", build_fused_prog)
    in_maps = _fused_in_maps(inputs)
    res = run_bass_kernel_spmd(nc, in_maps, core_ids=list(range(NCORES)))
    out = np.empty((B, S, D), np.float32)
    for c in range(8):
        out[c // 4, 2048 * (c % 4):2048 * (c % 4 + 1), :] = np.asarray(res.results[c]["xn"])
    return out


def kernel_unfused(**inputs):
    d = inputs
    x = np.asarray(d["L0_attn_hidden_states"], np.float32)
    xs = _shard_x(x)
    xs = _attn_sandwich(xs, "l0", d)
    xs = _ffn_sandwich(xs, "tern", d["L0_ffn_w1"], d["L0_ffn_w3"], d["L0_ffn_w2"])
    xs = _l1_sandwich(xs, d)
    xs = _ffn_sandwich(xs, "tern", d["L1_ffn_w1"], d["L1_ffn_w3"], d["L1_ffn_w2"])
    xs = _attn_sandwich(xs, "l2", d)
    xs = _ffn_sandwich(xs, "l2", d["L2_ffn_gate_w"], d["L2_ffn_up_w"],
                       d["L2_ffn_down_w"],
                       (d["L2_ffn_gate_s"], d["L2_ffn_up_s"], d["L2_ffn_down_s"]))
    xs = _attn_sandwich(xs, "l3", d)
    xs = _ffn_sandwich(xs, "l3", d["L3_ffn_wq_gate"], d["L3_ffn_wq_up"],
                       d["L3_ffn_wq_down"],
                       (d["L3_ffn_s_gate"], d["L3_ffn_s_up"], d["L3_ffn_s_down"]))
    out = np.empty((B, S, D), np.float32)
    for c in range(8):
        out[c // 4, 2048 * (c % 4):2048 * (c % 4 + 1), :] = xs[c]
    return out


# version 15 (final)
# speedup vs baseline: 1.1400x; 1.0160x over previous
import numpy as np
import ml_dtypes
from contextlib import ExitStack

import concourse.bass as bass
import concourse.mybir as mybir
from concourse.bass_utils import run_bass_kernel_spmd

F32 = mybir.dt.float32
BF16 = mybir.dt.bfloat16
I32 = mybir.dt.int32
AF = mybir.ActivationFunctionType
OP = mybir.AluOpType
AX = mybir.AxisListType
BF = ml_dtypes.bfloat16

NCORES = 8
D = 2048
S = 8192
B = 2
H = 16
DH = 128
TPC = 2048
NTT = TPC // 128
NDC = D // 128

EPS = 1e-5
CEXP = 16.0


class Sem:
    __slots__ = ("h", "n")

    def __init__(self, h):
        self.h = h
        self.n = 0

    def bump(self, k=1):
        self.n += k
        return self.n


class Chain:

    def __init__(self, p, name=None):
        p._semi += 1
        self.h = p.ctx.enter_context(p.nc.semaphore(name or f"ch{p._semi}"))
        self.n = 0

    def step(self, e, instr):
        self.n += 1
        instr.then_inc(self.h, 1)
        e.wait_ge(self.h, self.n)
        return instr


class Prog:

    def __init__(self, name="prog", pool_sems=False):
        self.nc = bass.Bass("TRN2", target_bir_lowering=False, debug=False)
        self.ctx = ExitStack()
        self.streams = {k: [] for k in ("sync", "vector", "tensor", "scalar", "gpsimd")}
        self._semi = 0
        self._namei = 0
        self._scopes = []
        self._pool_sems = pool_sems
        self._gen_pool = []
        self._gen_next = 0
        self._swdge_pool = []
        self._swdge_next = 0
        self._pb = None
        self._pb_n = 0

    def _newsem(self, name=None):
        self._semi += 1
        return Sem(self.ctx.enter_context(self.nc.semaphore(name or f"s{self._semi}")))

    GEN_CAP = 60
    SWDGE_CAP = 24

    def sem(self, name=None):
        if not self._pool_sems:
            return self._newsem(name)
        if len(self._gen_pool) < self.GEN_CAP:
            s = self._newsem(name)
            self._gen_pool.append(s)
            return s
        s = self._gen_pool[self._gen_next % self.GEN_CAP]
        self._gen_next += 1
        return s

    def swdge_sem(self, name=None):
        if not self._pool_sems:
            return self._newsem(name)
        if len(self._swdge_pool) < self.SWDGE_CAP:
            s = self._newsem(name)
            self._swdge_pool.append(s)
            return s
        s = self._swdge_pool[self._swdge_next % self.SWDGE_CAP]
        self._swdge_next += 1
        return s

    def _uname(self, name):
        self._namei += 1
        return f"{name}_{self._namei}"

    def push_scope(self):
        s = ExitStack()
        self._scopes.append(s)
        return s

    def pop_scope(self):
        self._scopes.pop().close()

    def _tgt(self):
        return self._scopes[-1] if self._scopes else self.ctx

    def sb(self, name, shape, dt=F32):
        return self._tgt().enter_context(self.nc.sbuf_tensor(self._uname(name), shape, dt))

    def scope(self):
        return ExitStack()

    def sbs(self, scope, name, shape, dt=F32):
        return scope.enter_context(self.nc.sbuf_tensor(self._uname(name), shape, dt))

    def ps(self, name, shape, dt=F32):
        return self._tgt().enter_context(self.nc.psum_tensor(self._uname(name), shape, dt))

    def din(self, name, shape, dt):
        return self.nc.dram_tensor(name, shape, dt, kind="ExternalInput").ap()

    def dout(self, name, shape, dt):
        return self.nc.dram_tensor(name, shape, dt, kind="ExternalOutput").ap()

    def dtmp(self, name, shape, dt):
        return self.nc.dram_tensor(name, shape, dt, kind="Internal").ap()

    def phase_barrier(self):
        if self._pb is None:
            self._pb = self._newsem("pb")
        self._pb_n += 5
        tgt = self._pb_n
        pb = self._pb
        for st in ("sync", "vector", "tensor", "scalar", "gpsimd"):
            self.streams[st].append(
                lambda e, tgt=tgt: (e.nop().then_inc(pb.h, 1),
                                    e.wait_ge(pb.h, tgt)))

    def sy(self, f):
        self.streams["sync"].append(f)

    def ve(self, f):
        self.streams["vector"].append(f)

    def te(self, f):
        self.streams["tensor"].append(f)

    def sc(self, f):
        self.streams["scalar"].append(f)

    def gp(self, f):
        self.streams["gpsimd"].append(f)

    def finalize(self):
        nc = self.nc
        with self.ctx, nc.Block() as block:
            block.sync(lambda e: [f(e) for f in self.streams["sync"]])
            block.vector(lambda e: [f(e) for f in self.streams["vector"]])
            block.tensor(lambda e: [f(e) for f in self.streams["tensor"]])
            block.scalar(lambda e: [f(e) for f in self.streams["scalar"]])
            block.gpsimd(lambda e: [f(e) for f in self.streams["gpsimd"]])
        return nc


def emit_preln(p, x_in, lnT_sb, idn_sb, out_dram=None, tag="ln", st_sem=None,
               tp_pool=None, xt_buf=None, lnt_buf=None, alloc=None):
    nc = p.nc
    alloc = alloc or p.sb
    xt = xt_buf if xt_buf is not None else alloc(tag + "_xt", [128, 2 * D], F32)
    lnt = lnt_buf if lnt_buf is not None else alloc(tag + "_lnt", [128, 2 * D], BF16)
    bn24 = alloc(tag + "_bn", [128, 24], F32)
    mv = alloc(tag + "_mv", [128, 2], F32)
    t1 = alloc(tag + "_t1", [128, 1], F32)
    t2 = alloc(tag + "_t2", [128, 1], F32)
    rstd = alloc(tag + "_rs", [128, 1], F32)
    nmr = alloc(tag + "_nm", [128, 1], F32)
    tp = tp_pool or [p.ps(tag + f"_tp{i}", [128, 1024], BF16) for i in range(2)]

    ld = p.sem()
    vs = p.sem()
    ss = p.sem()
    tt = p.sem()
    ev = p.sem()
    done = st_sem or p.sem()
    ss0, tt0, ev0 = ss.n, tt.n, ev.n

    for j in range(NTT):
        buf = j % 2
        nld = ld.bump(16)
        war = ss0 + 2 * (j - 2) + 2
        p.sy(lambda e, j=j, buf=buf, nld=nld, war=war: (
            (e.wait_ge(ss.h, war) if war > ss0 else None),
            e.dma_start(out=xt[:, buf * D:(buf + 1) * D],
                        in_=x_in[128 * j:128 * (j + 1), :]).then_inc(ld.h, 16)))

        nvs1 = vs.bump()
        p.ve(lambda e, buf=buf, nld=nld, nvs1=nvs1: (
            e.wait_ge(ld.h, nld),
            e.bn_stats(bn24[:, 0:6], xt[:, buf * D + 0:buf * D + 512]),
            e.bn_stats(bn24[:, 6:12], xt[:, buf * D + 512:buf * D + 1024]),
            e.bn_stats(bn24[:, 12:18], xt[:, buf * D + 1024:buf * D + 1536]),
            e.bn_stats(bn24[:, 18:24], xt[:, buf * D + 1536:buf * D + 2048]).then_inc(vs.h, 1)))
        nvs2 = vs.bump()
        p.ve(lambda e, nvs1=nvs1: (
            e.wait_ge(vs.h, nvs1),
            e.bn_aggr(mv[:], bn24[:]).then_inc(vs.h, 1)))
        nvs3 = vs.bump()
        p.ve(lambda e, nvs2=nvs2: (
            e.wait_ge(vs.h, nvs2),
            e.tensor_scalar(t1[:], mv[:, 1:2], EPS, None, OP.add).then_inc(vs.h, 1)))
        nvs4 = vs.bump()
        p.ve(lambda e, nvs3=nvs3: (
            e.wait_ge(vs.h, nvs3),
            e.reciprocal(t2[:], t1[:]).then_inc(vs.h, 1)))
        nss1 = ss.bump()
        p.sc(lambda e, nvs4=nvs4: (
            e.wait_ge(vs.h, nvs4),
            e.sqrt(rstd[:], t2[:]).then_inc(ss.h, 1)))
        nvs5 = vs.bump()
        p.ve(lambda e, nss1=nss1: (
            e.wait_ge(ss.h, nss1),
            e.scalar_tensor_tensor(nmr[:], mv[:, 0:1], -1.0, rstd[:],
                                   OP.mult, OP.mult).then_inc(vs.h, 1)))
        ntt_war = tt0 + 2 * (j - 1)
        nss2 = ss.bump()
        p.sc(lambda e, buf=buf, nvs5=nvs5, w=ntt_war: (
            e.wait_ge(vs.h, nvs5),
            (e.wait_ge(tt.h, w) if w > tt0 else None),
            e.activation(lnt[:, buf * D:(buf + 1) * D], xt[:, buf * D:(buf + 1) * D],
                         AF.Identity, bias=nmr[:, 0:1], scale=rstd[:, 0:1]).then_inc(ss.h, 1)))
        for half in range(2):
            ntt1 = tt.bump()
            p.te(lambda e, j=j, buf=buf, half=half, nss2=nss2, ntt1=ntt1: (
                e.wait_ge(ss.h, nss2),
                e.wait_ge(ev.h, ev0 + 2 * j),
                [e.transpose(tp[half][:, 128 * i:128 * (i + 1)],
                             lnt[:, buf * D + (8 * half + i) * 128: buf * D + (8 * half + i + 1) * 128],
                             idn_sb[:])
                 for i in range(8)][-1].then_inc(tt.h, 1)))
            nev = ev.bump()
            dst = lnT_sb.ap().rearrange("p (c t) -> p c t", c=NDC)[:, 8 * half:8 * half + 8, 128 * j:128 * (j + 1)]
            src = tp[half].ap().rearrange("p (c t) -> p c t", c=8)
            p.ve(lambda e, dst=dst, src=src, ntt1=ntt1: (
                e.wait_ge(tt.h, ntt1),
                e.tensor_copy(dst, src).then_inc(ev.h, 1)))

    if out_dram is not None:
        for i in range(NDC):
            nd = done.bump(16)
            p.sy(lambda e, i=i, ev_n=ev.n: (
                e.wait_ge(ev.h, ev_n),
                e.dma_start(out=out_dram[128 * i:128 * (i + 1), :],
                            in_=lnT_sb[:, i * TPC:(i + 1) * TPC]).then_inc(done.h, 16)))
    else:
        nd = done.bump()
        p.ve(lambda e, ev_n=ev.n: (e.wait_ge(ev.h, ev_n), e.engine_nop().then_inc(done.h, 1)))
    return done


def build_preln_prog():
    p = Prog("preln")
    x_in = p.din("x", [TPC, D], F32)
    lnT_out = p.dout("lnT", [D, TPC], BF16)
    idn = p.din("idn", [128, 128], BF16)
    idn_sb = p.sb("idn_sb", [128, 128], BF16)
    lnT_sb = p.sb("lnT_sb", [128, NDC * TPC], BF16)
    ld0 = p.sem()
    n0 = ld0.bump(16)
    p.sy(lambda e, n0=n0: e.dma_start(out=idn_sb[:], in_=idn).then_inc(ld0.h, 16))
    p.te(lambda e, n0=n0: e.wait_ge(ld0.h, n0))
    done = emit_preln(p, x_in, lnT_sb, idn_sb, out_dram=lnT_out)
    fin = done.n
    p.sy(lambda e, fin=fin: e.wait_ge(done.h, fin))
    return p.finalize()


class LNStats:
    def __init__(self, p, tag):
        self.p = p
        self.bn24 = p.sb(tag + "_bn", [128, 24], F32)
        self.mv = p.sb(tag + "_mv", [128, 2], F32)
        self.t1 = p.sb(tag + "_t1", [128, 1], F32)
        self.t2 = p.sb(tag + "_t2", [128, 1], F32)
        self.rstd = p.sb(tag + "_rs", [128, 1], F32)
        self.nmr = p.sb(tag + "_nm", [128, 1], F32)
        self.vs = p.sem()
        self.ss = p.sem()

    def emit(self, srcs, wait_v=None):
        p, bn24, mv = self.p, self.bn24, self.mv
        vs, ss = self.vs, self.ss
        n1 = vs.bump()
        p.ve(lambda e, srcs=srcs, wait_v=wait_v: (
            (e.wait_ge(wait_v[0].h, wait_v[1]) if wait_v else None),
            e.bn_stats(bn24[:, 0:6], srcs[0]),
            e.bn_stats(bn24[:, 6:12], srcs[1]),
            e.bn_stats(bn24[:, 12:18], srcs[2]),
            e.bn_stats(bn24[:, 18:24], srcs[3]).then_inc(vs.h, 1)))
        n2 = vs.bump()
        p.ve(lambda e, n1=n1: (
            e.wait_ge(vs.h, n1),
            e.bn_aggr(mv[:], bn24[:]).then_inc(vs.h, 1)))
        n3 = vs.bump()
        p.ve(lambda e, n2=n2: (
            e.wait_ge(vs.h, n2),
            e.tensor_scalar(self.t1[:], mv[:, 1:2], EPS, None, OP.add).then_inc(vs.h, 1)))
        n4 = vs.bump()
        p.ve(lambda e, n3=n3: (
            e.wait_ge(vs.h, n3),
            e.reciprocal(self.t2[:], self.t1[:]).then_inc(vs.h, 1)))
        ns1 = ss.bump()
        p.sc(lambda e, n4=n4: (
            e.wait_ge(vs.h, n4),
            e.sqrt(self.rstd[:], self.t2[:]).then_inc(ss.h, 1)))
        n5 = vs.bump()
        p.ve(lambda e, ns1=ns1: (
            e.wait_ge(ss.h, ns1),
            e.scalar_tensor_tensor(self.nmr[:], mv[:, 0:1], -1.0, self.rstd[:],
                                   OP.mult, OP.mult).then_inc(vs.h, 1)))
        return n5


def emit_wt_prep(p, w_in, wt_sb, O, I, idn_sb, tag, idn_wait=None, tp_pool=None):
    nc = p.nc
    noc = (O + 127) // 128
    nic = (I + 127) // 128
    wrow = p.sb(tag + "_wr", [128, 2 * I], BF16)
    tp = tp_pool or [p.ps(tag + f"_tp{i}", [128, 1024], BF16) for i in range(2)]
    glds = [p.swdge_sem(), p.swdge_sem()]
    tt = p.sem()
    ev = p.sem()
    tt0 = tt.n

    for oc in range(noc):
        buf = oc % 2
        ro = min(128, O - 128 * oc)
        gld = glds[buf]
        nld = gld.bump(16)
        nbanks = (nic + 7) // 8
        war = tt0 + (oc - 1) * nbanks
        p.gp(lambda e, oc=oc, buf=buf, ro=ro, war=war, nld=nld, tt0=tt0, gld=gld: (
            (e.wait_ge(tt.h, war) if war > tt0 else None),
            e.dma_start(out=wrow[0:ro, buf * I:buf * I + I],
                        in_=w_in[128 * oc:128 * oc + ro, :]).then_inc(gld.h, 16)))
        for half in range((nic + 7) // 8):
            ics = list(range(8 * half, min(8 * half + 8, nic)))
            tpb = (oc * nbanks + half) % 2
            ntt = tt.bump()
            evwar = ev.n - 1
            p.te(lambda e, oc=oc, buf=buf, ro=ro, ics=ics, half=half, nld=nld,
                        ntt=ntt, evwar=evwar, idn_wait=idn_wait, gld=gld, tpb=tpb: (
                e.wait_ge(gld.h, nld),
                (e.wait_ge(ev.h, evwar) if evwar > 0 else None),
                (e.wait_ge(idn_wait[0].h, idn_wait[1]) if (idn_wait and oc == 0 and half == 0) else None),
                [e.transpose(tp[tpb][0:min(128, I - 128 * ic), 128 * t:128 * t + ro],
                             wrow[0:ro, buf * I + 128 * ic:buf * I + min(128 * ic + 128, I)],
                             idn_sb[0:ro, 0:ro])
                 for t, ic in enumerate(ics)][-1].then_inc(tt.h, 1)))
            nev = ev.bump()
            p.ve(lambda e, oc=oc, ro=ro, ics=ics, half=half, ntt=ntt, tpb=tpb: (
                e.wait_ge(tt.h, ntt),
                [e.tensor_copy(
                    wt_sb[0:min(128, I - 128 * ic), ic * O + 128 * oc: ic * O + 128 * oc + ro],
                    tp[tpb][0:min(128, I - 128 * ic), 128 * t:128 * t + ro])
                 for t, ic in enumerate(ics)][-1].then_inc(ev.h, 1)))
    return ev


class PostLN:
    def __init__(self, p, x_in, x_out, tag):
        self.p = p
        self.x_in = x_in
        self.x_out = x_out
        self.lnst = LNStats(p, tag + "_lnst")
        self.xt = p.sb(tag + "_xt", [128, 2 * D], F32)
        self.lny = p.sb(tag + "_lny", [128, D], F32)
        self.ld = p.sem()
        self.ad = p.sem()
        self.st = p.swdge_sem()
        self.st0 = self.st.n

    def emit_ld(self, j, gate=None):
        p = self.p
        nld = self.ld.bump(16)
        stwar = self.st0 + 16 * (j - 1)
        p.sy(lambda e, j=j, nld=nld, stwar=stwar, gate=gate: (
            ([e.wait_ge(s.h, n) for (s, n) in gate]
             if (gate and j == 0) else None),
            (e.wait_ge(self.st.h, stwar) if stwar > self.st0 else None),
            e.dma_start(out=self.xt[:, (j % 2) * D:(j % 2 + 1) * D],
                        in_=self.x_in[128 * j:128 * (j + 1), :]).then_inc(self.ld.h, 16)))
        return nld

    def emit(self, j, srcs4, act_list, wait_v, nld=None):
        p = self.p
        if nld is None:
            nld = self.emit_ld(j)
        nvs = self.lnst.emit(srcs4, wait_v=wait_v)
        nss = self.lnst.ss.bump()
        p.sc(lambda e, act_list=act_list, nvs=nvs: (
            e.wait_ge(self.lnst.vs.h, nvs),
            [e.activation(self.lny[:, off:off + w], src, AF.Identity,
                          bias=self.lnst.nmr[:, 0:1], scale=self.lnst.rstd[:, 0:1])
             for (src, off, w) in act_list][-1].then_inc(self.lnst.ss.h, 1)))
        nad = self.ad.bump()
        p.ve(lambda e, j=j, nss=nss, nld=nld, nad=nad: (
            e.wait_ge(self.lnst.ss.h, nss),
            e.wait_ge(self.ld.h, nld),
            e.tensor_add(self.xt[:, (j % 2) * D:(j % 2 + 1) * D],
                         self.xt[:, (j % 2) * D:(j % 2 + 1) * D],
                         self.lny[:]).then_inc(self.ad.h, 1)))
        nst = self.st.bump(16)
        p.gp(lambda e, j=j, nad=nad, nst=nst: (
            e.wait_ge(self.ad.h, nad),
            e.dma_start(out=self.x_out[128 * j:128 * (j + 1), :],
                        in_=self.xt[:, (j % 2) * D:(j % 2 + 1) * D]).then_inc(self.st.h, 16)))
        return (self.lnst.ss, nss)

    def finish(self):
        p = self.p
        n = self.st.n
        p.sy(lambda e, n=n: e.wait_ge(self.st.h, n))


def build_outproj_prog():
    p = Prog("outproj")
    ct_in = p.din("ct", [D, TPC], BF16)
    wo_in = p.din("wo", [D, D], F32)
    x_in = p.din("x", [TPC, D], F32)
    idn = p.din("idn", [128, 128], BF16)
    x_out = p.dout("xn", [TPC, D], F32)

    idn_sb = p.sb("idn_sb", [128, 128], BF16)
    ld0 = p.sem()
    n0 = ld0.bump(16)
    p.sy(lambda e, n0=n0: e.dma_start(out=idn_sb[:], in_=idn).then_inc(ld0.h, 16))

    wt_sb = p.sb("wt", [128, NDC * D], BF16)
    wdone = emit_wt_prep(p, wo_in, wt_sb, D, D, idn_sb, "wp", idn_wait=(ld0, n0))
    nw = wdone.n

    ct_sb = p.sb("ct_sb", [128, 2 * NDC * 128], BF16)
    y_ps = [p.ps(f"y{n}", [128, 512], F32) for n in range(4)]
    cld = p.sem()
    mm = p.sem()
    pl = PostLN(p, x_in, x_out, "pl")

    free = None
    for j in range(NTT):
        buf = j % 2
        ncld = cld.bump(16)
        war = (j - 1)
        p.sy(lambda e, j=j, buf=buf, ncld=ncld, war=war: (
            (e.wait_ge(mm.h, war) if war > 0 else None),
            e.dma_start(out=ct_sb[:, buf * D:(buf + 1) * D]
                        .rearrange("p (c t) -> p c t", c=NDC),
                        in_=ct_in[:, 128 * j:128 * (j + 1)]
                        .rearrange("(c p) t -> p c t", p=128)).then_inc(cld.h, 16)))
        nmm = mm.bump()
        p.te(lambda e, j=j, buf=buf, ncld=ncld, nmm=nmm, nw=nw, free=free: (
            e.wait_ge(cld.h, ncld),
            (e.wait_ge(wdone.h, nw) if j == 0 else None),
            (e.wait_ge(free[0].h, free[1]) if free else None),
            [e.matmul(y_ps[n][:], ct_sb[:, buf * D + 128 * c:buf * D + 128 * (c + 1)],
                      wt_sb[:, c * D + 512 * n:c * D + 512 * (n + 1)],
                      start=(c == 0), stop=(c == NDC - 1))
             for c in range(NDC) for n in range(4)][-1].then_inc(mm.h, 1)))
        free = pl.emit(j, [yp[:] for yp in y_ps],
                       [(y_ps[n][:], 512 * n, 512) for n in range(4)], (mm, nmm))
    pl.finish()
    return p.finalize()


def build_ffn_prog(kind):
    NH = 5461 if kind == "tern" else 5632
    NHC = (NH + 127) // 128
    p = Prog("ffn_" + kind)
    wdt = I32 if kind == "l3" else F32
    x_in = p.din("x", [TPC, D], F32)
    w1 = p.din("w1", [NH, D], wdt)
    w3 = p.din("w3", [NH, D], wdt)
    w2 = p.din("w2", [D, NH], wdt)
    idn = p.din("idn", [128, 128], BF16)
    gs_in = us_in = ds_in = None
    if kind != "tern":
        gs_in = p.din("gs", [128, NHC], F32)
        us_in = p.din("us", [128, NHC], F32)
        ds_in = p.din("ds", [128, D], F32)
    x_out = p.dout("xn", [TPC, D], F32)
    emit_ffn(p, kind, x_in, x_out, w1, w3, w2, idn, gs_in, us_in, ds_in)
    return p.finalize()


def emit_ffn(p, kind, x_in, x_out, w1, w3, w2, idn, gs_in, us_in, ds_in):
    NH = 5461 if kind == "tern" else 5632
    NHC = (NH + 127) // 128
    NHPAD = NHC * 128
    NHB = (NHC + 7) // 8
    NQ = 4
    w_f32 = kind != "l2"
    g_scr = p.dtmp(p._uname("gscr"), [NHPAD, TPC], BF16)
    y_scr = p.dtmp(p._uname("yscr"), [TPC, 3 * 512], F32)

    tp = [p.ps(f"tp{i}", [128, 1024], BF16) for i in range(2)]
    psA = p.ps("psA", [128, 512], F32)
    psB = p.ps("psB", [128, 512], F32)
    psC = p.ps("psC", [128, 512], F32)
    psD = p.ps("psD", [128, 512], F32)
    ps_misc = p.ps("ps_misc", [128, 16], F32)

    idn_sb = p.sb("idn_sb", [128, 128], BF16)
    if kind == "tern":
        sgA_sb = p.sb("sgA", [128, NHPAD], BF16)
        sgB_sb = p.sb("sgB", [128, NHPAD], BF16)
        thbc = p.sb("thbc", [128, 4], F32)
        nthbc = p.sb("nthbc", [128, 4], F32)
    sg_sb = p.sb("sg_sb", [128, 2 * 512], BF16)
    hu_sb = p.sb("hu_sb", [128, 512], BF16)
    gt_sb = p.sb("gt_sb", [128, 2 * 512], BF16)
    y0_sb = p.sb("y0_sb", [128, 512], F32)
    y0r_sb = p.sb("y0r_sb", [128, 2 * 1536], F32)
    ysc_sb = p.sb("ysc_sb", [128, 512], F32)
    if kind != "tern":
        gs_sb = p.sb("gs_sb", [128, NHC], F32)
        us_sb = p.sb("us_sb", [128, NHC], F32)
        ds_sb = p.sb("ds_sb", [128, D], F32)

    ld0 = p.sem()
    n0 = ld0.bump(16)
    p.sy(lambda e: e.dma_start(out=idn_sb[:], in_=idn).then_inc(ld0.h, 16))
    if kind != "tern":
        n0 = ld0.bump(48)
        p.sy(lambda e: (e.dma_start(out=gs_sb[:], in_=gs_in).then_inc(ld0.h, 16),
                        e.dma_start(out=us_sb[:], in_=us_in).then_inc(ld0.h, 16),
                        e.dma_start(out=ds_sb[:], in_=ds_in).then_inc(ld0.h, 16)))
    p.te(lambda e, n0=n0: e.wait_ge(ld0.h, n0))
    p.ve(lambda e, n0=n0: e.wait_ge(ld0.h, n0))
    p.sc(lambda e, n0=n0: e.wait_ge(ld0.h, n0))

    s12 = p.scope()
    lnT_sb = p.sbs(s12, "lnT_sb", [128, NDC * TPC], BF16)
    lnt_buf = p.sbs(s12, "lnt_buf", [128, 2 * D], BF16)
    wrA = p.sbs(s12, "wrA", [128, 2 * D], F32 if w_f32 else BF16)
    wrB = p.sbs(s12, "wrB", [128, 2 * D], F32 if w_f32 else BF16)
    if kind != "l2":
        tr1 = p.sbs(s12, "tr1", [128, 2 * D], BF16)
        tr3 = p.sbs(s12, "tr3", [128, 2 * D], BF16)
    w1T = p.sbs(s12, "w1T", [128, 2 * D], BF16)
    w3T = p.sbs(s12, "w3T", [128, 2 * D], BF16)
    if kind == "tern":
        acc = p.sbs(s12, "acc", [128, 4], F32)
        red = p.sbs(s12, "red", [128, 1], F32)
        ones128 = p.sbs(s12, "ones128", [128, 1], F32)
        ones1 = p.sbs(s12, "ones1", [1, 128], F32)
        row13 = p.sbs(s12, "row13", [1, 4], F32)
        thw = p.sbs(s12, "thw", [128, 2 * D], F32)

    if w_f32:
        preln_xt = wrA
    else:
        preln_xt = p.sbs(s12, "preln_xt", [128, 2 * D], F32)
    lndone = emit_preln(p, x_in, lnT_sb, idn_sb, tp_pool=tp, xt_buf=preln_xt,
                        lnt_buf=lnt_buf,
                        alloc=lambda n, s, d=F32: p.sbs(s12, n, s, d))
    nln = lndone.n

    if kind == "tern":
        thlds = [p.sem(), p.sem()]
        thv = p.sem()
        tht = p.sem()
        p.ve(lambda e: (e.memset(acc[:], 0.0), e.memset(ones128[:], 1.0),
                        e.memset(ones1[:], 1.0).then_inc(thv.h, 1)))
        thv.bump()
        jobs = []
        for wi, w in enumerate((w1, w3, w2)):
            R, C = (NH, D) if wi < 2 else (D, NH)
            for r0 in range(0, R, 128):
                rr = min(128, R - r0)
                for c0 in range(0, C, 2048):
                    cc = min(2048, C - c0)
                    jobs.append((wi, w, r0, rr, c0, cc))
        for ji, (wi, w, r0, rr, c0, cc) in enumerate(jobs):
            buf = ji % 2
            thld = thlds[buf]
            nld = thld.bump(16)
            war = thv.n - 2
            p.sc(lambda e, w=w, r0=r0, rr=rr, c0=c0, cc=cc, buf=buf, nld=nld,
                        war=war, thld=thld: (
                (e.wait_ge(thv.h, war) if war > 1 else None),
                e.dma_start(out=thw[0:rr, buf * D:buf * D + cc],
                            in_=w[r0:r0 + rr, c0:c0 + cc]).then_inc(thld.h, 16)))
            nv1 = thv.bump()
            p.ve(lambda e, rr=rr, cc=cc, buf=buf, nld=nld, thld=thld: (
                e.wait_ge(thld.h, nld),
                e.tensor_reduce(red[0:rr, :], thw[0:rr, buf * D:buf * D + cc],
                                AX.X, OP.add, True).then_inc(thv.h, 1)))
            nv2 = thv.bump()
            p.ve(lambda e, wi=wi, rr=rr, nv1=nv1: (
                e.wait_ge(thv.h, nv1),
                e.tensor_add(acc[0:rr, wi:wi + 1], acc[0:rr, wi:wi + 1],
                             red[0:rr, :]).then_inc(thv.h, 1)))
        nred_done = thv.n
        ntt1 = tht.bump()
        p.te(lambda e, nv=thv.n: (
            e.wait_ge(thv.h, nv),
            e.matmul(ps_misc[0:1, 0:4], ones128[:], acc[:],
                     start=True, stop=True).then_inc(tht.h, 1)))
        nv3 = thv.bump()
        p.ve(lambda e, ntt1=ntt1: (
            e.wait_ge(tht.h, ntt1),
            e.tensor_scalar(row13[:], ps_misc[0:1, 0:4], 0.5 / (NH * D), None,
                            OP.mult).then_inc(thv.h, 1)))
        ntt2 = tht.bump()
        p.te(lambda e, nv3=nv3: (
            e.wait_ge(thv.h, nv3),
            e.matmul(ps_misc[:, 8:12], ones1[:], row13[:],
                     start=True, stop=True).then_inc(tht.h, 1)))
        nv5 = thv.bump()
        p.ve(lambda e, ntt2=ntt2: (
            e.wait_ge(tht.h, ntt2),
            e.tensor_copy(thbc[:], ps_misc[:, 8:12]),
            e.tensor_scalar(nthbc[:], ps_misc[:, 8:12], -1.0, None,
                            OP.mult).then_inc(thv.h, 1)))
        nth_ready = nv5
    else:
        thv = None
        nth_ready = 0
        nred_done = 0

    def scl1(hc, ro):
        return (thbc[0:ro, 0:1] if kind == "tern" else gs_sb[0:ro, hc:hc + 1])

    def scl3(hc, ro):
        return (thbc[0:ro, 1:2] if kind == "tern" else us_sb[0:ro, hc:hc + 1])

    upld = p.sem() if kind == "tern" else p.swdge_sem()
    qv = p.sem()
    qs = p.sem()
    utt = p.sem()
    uev = p.sem()
    umm = p.sem()
    ue = p.sem()
    gst = p.swdge_sem()

    emit_wld = p.sy if kind == "tern" else p.gp
    qs0, qv0, utt0 = qs.n, qv.n, utt.n
    for hc in range(NHC):
        ro = min(128, NH - 128 * hc)
        buf = hc % 2
        nld = upld.bump(32)
        if kind == "tern":
            war = (qs, qs0 + 4 * (hc - 1), qs0)
        elif kind == "l3":
            war = (qv, qv0 + 2 * (hc - 1), qv0)
        else:
            war = (utt, utt0 + 4 * (hc - 1), utt0)
        emit_wld(lambda e, hc=hc, ro=ro, buf=buf, nld=nld, war=war: (
            (e.wait_ge(lndone.h, nln) if (hc < 2 and w_f32) else None),
            (e.wait_ge(war[0].h, war[1]) if war[1] > war[2] else None),
            e.dma_start(out=wrA[0:ro, buf * D:buf * D + D],
                        in_=w1[128 * hc:128 * hc + ro, :]).then_inc(upld.h, 16),
            e.dma_start(out=wrB[0:ro, buf * D:buf * D + D],
                        in_=w3[128 * hc:128 * hc + ro, :]).then_inc(upld.h, 16)))
        if kind == "tern":
            for wrx, trx, wi in ((wrA, tr1, 0), (wrB, tr3, 1)):
                nqs = qs.bump(2)
                p.sc(lambda e, wrx=wrx, ro=ro, buf=buf, nld=nld, nqs=nqs,
                            wi=wi, hc=hc, war_v=qv.n: (
                    e.wait_ge(upld.h, nld),
                    (e.wait_ge(thv.h, nth_ready) if (hc == 0 and wi == 0) else None),
                    (e.wait_ge(qv.h, war_v) if war_v > 0 else None),
                    e.activation(sgA_sb[0:ro, 0:D], wrx[0:ro, buf * D:buf * D + D],
                                 AF.Sign, bias=nthbc[0:ro, wi:wi + 1]),
                    e.activation(sgB_sb[0:ro, 0:D], wrx[0:ro, buf * D:buf * D + D],
                                 AF.Sign, bias=thbc[0:ro, wi:wi + 1]).then_inc(qs.h, 2)))
                nqv = qv.bump()
                p.ve(lambda e, trx=trx, ro=ro, buf=buf, nqs=nqs,
                            war_t=utt.n - 2: (
                    e.wait_ge(qs.h, nqs),
                    (e.wait_ge(utt.h, war_t) if war_t > 0 else None),
                    e.tensor_add(trx[0:ro, buf * D:buf * D + D], sgA_sb[0:ro, 0:D],
                                 sgB_sb[0:ro, 0:D]).then_inc(qv.h, 1)))
            qsrc1, qsrc3 = tr1, tr3
            qwait = (qv, qv.n)
        elif kind == "l3":
            nqv = qv.bump(2)
            p.ve(lambda e, ro=ro, buf=buf, nld=nld, war_t=utt.n - 2: (
                e.wait_ge(upld.h, nld),
                (e.wait_ge(utt.h, war_t) if war_t > 0 else None),
                e.tensor_scalar(tr1[0:ro, buf * D:buf * D + D],
                                wrA[0:ro, buf * D:buf * D + D], 128.0, None,
                                OP.subtract),
                e.tensor_scalar(tr3[0:ro, buf * D:buf * D + D],
                                wrB[0:ro, buf * D:buf * D + D], 128.0, None,
                                OP.subtract).then_inc(qv.h, 2)))
            qsrc1, qsrc3 = tr1, tr3
            qwait = (qv, qv.n)
        else:
            qsrc1, qsrc3 = wrA, wrB
            qwait = (upld, upld.n)
        for gi, (src, dstT, half) in enumerate(((qsrc1, w1T, 0), (qsrc1, w1T, 1),
                                                (qsrc3, w3T, 0), (qsrc3, w3T, 1))):
            ntt = utt.bump()
            evwar = uev.n - 1
            p.te(lambda e, src=src, dstT=dstT, half=half, ro=ro, buf=buf, gi=gi,
                        qwait=qwait, ntt=ntt, evwar=evwar, hc=hc: (
                (e.wait_ge(qwait[0].h, qwait[1]) if gi == 0 else None),
                (e.wait_ge(lndone.h, nln) if (hc == 0 and gi == 0) else None),
                (e.wait_ge(uev.h, evwar) if evwar > 0 else None),
                [e.transpose(tp[gi % 2][:, 128 * t:128 * t + ro],
                             src[0:ro, buf * D + 128 * (8 * half + t):buf * D + 128 * (8 * half + t + 1)],
                             idn_sb[0:ro, 0:ro]) for t in range(8)][-1].then_inc(utt.h, 1)))
            nev = uev.bump()
            p.ve(lambda e, dstT=dstT, half=half, ro=ro, buf=buf, gi=gi, ntt=ntt: (
                e.wait_ge(utt.h, ntt),
                [e.tensor_copy(dstT[:, buf * D + 128 * (8 * half + t):buf * D + 128 * (8 * half + t) + ro],
                               tp[gi % 2][:, 128 * t:128 * t + ro])
                 for t in range(8)][-1].then_inc(uev.h, 1)))
        for tq in range(4):
            pa, pb = (psA, psB) if tq % 2 == 0 else (psC, psD)
            so = (tq % 2) * 512
            nmm = umm.bump(2)
            pw = ue.n - 4
            p.te(lambda e, ro=ro, buf=buf, tq=tq, nmm=nmm, nev=uev.n, pw=pw,
                        pa=pa, pb=pb: (
                e.wait_ge(uev.h, nev),
                (e.wait_ge(ue.h, pw) if pw > 0 else None),
                [e.matmul(pa[0:ro, :],
                          w1T[:, buf * D + 128 * i:buf * D + 128 * i + ro],
                          lnT_sb[:, i * TPC + 512 * tq:i * TPC + 512 * (tq + 1)],
                          start=(i == 0), stop=(i == NDC - 1)) for i in range(NDC)][-1],
                [e.matmul(pb[0:ro, :],
                          w3T[:, buf * D + 128 * i:buf * D + 128 * i + ro],
                          lnT_sb[:, i * TPC + 512 * tq:i * TPC + 512 * (tq + 1)],
                          start=(i == 0), stop=(i == NDC - 1)) for i in range(NDC)][-1]
                .then_inc(umm.h, 2)))
            ne1 = ue.bump()
            p.sc(lambda e, hc=hc, ro=ro, nmm=nmm, pa=pa, so=so, nw=ne1 - 4: (
                e.wait_ge(umm.h, nmm),
                (e.wait_ge(ue.h, nw) if nw > 0 else None),
                e.activation(sg_sb[0:ro, so:so + 512], pa[0:ro, :], AF.Silu,
                             scale=scl1(hc, ro)).then_inc(ue.h, 1)))
            ne2 = ue.bump()
            p.ve(lambda e, hc=hc, ro=ro, nmm=nmm, pb=pb, so=so: (
                e.wait_ge(umm.h, nmm),
                e.tensor_scalar(hu_sb[0:ro, :], pb[0:ro, :], scl3(hc, ro), None,
                                OP.mult).then_inc(ue.h, 1)))
            ne3 = ue.bump()
            gbuf = tq % 2
            war_g = gst.n - 16
            p.ve(lambda e, ro=ro, ne1=ne1, ne2=ne2, gbuf=gbuf, war_g=war_g, so=so: (
                e.wait_ge(ue.h, ne1),
                e.wait_ge(ue.h, ne2),
                (e.wait_ge(gst.h, war_g) if war_g > 0 else None),
                e.tensor_mul(gt_sb[0:ro, gbuf * 512:(gbuf + 1) * 512],
                             sg_sb[0:ro, so:so + 512], hu_sb[0:ro, :]).then_inc(ue.h, 1)))
            ngs = gst.bump(16)
            p.gp(lambda e, hc=hc, ro=ro, tq=tq, ne3=ne3, gbuf=gbuf: (
                e.wait_ge(ue.h, ne3),
                e.dma_start(out=g_scr[128 * hc:128 * hc + ro, 512 * tq:512 * (tq + 1)],
                            in_=gt_sb[0:ro, gbuf * 512:(gbuf + 1) * 512])
                .then_inc(gst.h, 16)))
    ngst_all = gst.n
    numm_all = umm.n

    s12.close()
    pl = PostLN(p, x_in, x_out, "pl")
    s3 = p.scope()
    w2T = p.sbs(s3, "w2T", [128, 2 * NHC * 512], BF16)
    g_sb = p.sbs(s3, "g_sb", [128, 2 * NHPAD], BF16)
    wdr = p.sbs(s3, "wdr", [128, NHPAD], F32 if w_f32 else BF16)
    if kind != "l2":
        tdr = p.sbs(s3, "tdr", [128, NHPAD], BF16)

    dld = p.sem() if kind == "tern" else p.swdge_sem()
    dqv = p.sem()
    dqs = p.sem()
    dtt = p.sem()
    dev = p.sem()
    gld = p.sem()
    dmm = p.sem()
    yev = p.sem()
    yst = p.swdge_sem()
    yld = p.sem()
    plfree = None

    emit_dld = p.sy if kind == "tern" else p.gp
    dqs0, dqv0, dtt0, dmm0 = dqs.n, dqv.n, dtt.n, dmm.n
    for q in range(NQ):
        for dc in range(4):
            dg = 4 * q + dc
            nld = dld.bump(16)
            if kind == "tern":
                war = (dqs, dqs0 + 2 * dg, dqs0)
            elif kind == "l3":
                war = (dqv, dqv0 + dg, dqv0)
            else:
                war = (dtt, dtt0 + NHB * dg, dtt0)
            emit_dld(lambda e, dg=dg, nld=nld, war=war: (
                (e.wait_ge(gst.h, ngst_all) if dg == 0 else None),
                (e.wait_ge(umm.h, numm_all) if dg == 0 else None),
                (e.wait_ge(war[0].h, war[1]) if war[1] > war[2] else None),
                e.dma_start(out=wdr[:, 0:NH],
                            in_=w2[128 * dg:128 * (dg + 1), :]).then_inc(dld.h, 16)))
            if kind == "tern":
                ndqs = dqs.bump(2)
                p.sc(lambda e, nld=nld, war_v=dqv.n: (
                    e.wait_ge(dld.h, nld),
                    (e.wait_ge(dqv.h, war_v) if war_v > 0 else None),
                    e.activation(sgA_sb[:, 0:NH], wdr[:, 0:NH],
                                 AF.Sign, bias=nthbc[:, 2:3]),
                    e.activation(sgB_sb[:, 0:NH], wdr[:, 0:NH],
                                 AF.Sign, bias=thbc[:, 2:3]).then_inc(dqs.h, 2)))
                ndqv = dqv.bump()
                p.ve(lambda e, ndqs=ndqs, war_t=dtt.n: (
                    e.wait_ge(dqs.h, ndqs),
                    (e.wait_ge(dtt.h, war_t) if war_t > 0 else None),
                    e.tensor_add(tdr[:, 0:NH], sgA_sb[:, 0:NH],
                                 sgB_sb[:, 0:NH]).then_inc(dqv.h, 1)))
                dq_src, dq_wait = tdr, (dqv, dqv.n)
            elif kind == "l3":
                ndqv = dqv.bump()
                p.ve(lambda e, nld=nld, war_t=dtt.n: (
                    e.wait_ge(dld.h, nld),
                    (e.wait_ge(dtt.h, war_t) if war_t > 0 else None),
                    e.tensor_scalar(tdr[:, 0:NH], wdr[:, 0:NH],
                                    128.0, None, OP.subtract).then_inc(dqv.h, 1)))
                dq_src, dq_wait = tdr, (dqv, dqv.n)
            else:
                dq_src, dq_wait = wdr, (dld, nld)
            for hb in range(NHB):
                hcs = list(range(8 * hb, min(8 * hb + 8, NHC)))
                ntt = dtt.bump()
                evwar = dev.n - 1
                p.te(lambda e, dq_src=dq_src, dq_wait=dq_wait, hcs=hcs,
                            hb=hb, ntt=ntt, evwar=evwar: (
                    (e.wait_ge(dq_wait[0].h, dq_wait[1]) if hb == 0 else None),
                    (e.wait_ge(dev.h, evwar) if evwar > 0 else None),
                    [e.transpose(tp[hb % 2][0:min(128, NH - 128 * hcx), 128 * t:128 * (t + 1)],
                                 dq_src[:, 128 * hcx:min(128 * (hcx + 1), NH)],
                                 idn_sb[:]) for t, hcx in enumerate(hcs)][-1]
                    .then_inc(dtt.h, 1)))
                ndev = dev.bump()
                w2war = dmm0 + 16 * (q - 1) if q > 1 else 0
                qb = (q % 2) * NHC * 512
                p.ve(lambda e, hcs=hcs, hb=hb, dc=dc, ntt=ntt, w2war=w2war,
                            qb=qb: (
                    e.wait_ge(dtt.h, ntt),
                    (e.wait_ge(dmm.h, w2war) if w2war > 0 else None),
                    [e.tensor_copy(w2T[0:min(128, NH - 128 * hcx),
                                       qb + hcx * 512 + 128 * dc:qb + hcx * 512 + 128 * (dc + 1)],
                                   tp[hb % 2][0:min(128, NH - 128 * hcx), 128 * t:128 * (t + 1)])
                     for t, hcx in enumerate(hcs)][-1].then_inc(dev.h, 1)))
        ndev_q = dev.n
        for tj in range(NTT):
            gti = 16 * q + tj
            buf = tj % 2
            psn = psC if tj % 2 == 0 else psD
            ngl = gld.bump(16)
            mmwar = dmm.n - 1
            p.sy(lambda e, tj=tj, buf=buf, ngl=ngl, mmwar=mmwar, q=q: (
                (e.wait_ge(gst.h, ngst_all) if (q == 0 and tj == 0) else None),
                (e.wait_ge(dmm.h, mmwar) if mmwar > 0 else None),
                e.dma_start(out=g_sb[:, buf * NHPAD:(buf + 1) * NHPAD]
                            .rearrange("p (c t) -> p c t", c=NHC),
                            in_=g_scr[:, 128 * tj:128 * (tj + 1)]
                            .rearrange("(c p) t -> p c t", p=128)).then_inc(gld.h, 16)))
            nmm = dmm.bump()
            if q < 3:
                psfree = (yev, yev.n - 1)
            else:
                psfree = plfree if plfree else (yev, yev.n - 1)
            qb = (q % 2) * NHC * 512
            p.te(lambda e, buf=buf, ngl=ngl, nmm=nmm, ndev=ndev_q, tj=tj,
                        psn=psn, psfree=psfree, qb=qb: (
                e.wait_ge(gld.h, ngl),
                (e.wait_ge(dev.h, ndev) if tj == 0 else None),
                (e.wait_ge(psfree[0].h, psfree[1]) if psfree[1] > 0 else None),
                [e.matmul(psn[:],
                          g_sb[0:min(128, NH - 128 * hcx), buf * NHPAD + 128 * hcx:buf * NHPAD + 128 * (hcx + 1)],
                          w2T[0:min(128, NH - 128 * hcx), qb + hcx * 512:qb + hcx * 512 + 512],
                          start=(hcx == 0), stop=(hcx == NHC - 1))
                 for hcx in range(NHC)][-1].then_inc(dmm.h, 1)))
            if q < 3:
                nye = yev.bump()
                stwar = yst.n - 16
                p.ve(lambda e, nmm=nmm, stwar=stwar, psn=psn: (
                    e.wait_ge(dmm.h, nmm),
                    (e.wait_ge(yst.h, stwar) if stwar > 0 else None),
                    e.tensor_copy(y0_sb[:], psn[:]).then_inc(yev.h, 1)))
                nys = yst.bump(16)
                p.gp(lambda e, tj=tj, q=q, nye=nye: (
                    e.wait_ge(yev.h, nye),
                    e.dma_start(out=y_scr[128 * tj:128 * (tj + 1), 512 * q:512 * (q + 1)],
                                in_=y0_sb[:]).then_inc(yst.h, 16)))
            else:
                nyl = yld.bump(16)
                ldwar = pl.lnst.ss.n - 2
                p.sy(lambda e, tj=tj, buf=buf, nyl=nyl, ldwar=ldwar: (
                    (e.wait_ge(pl.lnst.ss.h, ldwar) if ldwar > 0 else None),
                    e.dma_start(out=y0r_sb[:, buf * 1536:(buf + 1) * 1536],
                                in_=y_scr[128 * tj:128 * (tj + 1), :]).then_inc(yld.h, 16)))
                if kind != "tern":
                    nsc = yev.bump()
                    p.ve(lambda e, buf=buf, nmm=nmm, nyl=nyl, psn=psn: (
                        e.wait_ge(dmm.h, nmm),
                        e.wait_ge(yld.h, nyl),
                        e.tensor_mul(y0r_sb[:, buf * 1536:(buf + 1) * 1536],
                                     y0r_sb[:, buf * 1536:(buf + 1) * 1536],
                                     ds_sb[:, 0:1536]),
                        e.tensor_mul(ysc_sb[:], psn[:],
                                     ds_sb[:, 1536:2048]).then_inc(yev.h, 1)))
                    srcs = [y0r_sb[:, buf * 1536 + 512 * i:buf * 1536 + 512 * (i + 1)]
                            for i in range(3)] + [ysc_sb[:]]
                    acts = [(y0r_sb[:, buf * 1536:(buf + 1) * 1536], 0, 1536),
                            (ysc_sb[:], 1536, 512)]
                    plfree = pl.emit(tj, srcs, acts, (yev, nsc))
                else:
                    nsc = yev.bump()
                    p.ve(lambda e, nmm=nmm, nyl=nyl: (
                        e.wait_ge(dmm.h, nmm),
                        e.wait_ge(yld.h, nyl),
                        e.engine_nop().then_inc(yev.h, 1)))
                    srcs = [y0r_sb[:, buf * 1536 + 512 * i:buf * 1536 + 512 * (i + 1)]
                            for i in range(3)] + [psn[:]]
                    acts = [(y0r_sb[:, buf * 1536:(buf + 1) * 1536], 0, 1536),
                            (psn[:], 1536, 512)]
                    plfree = pl.emit(tj, srcs, acts, (yev, nsc))
    s3.close()
    pl.finish()


def build_attn3_prog():
    p = Prog("attn3")
    lnt_in = p.din("lnt", [D, S], BF16)
    wq_in = p.din("wq", [512, D], F32)
    wk_in = p.din("wk", [512, D], F32)
    wv_in = p.din("wv", [512, D], F32)
    idn = p.din("idn", [128, 128], BF16)
    dmask = p.din("dmask", [128, 4 * 512], BF16)
    ct_out = p.dout("ct", [512, S], BF16)
    emit_attn3(p, lambda ts: lnt_in[:, 512 * ts:512 * (ts + 1)],
               wq_in, wk_in, wv_in, idn, dmask, ct_out)
    return p.finalize()


def emit_attn3(p, lnt_tile, wq_in, wk_in, wv_in, idn, dmask, ct_out,
               w_wait=None, lnt_wait=None):

    tp = [p.ps(f"tp{i}", [128, 1024], BF16) for i in range(2)]
    ps0 = p.ps("ps0", [128, 512], F32)
    ps1 = p.ps("ps1", [128, 512], F32)
    ps2 = p.ps("ps2", [128, 512], F32)
    ps3 = p.ps("ps3", [128, 512], F32)
    psDn = p.ps("psDn", [1, 512], F32)
    psDn2 = p.ps("psDn2", [1, 512], F32)

    idn_sb = p.sb("idn_sb", [128, 128], BF16)
    dmask_sb = p.sb("dmask_sb", [128, 4 * 512], BF16)
    negc = p.sb("negc", [128, 1], F32)
    ones128 = p.sb("ones128", [128, 1], BF16)
    one1 = p.sb("one1", [1, 128], F32)
    wrow = p.sb("wrow", [128, D], BF16)
    wqT = p.sb("wqT", [128, D], BF16)
    wkT = p.sb("wkT", [128, D], BF16)
    wvT = p.sb("wvT", [128, D], BF16)
    lnt_sb = p.sb("lnt_sb", [128, 2 * 16 * 512], BF16)
    qT_h = p.sb("qT_h", [128, S], BF16)
    kT_h = p.sb("kT_h", [128, S], BF16)
    vT_h = p.sb("vT_h", [128, S], BF16)
    vsm = p.sb("vsm", [128, S], BF16)
    pt_sb = p.sb("pt_sb", [128, 32 * 512], BF16)
    rd_sb = p.sb("rd_sb", [1, 512], F32)
    rdd_sb = p.sb("rdd_sb", [1, 512], F32)
    os_sb = p.sb("os_sb", [128, 512], F32)
    bc_sb = p.sb("bc_sb", [128, 512], F32)
    ct_sb = p.sb("ct_sb", [128, 2 * 512], BF16)

    ld0 = p.sem()
    n0 = ld0.bump(32)
    p.sy(lambda e, w_wait=w_wait: (
        (e.wait_ge(w_wait[0].h, w_wait[1]) if w_wait else None),
        e.dma_start(out=idn_sb[:], in_=idn).then_inc(ld0.h, 16),
        e.dma_start(out=dmask_sb[:], in_=dmask).then_inc(ld0.h, 16)))
    vc0 = p.sem()
    nvc0 = vc0.bump()
    p.ve(lambda e, n0=n0: (
        e.wait_ge(ld0.h, n0),
        e.memset(negc[:], -CEXP), e.memset(ones128[:], 1.0),
        e.memset(one1[:], 1.0).then_inc(vc0.h, 1)))
    p.te(lambda e, n0=n0: e.wait_ge(ld0.h, n0))
    p.sc(lambda e, nvc0=nvc0: e.wait_ge(vc0.h, nvc0))
    p.ve(lambda e, n0=n0: e.wait_ge(ld0.h, n0))

    gld = p.swdge_sem()
    wtt = p.sem()
    wev = p.sem()
    lnld = p.sem()
    pmm = p.sem()
    pev = p.sem()
    vtt = p.sem()
    vev = p.sem()
    fmm = p.sem()
    fex = p.sem()
    fpt = p.sem()
    fac = p.sem()
    fden = p.sem()
    fpo = p.sem()
    cst = p.swdge_sem()
    hbar = p.sem()
    hbar0 = hbar.n

    SCL = 1.0 / float(np.sqrt(128.0))
    bank_last_exp = [None, None]
    slot_last_den = [0] * 32

    for hi in range(4):
        for wi, (w_in, wT) in enumerate(((wq_in, wqT), (wk_in, wkT), (wv_in, wvT))):
            nld = gld.bump(16)
            p.gp(lambda e, hi=hi, wi=wi, w_in=w_in, nld=nld, wtw=wtt.n,
                        w_wait=w_wait: (
                (e.wait_ge(w_wait[0].h, w_wait[1])
                 if (w_wait and hi == 0 and wi == 0) else None),
                (e.wait_ge(wtt.h, wtw) if wtw > 0 else None),
                e.dma_start(out=wrow[:], in_=w_in[128 * hi:128 * (hi + 1), :])
                .then_inc(gld.h, 16)))
            for half in range(2):
                ntt = wtt.bump()
                p.te(lambda e, half=half, nld=nld, ntt=ntt, evw=wev.n - 1, hi=hi: (
                    e.wait_ge(gld.h, nld),
                    (e.wait_ge(wev.h, evw) if evw > 0 else None),
                    [e.transpose(tp[half][:, 128 * t:128 * (t + 1)],
                                 wrow[:, 128 * (8 * half + t):128 * (8 * half + t + 1)],
                                 idn_sb[:]) for t in range(8)][-1].then_inc(wtt.h, 1)))
                nev = wev.bump()
                p.ve(lambda e, wT=wT, half=half, ntt=ntt, hb=hi - 1, hi=hi: (
                    e.wait_ge(wtt.h, ntt),
                    (e.wait_ge(hbar.h, hbar0 + hb + 1) if (hb >= 0) else None),
                    [e.tensor_copy(wT[:, 128 * (8 * half + t):128 * (8 * half + t) + 128],
                                   tp[half][:, 128 * t:128 * (t + 1)])
                     for t in range(8)][-1].then_inc(wev.h, 1)))
        nwev_h = wev.n
        for ts in range(16):
            buf = ts % 2
            nll = lnld.bump(16)
            pmwar = pmm.n - 1
            p.sy(lambda e, ts=ts, buf=buf, nll=nll, pmwar=pmwar, hi=hi,
                        lnt_wait=lnt_wait: (
                (e.wait_ge(lnt_wait[0].h, lnt_wait[1])
                 if (lnt_wait and hi == 0 and ts == 0) else None),
                (e.wait_ge(pmm.h, pmwar) if pmwar > 0 else None),
                e.dma_start(out=lnt_sb[:, buf * 8192:(buf + 1) * 8192]
                            .rearrange("p (c t) -> p c t", c=16),
                            in_=lnt_tile(ts)
                            .rearrange("(c p) t -> p c t", p=128)).then_inc(lnld.h, 16)))
            nmm = pmm.bump()
            p.te(lambda e, ts=ts, buf=buf, nll=nll, nmm=nmm, nwev=nwev_h,
                        pvw=pev.n: (
                e.wait_ge(lnld.h, nll),
                (e.wait_ge(wev.h, nwev) if ts == 0 else None),
                (e.wait_ge(pev.h, pvw) if pvw > 0 else None),
                [e.matmul(psm[:], wT[:, 128 * i:128 * (i + 1)],
                          lnt_sb[:, buf * 8192 + 512 * i:buf * 8192 + 512 * (i + 1)],
                          start=(i == 0), stop=(i == NDC - 1))
                 for wT, psm in ((wqT, ps0), (wkT, ps1), (wvT, ps2))
                 for i in range(NDC)][-1].then_inc(pmm.h, 1)))
            npe = pev.bump(3)
            p.ve(lambda e, ts=ts, nmm=nmm, hb=hi - 1: (
                e.wait_ge(pmm.h, nmm),
                (e.wait_ge(hbar.h, hbar0 + hb + 1) if hb >= 0 else None),
                e.tensor_copy(qT_h[:, 512 * ts:512 * (ts + 1)], ps0[:]),
                e.tensor_copy(kT_h[:, 512 * ts:512 * (ts + 1)], ps1[:]),
                e.tensor_copy(vT_h[:, 512 * ts:512 * (ts + 1)], ps2[:]).then_inc(pev.h, 3)))
        npev_h = pev.n
        for g in range(8):
            ntt = vtt.bump()
            p.te(lambda e, g=g, npev=npev_h, evw=vev.n - 1: (
                (e.wait_ge(pev.h, npev) if g == 0 else None),
                (e.wait_ge(vev.h, evw) if evw > 0 else None),
                [e.transpose(tp[g % 2][:, 128 * t:128 * (t + 1)],
                             vT_h[:, 128 * (8 * g + t):128 * (8 * g + t + 1)],
                             idn_sb[:]) for t in range(8)][-1].then_inc(vtt.h, 1)))
            nev = vev.bump()
            p.ve(lambda e, g=g, ntt=ntt: (
                e.wait_ge(vtt.h, ntt),
                [e.tensor_copy(vsm[:, 128 * (8 * g + t):128 * (8 * g + t) + 128],
                               tp[g % 2][:, 128 * t:128 * (t + 1)])
                 for t in range(8)][-1].then_inc(vev.h, 1)))
        nvev_h = vev.n
        for qt in range(16):
            nkc = 4 * qt + 4
            nhalves = 1 if nkc <= 32 else 2
            psS = (ps0, ps2)
            for H in range(nhalves):
                lo, hi_kc = 32 * H, min(32 * (H + 1), nkc)
                this_half_pt = {}
                for kc in range(lo, hi_kc):
                    s = kc - lo
                    diag_r = kc - 4 * qt
                    war = bank_last_exp[kc % 2]
                    nfm = fmm.bump()
                    fw = fpo.n if (kc == 0) else 0
                    p.te(lambda e, qt=qt, kc=kc, nfm=nfm, war=war,
                                nvev=nvev_h, fw=fw: (
                        (e.wait_ge(vev.h, nvev) if (qt == 0 and kc == 0) else None),
                        (e.wait_ge(war[0].h, war[1]) if war else None),
                        (e.wait_ge(fpo.h, fw) if fw > 0 else None),
                        e.matmul(psS[kc % 2][:], kT_h[:, 128 * kc:128 * (kc + 1)],
                                 qT_h[:, 512 * qt:512 * (qt + 1)],
                                 start=True, stop=True).then_inc(fmm.h, 1)))
                    sw = slot_last_den[s]
                    if diag_r < 0:
                        nfp = fpt.bump()
                        p.sc(lambda e, s=s, kcp=kc % 2, nfm=nfm, sw=sw: (
                            e.wait_ge(fmm.h, nfm),
                            (e.wait_ge(fden.h, sw) if sw > 0 else None),
                            e.activation(pt_sb[:, 512 * s:512 * (s + 1)],
                                         psS[kcp][:], AF.Exp, bias=negc[:, 0:1],
                                         scale=SCL).then_inc(fpt.h, 1)))
                        bank_last_exp[kc % 2] = (fpt, fpt.n)
                    else:
                        nfe = fex.bump()
                        p.sc(lambda e, s=s, kcp=kc % 2, nfm=nfm, sw=sw: (
                            e.wait_ge(fmm.h, nfm),
                            (e.wait_ge(fden.h, sw) if sw > 0 else None),
                            e.activation(pt_sb[:, 512 * s:512 * (s + 1)],
                                         psS[kcp][:], AF.Exp, bias=negc[:, 0:1],
                                         scale=SCL).then_inc(fex.h, 1)))
                        bank_last_exp[kc % 2] = (fex, fex.n)
                        nfp = fpt.bump()
                        p.ve(lambda e, s=s, nfe=nfe, r=diag_r: (
                            e.wait_ge(fex.h, nfe),
                            e.tensor_mul(pt_sb[:, 512 * s:512 * (s + 1)],
                                         pt_sb[:, 512 * s:512 * (s + 1)],
                                         dmask_sb[:, 512 * r:512 * (r + 1)])
                            .then_inc(fpt.h, 1)))
                    this_half_pt[s] = fpt.n
                psO = ps1 if H == 0 else ps3
                psD = psDn if H == 0 else psDn2
                for kc in range(lo, hi_kc):
                    s = kc - lo
                    xw = this_half_pt[s]
                    last = (kc == hi_kc - 1)
                    if last:
                        nfa = fac.bump()
                        p.te(lambda e, kc=kc, s=s, xw=xw, psO=psO, nfa=nfa,
                                    st=(s == 0): (
                            e.wait_ge(fpt.h, xw),
                            e.matmul(psO[:], vsm[:, 128 * kc:128 * (kc + 1)],
                                     pt_sb[:, 512 * s:512 * (s + 1)],
                                     start=st, stop=True).then_inc(fac.h, 1)))
                    else:
                        p.te(lambda e, kc=kc, s=s, xw=xw, psO=psO,
                                    st=(s == 0): (
                            e.wait_ge(fpt.h, xw),
                            e.matmul(psO[:], vsm[:, 128 * kc:128 * (kc + 1)],
                                     pt_sb[:, 512 * s:512 * (s + 1)],
                                     start=st, stop=False)))
                for kc in range(lo, hi_kc):
                    s = kc - lo
                    nfd = fden.bump()
                    p.te(lambda e, s=s, psD=psD, nfd=nfd, st=(s == 0),
                                last=(kc == hi_kc - 1): (
                        e.matmul(psD[:], ones128[:],
                                 pt_sb[:, 512 * s:512 * (s + 1)],
                                 start=st, stop=last).then_inc(fden.h, 1)))
                    slot_last_den[s] = fden.n
            nfo1 = fpo.bump()
            if nhalves == 2:
                p.ve(lambda e, nfa=fac.n, nfd=fden.n: (
                    e.wait_ge(fac.h, nfa),
                    e.wait_ge(fden.h, nfd),
                    e.tensor_copy(os_sb[:], ps1[:]),
                    e.tensor_add(os_sb[:], os_sb[:], ps3[:]),
                    e.tensor_copy(rdd_sb[:], psDn[0:1, :]),
                    e.tensor_add(rdd_sb[:], rdd_sb[:], psDn2[0:1, :]),
                    e.reciprocal(rd_sb[:], rdd_sb[:]).then_inc(fpo.h, 1)))
            else:
                p.ve(lambda e, nfa=fac.n, nfd=fden.n: (
                    e.wait_ge(fac.h, nfa),
                    e.wait_ge(fden.h, nfd),
                    e.tensor_copy(os_sb[:], ps1[:]),
                    e.reciprocal(rd_sb[:], psDn[0:1, :]).then_inc(fpo.h, 1)))
            nfo2 = fpo.bump()
            p.te(lambda e, nfo1=nfo1, xw1=bank_last_exp[0], xw2=bank_last_exp[1]: (
                e.wait_ge(fpo.h, nfo1),
                (e.wait_ge(xw1[0].h, xw1[1]) if xw1 else None),
                (e.wait_ge(xw2[0].h, xw2[1]) if xw2 else None),
                e.matmul(ps2[:], one1[:], rd_sb[0:1, :],
                         start=True, stop=True).then_inc(fpo.h, 1)))
            nfo3 = fpo.bump()
            p.ve(lambda e, nfo2=nfo2: (
                e.wait_ge(fpo.h, nfo2),
                e.tensor_copy(bc_sb[:], ps2[:]).then_inc(fpo.h, 1)))
            nfo4 = fpo.bump()
            cbuf = qt % 2
            p.ve(lambda e, nfo3=nfo3, cbuf=cbuf, stw=cst.n - 16: (
                e.wait_ge(fpo.h, nfo3),
                (e.wait_ge(cst.h, stw) if stw > 0 else None),
                e.tensor_mul(ct_sb[:, cbuf * 512:(cbuf + 1) * 512], os_sb[:],
                             bc_sb[:]).then_inc(fpo.h, 1)))
            ncs = cst.bump(16)
            p.gp(lambda e, hi=hi, qt=qt, cbuf=cbuf, nfo4=nfo4: (
                e.wait_ge(fpo.h, nfo4),
                e.dma_start(out=ct_out[128 * hi:128 * (hi + 1), 512 * qt:512 * (qt + 1)],
                            in_=ct_sb[:, cbuf * 512:(cbuf + 1) * 512])
                .then_inc(cst.h, 16)))
        nhb = hbar.bump()
        p.ve(lambda e, nfo=fpo.n, nhb=nhb: (
            e.wait_ge(fpo.h, nfo), e.engine_nop().then_inc(hbar.h, 1)))
    p.sy(lambda e, n=cst.n: e.wait_ge(cst.h, n))


def build_l1qkv_prog():
    p = Prog("l1qkv")
    x_in = p.din("x", [TPC, D], F32)
    w_in = p.din("wqkv", [3 * D, D], F32)
    idn = p.din("idn", [128, 128], BF16)
    qkvT_out = p.dout("qkvT", [3 * D, TPC], BF16)
    emit_l1qkv(p, x_in, w_in, idn, qkvT_out)
    return p.finalize()


def emit_l1qkv(p, x_in, w_in, idn, qkvT_out):

    tp = [p.ps(f"tp{i}", [128, 1024], BF16) for i in range(2)]
    psm = [p.ps(f"psm{i}", [128, 512], F32) for i in range(4)]

    idn_sb = p.sb("idn_sb", [128, 128], BF16)
    ld0 = p.sem()
    n0 = ld0.bump(16)
    p.sy(lambda e: e.dma_start(out=idn_sb[:], in_=idn).then_inc(ld0.h, 16))
    p.te(lambda e, n0=n0: e.wait_ge(ld0.h, n0))
    p.ve(lambda e, n0=n0: e.wait_ge(ld0.h, n0))

    lnT_sb = p.sb("lnT_sb", [128, NDC * TPC], BF16)
    lndone = emit_preln(p, x_in, lnT_sb, idn_sb, tp_pool=tp)
    nln = lndone.n

    wrow = p.sb("wrow", [128, 2 * D], BF16)
    wTc = p.sb("wTc", [128, 2 * D], BF16)
    ost = p.sb("ost", [128, 2 * D], BF16)
    gld = p.swdge_sem()
    wtt = p.sem()
    wev = p.sem()
    mm = p.sem()
    oev = p.sem()
    ost_st = p.swdge_sem()

    for oc in range(48):
        buf = oc % 2
        nld = gld.bump(16)
        p.gp(lambda e, oc=oc, buf=buf, nld=nld, ttw=wtt.n - 2: (
            (e.wait_ge(wtt.h, ttw) if ttw > 0 else None),
            e.dma_start(out=wrow[:, buf * D:(buf + 1) * D],
                        in_=w_in[128 * oc:128 * (oc + 1), :]).then_inc(gld.h, 16)))
        for half in range(2):
            ntt = wtt.bump()
            p.te(lambda e, oc=oc, buf=buf, half=half, nld=nld, ntt=ntt,
                        evw=wev.n - 1: (
                e.wait_ge(gld.h, nld),
                (e.wait_ge(lndone.h, nln) if (oc == 0 and half == 0) else None),
                (e.wait_ge(wev.h, evw) if evw > 0 else None),
                [e.transpose(tp[half][:, 128 * t:128 * (t + 1)],
                             wrow[:, buf * D + 128 * (8 * half + t):buf * D + 128 * (8 * half + t + 1)],
                             idn_sb[:]) for t in range(8)][-1].then_inc(wtt.h, 1)))
            nev = wev.bump()
            p.ve(lambda e, buf=buf, half=half, ntt=ntt, mmw=mm.n - 1: (
                e.wait_ge(wtt.h, ntt),
                (e.wait_ge(mm.h, mmw) if mmw > 0 else None),
                [e.tensor_copy(wTc[:, buf * D + 128 * (8 * half + t):buf * D + 128 * (8 * half + t) + 128],
                               tp[half][:, 128 * t:128 * (t + 1)])
                 for t in range(8)][-1].then_inc(wev.h, 1)))
        nmm = mm.bump()
        p.te(lambda e, oc=oc, buf=buf, nmm=nmm, nev=wev.n, oevw=oev.n - 1: (
            e.wait_ge(wev.h, nev),
            (e.wait_ge(oev.h, oevw) if oevw > 0 else None),
            [e.matmul(psm[tq][:], wTc[:, buf * D + 128 * i:buf * D + 128 * (i + 1)],
                      lnT_sb[:, i * TPC + 512 * tq:i * TPC + 512 * (tq + 1)],
                      start=(i == 0), stop=(i == NDC - 1))
             for tq in range(4) for i in range(NDC)][-1].then_inc(mm.h, 1)))
        nov = oev.bump()
        p.ve(lambda e, buf=buf, nmm=nmm, stw=ost_st.n - 16: (
            e.wait_ge(mm.h, nmm),
            (e.wait_ge(ost_st.h, stw) if stw > 0 else None),
            [e.tensor_copy(ost[:, buf * D + 512 * tq:buf * D + 512 * (tq + 1)],
                           psm[tq][:]) for tq in range(4)][-1].then_inc(oev.h, 1)))
        nst = ost_st.bump(16)
        p.gp(lambda e, oc=oc, buf=buf, nov=nov: (
            e.wait_ge(oev.h, nov),
            e.dma_start(out=qkvT_out[128 * oc:128 * (oc + 1), :],
                        in_=ost[:, buf * D:(buf + 1) * D]).then_inc(ost_st.h, 16)))
    p.sy(lambda e, n=ost_st.n: e.wait_ge(ost_st.h, n))


def build_l1comb_prog():
    p = Prog("l1comb")
    qT_in = p.din("qT", [D, TPC], BF16)
    koT_in = p.din("koT", [D, TPC], BF16)
    voT_in = p.din("voT", [D, TPC], BF16)
    kfT_in = p.din("kfT", [D, TPC], BF16)
    vfT_in = p.din("vfT", [D, TPC], BF16)
    msk_in = p.din("msk", [128, 1], F32)
    selh_in = p.din("selh", [16, 16 * 128], BF16)
    wo_in = p.din("wo", [D, D], F32)
    x_in = p.din("x", [TPC, D], F32)
    idn = p.din("idn", [128, 128], BF16)
    x_out = p.dout("xn", [TPC, D], F32)
    emit_l1comb(p, qT_in, koT_in, voT_in, kfT_in, vfT_in, msk_in, selh_in,
                wo_in, x_in, idn, x_out)
    return p.finalize()


def emit_l1comb(p, qT_in, koT_in, voT_in, kfT_in, vfT_in, msk_in, selh_in,
                wo_in, x_in, idn, x_out):

    tp = [p.ps(f"tp{i}", [128, 1024], BF16) for i in range(2)]
    y_ps = [p.ps(f"y{n}", [128, 512], F32) for n in range(4)]
    psa = tp[0].ap()[:, 0:64].bitcast(F32)
    psb = tp[0].ap()[:, 256:512].bitcast(F32)
    psdt = tp[1].ap()[0:16, 0:128]

    idn_sb = p.sb("idn_sb", [128, 128], BF16)
    msk_sb = p.sb("msk_sb", [128, 1], F32)
    selh_sb = p.sb("selh_sb", [16, 16 * 128], BF16)
    ones128 = p.sb("ones128b", [128, 1], BF16)
    ld0 = p.sem()
    n0 = ld0.bump(48)
    p.sy(lambda e: (e.dma_start(out=idn_sb[:], in_=idn).then_inc(ld0.h, 16),
                    e.dma_start(out=msk_sb[:], in_=msk_in).then_inc(ld0.h, 16),
                    e.dma_start(out=selh_sb[:], in_=selh_in).then_inc(ld0.h, 16)))
    vc0 = p.sem()
    nvc0 = vc0.bump()
    p.ve(lambda e, n0=n0: (e.wait_ge(ld0.h, n0),
                           e.memset(ones128[:], 1.0).then_inc(vc0.h, 1)))
    p.te(lambda e, n0=n0: e.wait_ge(ld0.h, n0))
    p.sc(lambda e, n0=n0: e.wait_ge(ld0.h, n0))

    wt_sb = p.sb("wt", [128, NDC * D], BF16)
    wdone = emit_wt_prep(p, wo_in, wt_sb, D, D, idn_sb, "wp", idn_wait=(ld0, n0),
                         tp_pool=tp)
    nw = wdone.n

    qc = p.sb("qc", [128, 2 * D], BF16)
    kfc = p.sb("kfc", [128, 2 * D], BF16)
    koc = p.sb("koc", [128, 2 * D], BF16)
    vfc = p.sb("vfc", [128, 2 * D], BF16)
    voc = p.sb("voc", [128, 2 * D], BF16)
    prod = p.sb("prod", [128, D], BF16)
    a_sb = p.sb("a_sb", [128, 32], F32)
    dsg = p.sb("dsg", [128, 16], F32)
    dsgb = p.sb("dsgb", [128, 16], BF16)
    dT_sb = p.sb("dT_sb", [16, 128], BF16)
    dbc = p.sb("dbc", [128, 128], BF16)
    vd_sb = p.sb("vd_sb", [128, 128], BF16)
    outT = p.sb("outT", [128, 2 * D], BF16)

    cld = p.sem()
    av = p.sem()
    at = p.sem()
    asg = p.sem()
    cmb = p.sem()
    cmb0 = cmb.n
    mm = p.sem()
    pl = PostLN(p, x_in, x_out, "pl")
    free = None

    for j in range(NTT):
        buf = j % 2
        ncl = cld.bump(5 * 16)
        cw = cmb0 + 16 * (j - 1)
        p.sy(lambda e, j=j, buf=buf, ncl=ncl, cw=cw, cmb0=cmb0: (
            (e.wait_ge(cmb.h, cw) if cw > cmb0 else None),
            [e.dma_start(out=dst[:, buf * D:(buf + 1) * D]
                         .rearrange("p (c t) -> p c t", c=NDC),
                         in_=src[:, 128 * j:128 * (j + 1)]
                         .rearrange("(c p) t -> p c t", p=128)).then_inc(cld.h, 16)
             for dst, src in ((qc, qT_in), (kfc, kfT_in), (koc, koT_in),
                              (vfc, vfT_in), (voc, voT_in))][-1]))
        for h in range(16):
            for which, ksrc, col in ((0, kfc, h), (1, koc, 16 + h)):
                nav = av.bump()
                p.ve(lambda e, h=h, buf=buf, ksrc=ksrc, ncl=ncl, nav=nav,
                            atw=at.n: (
                    e.wait_ge(cld.h, ncl),
                    (e.wait_ge(at.h, atw) if atw > 0 else None),
                    e.tensor_mul(prod[:, 0:128],
                                 qc[:, buf * D + 128 * h:buf * D + 128 * (h + 1)],
                                 ksrc[:, buf * D + 128 * h:buf * D + 128 * (h + 1)])
                    .then_inc(av.h, 1)))
                nat = at.bump()
                p.te(lambda e, col=col, nav=nav, nat=nat, j=j, nw=nw: (
                    e.wait_ge(av.h, nav),
                    (e.wait_ge(vc0.h, nvc0) if (j == 0 and col == 0) else None),
                    (e.wait_ge(wdone.h, nw) if (j == 0 and col == 0) else None),
                    e.matmul(psa[:, col:col + 1], prod[:, 0:128], ones128[:],
                             start=True, stop=True).then_inc(at.h, 1)))
        nav1 = av.bump()
        p.ve(lambda e, nat=at.n, nav1=nav1: (
            e.wait_ge(at.h, nat),
            e.tensor_copy(a_sb[:], psa[:]).then_inc(av.h, 1)))
        nav2 = av.bump()
        p.ve(lambda e, nav1=nav1: (
            e.wait_ge(av.h, nav1),
            e.tensor_sub(dsg[:], a_sb[:, 0:16], a_sb[:, 16:32]).then_inc(av.h, 1)))
        nsg = asg.bump()
        p.sc(lambda e, nav2=nav2: (
            e.wait_ge(av.h, nav2),
            e.activation(dsg[:], dsg[:], AF.Sigmoid,
                         scale=float(DH ** -0.5)).then_inc(asg.h, 1)))
        nav3 = av.bump()
        p.ve(lambda e, nsg=nsg, nav3=nav3: (
            e.wait_ge(asg.h, nsg),
            e.tensor_scalar(dsgb[:], dsg[:], msk_sb[:, 0:1], None,
                            OP.mult).then_inc(av.h, 1)))
        nat2 = at.bump()
        p.te(lambda e, nav3=nav3, nat2=nat2: (
            e.wait_ge(av.h, nav3),
            e.transpose(psdt[:], dsgb[:], idn_sb[:]).then_inc(at.h, 1)))
        nav4 = av.bump()
        p.ve(lambda e, nat2=nat2, nav4=nav4: (
            e.wait_ge(at.h, nat2),
            e.tensor_copy(dT_sb[:], psdt[:]).then_inc(av.h, 1)))
        for h in range(16):
            nat3 = at.bump()
            p.te(lambda e, h=h, nav4=nav4, nat3=nat3, cmw=cmb.n: (
                e.wait_ge(av.h, nav4),
                (e.wait_ge(cmb.h, cmw) if cmw > 0 else None),
                e.matmul(psb[:], selh_sb[:, 128 * h:128 * (h + 1)], dT_sb[:],
                         start=True, stop=True).then_inc(at.h, 1)))
            ncm = cmb.bump()
            p.ve(lambda e, h=h, buf=buf, nat3=nat3, ncm=ncm: (
                e.wait_ge(at.h, nat3),
                e.tensor_copy(dbc[:], psb[:]),
                e.tensor_sub(vd_sb[:],
                             vfc[:, buf * D + 128 * h:buf * D + 128 * (h + 1)],
                             voc[:, buf * D + 128 * h:buf * D + 128 * (h + 1)]),
                e.tensor_mul(vd_sb[:], vd_sb[:], dbc[:]),
                e.tensor_add(outT[:, buf * D + 128 * h:buf * D + 128 * (h + 1)],
                             vd_sb[:],
                             voc[:, buf * D + 128 * h:buf * D + 128 * (h + 1)])
                .then_inc(cmb.h, 1)))
        nmm = mm.bump()
        p.te(lambda e, j=j, buf=buf, nmm=nmm, ncmb=cmb.n, nw=nw, free=free: (
            e.wait_ge(cmb.h, ncmb),
            (e.wait_ge(wdone.h, nw) if j == 0 else None),
            (e.wait_ge(free[0].h, free[1]) if free else None),
            [e.matmul(y_ps[n][:], outT[:, buf * D + 128 * c:buf * D + 128 * (c + 1)],
                      wt_sb[:, c * D + 512 * n:c * D + 512 * (n + 1)],
                      start=(c == 0), stop=(c == NDC - 1))
             for c in range(NDC) for n in range(4)][-1].then_inc(mm.h, 1)))
        free = pl.emit(j, [yp[:] for yp in y_ps],
                       [(y_ps[n][:], 512 * n, 512) for n in range(4)], (mm, nmm))
    pl.finish()


def build_attn0_prog(debug=False):
    p = Prog("attn0")
    lnt_in = p.din("lnt", [D, S], BF16)
    wq_in = p.din("wq", [512, D], F32)
    wk_in = p.din("wk", [512, D], F32)
    wv_in = p.din("wv", [512, D], F32)
    cosp_in = p.din("cosp", [128, 2 * S], BF16)
    sinp_in = p.din("sinp", [128, 2 * S], BF16)
    gate_in = p.din("gate", [1, 4], F32)
    idn = p.din("idn", [128, 128], BF16)
    ct_out = p.dout("ct", [512, S], BF16)
    emit_attn0(p, lambda ts: lnt_in[:, 512 * ts:512 * (ts + 1)],
               wq_in, wk_in, wv_in, cosp_in, sinp_in, gate_in, idn, ct_out)
    return p.finalize()


def emit_attn0(p, lnt_tile, wq_in, wk_in, wv_in, cosp_in, sinp_in, gate_in,
               idn, ct_out, debug=False, w_wait=None, lnt_wait=None):
    if debug:
        dbgq = p.dout("dbgq", [128, 2 * S], BF16)
        dbgk = p.dout("dbgk", [128, 2 * S], BF16)
        dbgv = p.dout("dbgv", [128, 2 * S], BF16)

    tp = [p.ps(f"tp{i}", [128, 1024], BF16) for i in range(2)]
    ps0 = p.ps("ps0", [128, 512], F32)
    ps1 = p.ps("ps1", [128, 512], F32)
    ps2 = p.ps("ps2", [128, 512], F32)
    ps3 = p.ps("ps3", [128, 512], F32)
    ps4 = p.ps("ps4", [128, 512], F32)
    ps5 = p.ps("ps5", [128, 512], F32)
    psDn = ps4[0:1, 0:512]
    psLd = ps2[0:1, 0:512]

    idn_sb = p.sb("idn_sb", [128, 128], BF16)
    negc = p.sb("negc", [128, 1], F32)
    ones128 = p.sb("ones128", [128, 1], BF16)
    one1 = p.sb("one1", [1, 128], F32)
    g_sb = p.sb("g_sb", [1, 4], F32)
    omg_sb = p.sb("omg_sb", [1, 4], F32)
    qTr = p.sb("qTr", [128, 2 * S], BF16)
    kTr = p.sb("kTr", [128, 2 * S], BF16)
    vT2 = p.sb("vT2", [128, 2 * S], BF16)
    mem_sb = p.sb("mem_sb", [128, 129], BF16)
    qf_sb = p.sb("qf_sb", [128, 512], BF16)
    qmin_sb = p.sb("qmin_sb", [128, 512], F32)
    rd1 = p.sb("rd1", [1, 512], F32)
    rd2 = p.sb("rd2", [1, 512], F32)
    rdd = p.sb("rdd", [1, 512], F32)
    ab_sb = p.sb("ab_sb", [128, 512], F32)
    bb_sb = p.sb("bb_sb", [128, 512], F32)
    ct_sb = p.sb("ct_sb", [128, 2 * 512], BF16)

    ld0 = p.sem()
    n0 = ld0.bump(16 * 2)
    p.sy(lambda e, w_wait=w_wait: (
        (e.wait_ge(w_wait[0].h, w_wait[1]) if w_wait else None),
        e.dma_start(out=idn_sb[:], in_=idn).then_inc(ld0.h, 16),
        e.dma_start(out=g_sb[:], in_=gate_in).then_inc(ld0.h, 16)))
    csld = p.sem()
    vc0 = p.sem()
    sc0 = p.sem()
    nsc0 = sc0.bump()
    p.sc(lambda e, n0=n0: (e.wait_ge(ld0.h, n0),
                           e.activation(g_sb[:], g_sb[:], AF.Sigmoid).then_inc(sc0.h, 1)))
    nvc0 = vc0.bump()
    p.ve(lambda e, nsc0=nsc0: (
        e.wait_ge(sc0.h, nsc0),
        e.memset(negc[:], -CEXP), e.memset(ones128[:], 1.0),
        e.memset(one1[:], 1.0),
        e.tensor_scalar(omg_sb[:], g_sb[:], -1.0, 1.0, OP.mult, OP.add)
        .then_inc(vc0.h, 1)))
    p.te(lambda e, n0=n0: e.wait_ge(ld0.h, n0))

    gld = p.swdge_sem()
    wtt = p.sem()
    wev = p.sem()
    lnld = p.sem()
    pmm = p.sem()
    pro = p.sem()
    kfs = p.sem()
    kfe = p.sem()
    vtt = p.sem()
    vev = p.sem()
    mmm = p.sem()
    mev = p.sem()
    qfv = p.sem()
    qfe = p.sem()
    fmm = p.sem()
    fpt = p.sem()
    fac = p.sem()
    fpo = p.sem()
    fden = p.sem()
    fmo = p.sem()
    cst = p.swdge_sem()
    pbar = p.sem()
    pbar0 = pbar.n

    SCL = 1.0 / float(np.sqrt(128.0))
    bank_last_exp = [0, 0]
    slot_last_den = [0] * 32

    for pi in range(2):
        sp = p.scope()
        lnt_sb = p.sbs(sp, "lnt_sb%d" % pi, [128, 2 * 8192], BF16)
        wT6 = p.sbs(sp, "wT6%d" % pi, [128, 6 * D], BF16)
        wrow = p.sbs(sp, "wrow%d" % pi, [128, D], BF16)
        rtmp1 = p.sbs(sp, "rtmp1%d" % pi, [128, 512], BF16)
        rtmp2 = p.sbs(sp, "rtmp2%d" % pi, [128, 512], BF16)
        cos_sb = p.sbs(sp, "cos_sb%d" % pi, [128, S], BF16)
        sin_sb = p.sbs(sp, "sin_sb%d" % pi, [128, S], BF16)

        ncs_ld = csld.bump(32)
        p.sy(lambda e, pi=pi, ncs_ld=ncs_ld, pb=pi, cos_sb=cos_sb,
                    sin_sb=sin_sb: (
            (e.wait_ge(pbar.h, pbar0 + pb) if pb > 0 else None),
            e.dma_start(out=cos_sb[:], in_=cosp_in[:, pi * S:(pi + 1) * S]).then_inc(csld.h, 16),
            e.dma_start(out=sin_sb[:], in_=sinp_in[:, pi * S:(pi + 1) * S]).then_inc(csld.h, 16)))

        for si, (w_in, ri) in enumerate(((wq_in, 0), (wq_in, 1), (wk_in, 0),
                                         (wk_in, 1), (wv_in, 0), (wv_in, 1))):
            rowbase = 128 * (2 * pi + ri)
            nld = gld.bump(16)
            p.gp(lambda e, w_in=w_in, rowbase=rowbase, nld=nld, ttw=wtt.n, pb=pi,
                        si=si, w_wait=w_wait: (
                (e.wait_ge(w_wait[0].h, w_wait[1])
                 if (w_wait and pb == 0 and si == 0) else None),
                (e.wait_ge(pbar.h, pbar0 + pb) if (pb > 0 and si == 0) else None),
                (e.wait_ge(wtt.h, ttw) if ttw > 0 else None),
                e.dma_start(out=wrow[:], in_=w_in[rowbase:rowbase + 128, :])
                .then_inc(gld.h, 16)))
            for half in range(2):
                ntt = wtt.bump()
                p.te(lambda e, half=half, nld=nld, ntt=ntt, evw=wev.n - 1: (
                    e.wait_ge(gld.h, nld),
                    (e.wait_ge(wev.h, evw) if evw > 0 else None),
                    [e.transpose(tp[half][:, 128 * t:128 * (t + 1)],
                                 wrow[:, 128 * (8 * half + t):128 * (8 * half + t + 1)],
                                 idn_sb[:]) for t in range(8)][-1].then_inc(wtt.h, 1)))
                nev = wev.bump()
                p.ve(lambda e, si=si, half=half, ntt=ntt: (
                    e.wait_ge(wtt.h, ntt),
                    [e.tensor_copy(wT6[:, si * D + 128 * (8 * half + t):si * D + 128 * (8 * half + t) + 128],
                                   tp[half][:, 128 * t:128 * (t + 1)])
                     for t in range(8)][-1].then_inc(wev.h, 1)))
        nwev_p = wev.n
        psl = (ps0, ps1, ps2, ps3, ps4, ps5)
        for ts in range(16):
            nll = lnld.bump(16)
            pmwar = pmm.n - 1
            p.sy(lambda e, ts=ts, nll=nll, pmwar=pmwar, pb=pi, lnt_wait=lnt_wait: (
                (e.wait_ge(lnt_wait[0].h, lnt_wait[1])
                 if (lnt_wait and pb == 0 and ts == 0) else None),
                (e.wait_ge(pbar.h, pbar0 + pb) if (pb > 0 and ts == 0) else None),
                (e.wait_ge(pmm.h, pmwar) if pmwar > 0 else None),
                e.dma_start(out=lnt_sb.ap()[:, (ts % 2) * 8192:(ts % 2 + 1) * 8192]
                            .rearrange("p (c t) -> p c t", c=16),
                            in_=lnt_tile(ts)
                            .rearrange("(c p) t -> p c t", p=128)).then_inc(lnld.h, 16)))
            nmm = pmm.bump()
            p.te(lambda e, ts=ts, nll=nll, nmm=nmm, nwev=nwev_p, prw=pro.n: (
                e.wait_ge(lnld.h, nll),
                (e.wait_ge(wev.h, nwev) if ts == 0 else None),
                (e.wait_ge(pro.h, prw) if prw > 0 else None),
                [e.matmul(psl[si][0:128, 0:512],
                          wT6[:, si * D + 128 * i:si * D + 128 * (i + 1)],
                          lnt_sb[:, (ts % 2) * 8192 + 512 * i:(ts % 2) * 8192 + 512 * (i + 1)],
                          start=(i == 0), stop=(i == NDC - 1))
                 for si in range(6) for i in range(NDC)][-1].then_inc(pmm.h, 1)))
            npr = pro.bump()
            tsl = slice(512 * ts, 512 * (ts + 1))
            ts2 = slice(S + 512 * ts, S + 512 * (ts + 1))
            p.ve(lambda e, nmm=nmm, npr=npr, tsl=tsl, ts2=ts2, ncs_ld=ncs_ld,
                        cos_sb=cos_sb, sin_sb=sin_sb: (
                e.wait_ge(pmm.h, nmm),
                e.wait_ge(csld.h, ncs_ld),
                e.tensor_mul(rtmp1[:], ps0[0:128, 0:512], cos_sb[:, tsl]),
                e.tensor_mul(rtmp2[:], ps1[0:128, 0:512], sin_sb[:, tsl]),
                e.tensor_sub(qTr.ap()[:, tsl], rtmp1[:], rtmp2[:]),
                e.tensor_mul(rtmp1[:], ps1[0:128, 0:512], cos_sb[:, tsl]),
                e.tensor_mul(rtmp2[:], ps0[0:128, 0:512], sin_sb[:, tsl]),
                e.tensor_add(qTr.ap()[:, ts2], rtmp1[:], rtmp2[:]),
                e.tensor_mul(rtmp1[:], ps2[0:128, 0:512], cos_sb[:, tsl]),
                e.tensor_mul(rtmp2[:], ps3[0:128, 0:512], sin_sb[:, tsl]),
                e.tensor_sub(kTr.ap()[:, tsl], rtmp1[:], rtmp2[:]),
                e.tensor_mul(rtmp1[:], ps3[0:128, 0:512], cos_sb[:, tsl]),
                e.tensor_mul(rtmp2[:], ps2[0:128, 0:512], sin_sb[:, tsl]),
                e.tensor_add(kTr.ap()[:, ts2], rtmp1[:], rtmp2[:]),
                e.tensor_copy(vT2.ap()[:, tsl], ps4[0:128, 0:512]),
                e.tensor_copy(vT2.ap()[:, ts2], ps5[0:128, 0:512]).then_inc(pro.h, 1)))
        npro_p = pro.n
        if debug and pi == 0:
            dbst = p.sem()
            nd = dbst.bump(48)
            p.gp(lambda e, npro=npro_p: (
                e.wait_ge(pro.h, npro),
                e.dma_start(out=dbgq, in_=qTr[:]).then_inc(dbst.h, 16),
                e.dma_start(out=dbgk, in_=kTr[:]).then_inc(dbst.h, 16),
                e.dma_start(out=dbgv, in_=vT2[:]).then_inc(dbst.h, 16)))
            p.sy(lambda e, nd=nd: e.wait_ge(dbst.h, nd))
        sp.close()
        sf = p.scope()
        vaug = p.sbs(sf, "vaug%d" % pi, [128, 64 * 129], BF16)
        kfsm = p.sbs(sf, "kfsm%d" % pi, [128, S], BF16)
        kfblk = p.sbs(sf, "kfblk%d" % pi, [128, 2048], BF16)
        ftmp = p.sbs(sf, "ftmp%d" % pi, [128, 2048], F32)
        pt_sb = p.sbs(sf, "pt%d" % pi, [128, 32 * 512], BF16)
        tpf0 = tp[0].ap()[:, 0:1024].bitcast(F32)
        tpf1 = tp[1].ap()[:, 0:1024].bitcast(F32)

        nvo = vc0.n
        for hh in range(2):
            off = hh * S
            hidx = pi + 2 * hh
            nvz = vev.bump()
            p.ve(lambda e, npro=npro_p, nvz=nvz, facw=fac.n: (
                e.wait_ge(pro.h, npro),
                (e.wait_ge(fac.h, facw) if facw > 0 else None),
                e.memset(vaug.ap().rearrange("p (c w) -> p c w", w=129)[:, :, 128:129],
                         1.0).then_inc(vev.h, 1)))
            for g in range(8):
                ntt = vtt.bump()
                fpw = fpo.n if g == 0 else 0
                p.te(lambda e, g=g, off=off, ntt=ntt, npro=npro_p, evw=vev.n - 1,
                            nvz=nvz, fpw=fpw: (
                    (e.wait_ge(pro.h, npro) if g == 0 else None),
                    (e.wait_ge(fpo.h, fpw) if fpw > 0 else None),
                    (e.wait_ge(vev.h, evw) if evw > 0 else None),
                    [e.transpose(tp[g % 2][:, 128 * t:128 * (t + 1)],
                                 vT2.ap()[:, off + 128 * (8 * g + t):off + 128 * (8 * g + t + 1)],
                                 idn_sb[:]) for t in range(8)][-1].then_inc(vtt.h, 1)))
                nev = vev.bump()
                p.ve(lambda e, g=g, ntt=ntt, nvz=nvz: (
                    e.wait_ge(vtt.h, ntt),
                    e.wait_ge(vev.h, nvz),
                    [e.tensor_copy(vaug[:, 129 * (8 * g + t):129 * (8 * g + t) + 128],
                                   tp[g % 2][:, 128 * t:128 * (t + 1)])
                     for t in range(8)][-1].then_inc(vev.h, 1)))
            for blk in range(4):
                bo = off + 2048 * blk
                nkf1 = kfs.bump()
                p.ve(lambda e, bo=bo, nkf1=nkf1, npro=npro_p, kew=kfe.n: (
                    (e.wait_ge(pro.h, npro) if nkf1 == 1 else None),
                    (e.wait_ge(kfe.h, kew) if kew > 0 else None),
                    e.tensor_scalar(ftmp[:], kTr.ap()[:, bo:bo + 2048], 0.0, None,
                                    OP.min).then_inc(kfs.h, 1)))
                nke = kfe.bump()
                p.sc(lambda e, nkf1=nkf1, nke=nke, kfw=kfs.n: (
                    e.wait_ge(kfs.h, nkf1),
                    (e.wait_ge(kfs.h, kfw) if kfw > 0 else None),
                    e.activation(ftmp[:], ftmp[:], AF.Exp).then_inc(kfe.h, 1)))
                nkf2 = kfs.bump()
                p.ve(lambda e, bo=bo, nke=nke, nkf2=nkf2, ttw=vtt.n: (
                    e.wait_ge(kfe.h, nke),
                    (e.wait_ge(vtt.h, ttw) if ttw > 0 else None),
                    e.scalar_tensor_tensor(kfblk[:], kTr.ap()[:, bo:bo + 2048], 1.0,
                                           ftmp[:], OP.add, OP.max).then_inc(kfs.h, 1)))
                for g in range(2):
                    ntt = vtt.bump()
                    p.te(lambda e, g=g, blk=blk, ntt=ntt, nkf2=nkf2, evw=vev.n - 1: (
                        e.wait_ge(kfs.h, nkf2),
                        (e.wait_ge(vev.h, evw) if evw > 0 else None),
                        [e.transpose(tp[g % 2][:, 128 * t:128 * (t + 1)],
                                     kfblk[:, 128 * (8 * g + t):128 * (8 * g + t + 1)],
                                     idn_sb[:]) for t in range(8)][-1].then_inc(vtt.h, 1)))
                    nev = vev.bump()
                    p.ve(lambda e, g=g, blk=blk, ntt=ntt, mw=mmm.n: (
                        e.wait_ge(vtt.h, ntt),
                        (e.wait_ge(mmm.h, mw) if (mw > 0 and blk == 0 and g == 0) else None),
                        [e.tensor_copy(kfsm[:, 2048 * blk + 128 * (8 * g + t):2048 * blk + 128 * (8 * g + t) + 128],
                                       tp[g % 2][:, 128 * t:128 * (t + 1)])
                         for t in range(8)][-1].then_inc(vev.h, 1)))
            nvev_h = vev.n
            nme = mmm.bump()
            p.te(lambda e, nvev=nvev_h, nme=nme, mevw=mev.n, fpw=fpo.n,
                        vaug=vaug: (
                e.wait_ge(vev.h, nvev),
                (e.wait_ge(mev.h, mevw) if mevw > 0 else None),
                (e.wait_ge(fpo.h, fpw) if fpw > 0 else None),
                [e.matmul(ps5.ap()[0:128, 0:129], kfsm[:, 128 * c:128 * (c + 1)],
                          vaug[:, 129 * c:129 * (c + 1)],
                          start=(c == 0), stop=(c == 63)) for c in range(64)][-1]
                .then_inc(mmm.h, 1)))
            nmv = mev.bump()
            p.ve(lambda e, nme=nme, nmv=nmv: (
                e.wait_ge(mmm.h, nme),
                e.tensor_copy(mem_sb[:], ps5.ap()[0:128, 0:129]).then_inc(mev.h, 1)))
            for qt in range(16):
                nqf1 = qfv.bump()
                tsl = slice(off + 512 * qt, off + 512 * (qt + 1))
                p.ve(lambda e, tsl=tsl, nqf1=nqf1, qew=qfe.n: (
                    (e.wait_ge(qfe.h, qew) if qew > 0 else None),
                    e.tensor_scalar(qmin_sb[:], qTr.ap()[:, tsl], 0.0, None,
                                    OP.min).then_inc(qfv.h, 1)))
                nqe = qfe.bump()
                p.sc(lambda e, nqf1=nqf1, nqe=nqe: (
                    e.wait_ge(qfv.h, nqf1),
                    e.activation(qmin_sb[:], qmin_sb[:], AF.Exp).then_inc(qfe.h, 1)))
                nqf2 = qfv.bump()
                p.ve(lambda e, tsl=tsl, nqe=nqe, nqf2=nqf2, fw=fmo.n: (
                    e.wait_ge(qfe.h, nqe),
                    (e.wait_ge(fmo.h, fw) if fw > 0 else None),
                    e.scalar_tensor_tensor(qf_sb[:], qTr.ap()[:, tsl], 1.0,
                                           qmin_sb[:], OP.add, OP.max)
                    .then_inc(qfv.h, 1)))
                psS = (ps0, ps2)
                for H in range(2):
                    this_half_exp = [0] * 32
                    for s in range(32):
                        kc = 32 * H + s
                        war = bank_last_exp[kc % 2]
                        nfm = fmm.bump()
                        fw = fpo.n if (H == 0 and s == 0) else 0
                        p.te(lambda e, kc=kc, qt=qt, off=off, war=war, nfm=nfm,
                                    fw=fw: (
                            (e.wait_ge(fpo.h, fw) if fw > 0 else None),
                            (e.wait_ge(fpt.h, war) if war > 0 else None),
                            e.matmul(psS[kc % 2][0:128, 0:512],
                                     kTr.ap()[:, off + 128 * kc:off + 128 * (kc + 1)],
                                     qTr.ap()[:, off + 512 * qt:off + 512 * (qt + 1)],
                                     start=True, stop=True).then_inc(fmm.h, 1)))
                        sw = slot_last_den[s]
                        nfp = fpt.bump()
                        p.sc(lambda e, s=s, kcp=kc % 2, nfm=nfm, sw=sw,
                                    pt_sb=pt_sb: (
                            e.wait_ge(fmm.h, nfm),
                            (e.wait_ge(fden.h, sw) if sw > 0 else None),
                            e.activation(pt_sb[:, 512 * s:512 * (s + 1)],
                                         psS[kcp][0:128, 0:512], AF.Exp,
                                         bias=negc[:, 0:1], scale=SCL)
                            .then_inc(fpt.h, 1)))
                        bank_last_exp[kc % 2] = fpt.n
                        this_half_exp[s] = fpt.n
                    psO = ps1 if H == 0 else ps3
                    psD = ps4 if H == 0 else ps5
                    for s in range(32):
                        kc = 32 * H + s
                        xw = this_half_exp[s]
                        if s == 31:
                            nfa = fac.bump()
                            p.te(lambda e, kc=kc, s=s, xw=xw, psO=psO, nfa=nfa,
                                        vaug=vaug, pt_sb=pt_sb: (
                                e.wait_ge(fpt.h, xw),
                                e.matmul(psO[0:128, 0:512],
                                         vaug[:, 129 * kc:129 * kc + 128],
                                         pt_sb[:, 512 * s:512 * (s + 1)],
                                         start=False, stop=True)
                                .then_inc(fac.h, 1)))
                        else:
                            p.te(lambda e, kc=kc, s=s, xw=xw, psO=psO,
                                        vaug=vaug, pt_sb=pt_sb: (
                                e.wait_ge(fpt.h, xw),
                                e.matmul(psO[0:128, 0:512],
                                         vaug[:, 129 * kc:129 * kc + 128],
                                         pt_sb[:, 512 * s:512 * (s + 1)],
                                         start=(s == 0), stop=False)))
                    for s in range(32):
                        nfd = fden.bump()
                        mw = nmv if (qt == 0 and H == 1 and s == 0) else 0
                        p.te(lambda e, s=s, psD=psD, nfd=nfd, mw=mw,
                                    pt_sb=pt_sb: (
                            (e.wait_ge(mev.h, mw) if mw > 0 else None),
                            e.matmul(psD[0:1, 0:512], ones128[:],
                                     pt_sb[:, 512 * s:512 * (s + 1)],
                                     start=(s == 0), stop=(s == 31))
                            .then_inc(fden.h, 1)))
                        slot_last_den[s] = fden.n
                nmo = fmo.bump()
                p.te(lambda e, nmo=nmo, qv=qfv.n, mw=nmv, q0=(qt == 0),
                            vw=nvev_h: (
                    e.wait_ge(qfv.h, qv),
                    (e.wait_ge(mev.h, mw) if q0 else None),
                    (e.wait_ge(vev.h, vw) if q0 else None),
                    e.matmul(tpf0[0:128, 0:512], mem_sb[:, 0:128], qf_sb[:],
                             start=True, stop=True),
                    e.matmul(tpf1[0:1, 0:512], mem_sb[:, 128:129], qf_sb[:],
                             start=True, stop=True).then_inc(fmo.h, 1)))
                nfo1 = fpo.bump()
                p.ve(lambda e, hidx=hidx, nfa=fac.n, nfd=fden.n, nmo=fmo.n,
                            nvo=nvo: (
                    e.wait_ge(fac.h, nfa),
                    e.wait_ge(fden.h, nfd),
                    e.wait_ge(fmo.h, nmo),
                    e.wait_ge(vc0.h, nvo),
                    e.tensor_copy(ab_sb[:], ps1[0:128, 0:512]),
                    e.tensor_add(ab_sb[:], ab_sb[:], ps3[0:128, 0:512]),
                    e.tensor_copy(bb_sb[:], tpf0[0:128, 0:512]),
                    e.tensor_copy(rdd[:], ps4[0:1, 0:512]),
                    e.tensor_add(rdd[:], rdd[:], ps5[0:1, 0:512]),
                    e.reciprocal(rd1[:], tpf1[0:1, 0:512]),
                    e.reciprocal(rd2[:], rdd[:]),
                    e.tensor_scalar(rd1[:], rd1[:], g_sb[0:1, hidx:hidx + 1], None,
                                    OP.mult),
                    e.tensor_scalar(rd2[:], rd2[:], omg_sb[0:1, hidx:hidx + 1], None,
                                    OP.mult).then_inc(fpo.h, 1)))
                nfo2 = fpo.bump()
                p.te(lambda e, nfo1=nfo1, xw=fpt.n: (
                    e.wait_ge(fpo.h, nfo1),
                    e.wait_ge(fpt.h, xw),
                    e.matmul(ps2[0:128, 0:512], one1[:], rd1[:], start=True,
                             stop=True).then_inc(fpo.h, 1)))
                nfo3 = fpo.bump()
                p.ve(lambda e, nfo2=nfo2: (
                    e.wait_ge(fpo.h, nfo2),
                    e.tensor_mul(bb_sb[:], bb_sb[:], ps2[0:128, 0:512])
                    .then_inc(fpo.h, 1)))
                nfo4 = fpo.bump()
                p.te(lambda e, nfo3=nfo3: (
                    e.wait_ge(fpo.h, nfo3),
                    e.matmul(ps2[0:128, 0:512], one1[:], rd2[:], start=True,
                             stop=True).then_inc(fpo.h, 1)))
                nfo5 = fpo.bump()
                cbuf = qt % 2
                p.ve(lambda e, nfo4=nfo4, cbuf=cbuf, stw=cst.n - 16: (
                    e.wait_ge(fpo.h, nfo4),
                    (e.wait_ge(cst.h, stw) if stw > 0 else None),
                    e.tensor_mul(ab_sb[:], ab_sb[:], ps2[0:128, 0:512]),
                    e.tensor_add(ct_sb[:, cbuf * 512:(cbuf + 1) * 512], bb_sb[:],
                                 ab_sb[:]).then_inc(fpo.h, 1)))
                ncs = cst.bump(16)
                p.gp(lambda e, hidx=hidx, qt=qt, cbuf=cbuf, nfo5=nfo5: (
                    e.wait_ge(fpo.h, nfo5),
                    e.dma_start(out=ct_out[128 * hidx:128 * (hidx + 1),
                                           512 * qt:512 * (qt + 1)],
                                in_=ct_sb[:, cbuf * 512:(cbuf + 1) * 512])
                    .then_inc(cst.h, 16)))
        sf.close()
        npb = pbar.bump()
        p.ve(lambda e, nfo=fpo.n, npb=npb: (
            e.wait_ge(fpo.h, nfo), e.engine_nop().then_inc(pbar.h, 1)))
    p.sy(lambda e, n=cst.n: e.wait_ge(cst.h, n))


def build_attn2_prog(debug=False, stages=4, nheads=4, nchunks=64, flash_stages=7):
    p = Prog("attn2")
    lnt_in = p.din("lnt", [D, S], BF16)
    wq_in = p.din("wq", [512, D], F32)
    wk_in = p.din("wk", [512, D], F32)
    wv_in = p.din("wv", [512, D], F32)
    wg_in = p.din("wg", [512, D], F32)
    ws1_in = p.din("ws1", [16, D], F32)
    ws2_in = p.din("ws2", [256, 16], F32)
    bs2_in = p.din("bs2p", [128, 2], F32)
    gnw_in = p.din("gnw", [128, 1], F32)
    cos_in = p.din("cosr", [128, S], BF16)
    sin_in = p.din("sinr", [128, S], BF16)
    pmat_in = p.din("pmat", [128, 128], BF16)
    cmaskA_in = p.din("cmaskA", [128, 128], BF16)
    cmaskW_in = p.din("cmaskW", [128, 128], BF16)
    idn = p.din("idn", [128, 128], BF16)
    idn32_in = p.din("idn32", [128, 128], F32)
    ct_out = p.dout("ct", [512, S], BF16)
    emit_attn2(p, lambda ts: lnt_in[:, 512 * ts:512 * (ts + 1)],
               wq_in, wk_in, wv_in, wg_in, ws1_in, ws2_in, bs2_in, gnw_in,
               cos_in, sin_in, pmat_in, cmaskA_in, cmaskW_in, idn, idn32_in,
               ct_out, debug=debug, stages=stages, nheads=nheads,
               nchunks=nchunks, flash_stages=flash_stages)
    return p.finalize()


def emit_attn2(p, lnt_tile, wq_in, wk_in, wv_in, wg_in, ws1_in, ws2_in,
               bs2_in, gnw_in, cos_in, sin_in, pmat_in, cmaskA_in, cmaskW_in,
               idn, idn32_in, ct_out, debug=False, stages=4, nheads=4,
               nchunks=64, w_wait=None, lnt_wait=None, flash_stages=7):
    if debug:
        dbg_es = p.dout("dbg_es", [128, 2 * S], BF16)
        dbg_enz = p.dout("dbg_enz", [128, 2 * S], BF16)
        dbg_q = p.dout("dbg_q", [128, S], BF16)
        dbg_k = p.dout("dbg_k", [128, S], BF16)
        dbg_cb = p.dout("dbg_cb", [128, 512], BF16)
        dbg_cf = p.dout("dbg_cf", [128, 66], F32)
        dbg_e = p.dout("dbg_e", [128, 2], F32)
        dbg_cf2 = p.dout("dbg_cf2", [128, 128], BF16)

    tp = [p.ps(f"tp{i}", [128, 1024], BF16) for i in range(2)]
    f = [p.ps(f"f{i}", [128, 512], F32) for i in range(6)]

    idn_sb = p.sb("idn_sb", [128, 128], BF16)
    idn32_sb = p.sb("idn32_sb", [128, 128], F32)
    pmat_sb = p.sb("pmat_sb[:]", [128, 128], BF16)
    cmA_sb = p.sb("cmA_sb", [128, 128], BF16)
    cmW_sb = p.sb("cmW_sb", [128, 128], BF16)
    gnw_sb = p.sb("gnw_sb", [128, 1], F32)
    bs2_sb = p.sb("bs2_sb", [128, 2], F32)
    negc = p.sb("negc", [128, 1], F32)
    ones128 = p.sb("ones128", [128, 1], BF16)
    one1 = p.sb("one1", [1, 128], F32)
    cos_sb = p.sb("cos_sb", [128, S], BF16)
    sin_sb = p.sb("sin_sb", [128, S], BF16)
    esT2 = p.sb("esT2", [128, 2 * S], BF16)
    enzT2 = p.sb("enzT2", [128, 2 * S], BF16)
    qT_h = p.sb("qT_h", [128, S], BF16)
    kT_h = p.sb("kT_h", [128, S], BF16)
    vT_h = p.sb("vT_h", [128, S], BF16)
    gT_h = p.sb("gT_h", [128, S], BF16)
    wT4 = p.sb("wT4", [128, 4 * D], BF16)
    wrow = p.sb("wrow", [128, 1024], BF16)
    lnt_sb = p.sb("lnt_sb", [128, 8192], BF16)
    ws1T = p.sb("ws1T", [128, 256], BF16)
    ws2T = p.sb("ws2T", [16, 256], BF16)
    z1t = p.sb("z1t", [16, 512], BF16)
    U = p.sb("U", [128, 2048], BF16)
    qraw = U.ap()[:, 1024:1536]
    rtmp1 = U.ap()[:, 0:512]
    rtmp2 = U.ap()[:, 512:1024]
    ok_sb = p.sb("ok_sb", [128, 2 * 64], F32)
    eq_sb = p.sb("eq_sb", [128, 2 * 64], BF16)
    den_sb = p.sb("den_sb", [128, 2], F32)
    p2_sb = p.sb("p2_sb", [128, 2 * 64], BF16)
    Mst_f = p.sb("Mst_f", [128, 64], F32)
    MstB2 = p.sb("MstB2", [128, 3 * 64], BF16)
    Sst_f = p.sb("Sst_f", [64, 128], F32)
    SstB2 = p.sb("SstB2", [64, 3 * 128], BF16)
    sq_sb = p.sb("sq_sb", [128, 2 * 128], BF16)
    sg2_sb = p.sb("sg2_sb", [128, 2 * 128], BF16)
    f3c_sb = p.sb("f3c_sb", [128, 2 * 128], BF16)
    e1_sb = p.sb("e1_sb", [128, 1], F32)
    e2_sb = p.sb("e2_sb", [128, 2], F32)
    e3_sb = p.sb("e3_sb", [128, 1], F32)
    rstd_sb = p.sb("rstd_sb", [128, 1], F32)
    rbc_sb = U.ap()[:, 1792:1920]
    ct_sb = p.sb("ct_sb", [128, 256], BF16)
    dbge_sb = ct_sb.ap().bitcast(F32)[:, 0:2] if debug else None

    erow_sb = z1t.ap()[0:1, 0:256].bitcast(F32)

    ld0 = p.sem()
    n0 = ld0.bump(16 * 9)
    p.sy(lambda e, w_wait=w_wait: (
        (e.wait_ge(w_wait[0].h, w_wait[1]) if w_wait else None),
        e.dma_start(out=idn_sb[:], in_=idn).then_inc(ld0.h, 16),
        e.dma_start(out=idn32_sb[:], in_=idn32_in).then_inc(ld0.h, 16),
        e.dma_start(out=pmat_sb[:], in_=pmat_in).then_inc(ld0.h, 16),
        e.dma_start(out=cmA_sb[:], in_=cmaskA_in).then_inc(ld0.h, 16),
        e.dma_start(out=cmW_sb[:], in_=cmaskW_in).then_inc(ld0.h, 16),
        e.dma_start(out=gnw_sb[:], in_=gnw_in).then_inc(ld0.h, 16),
        e.dma_start(out=bs2_sb[:], in_=bs2_in).then_inc(ld0.h, 16),
        e.dma_start(out=cos_sb[:], in_=cos_in).then_inc(ld0.h, 16),
        e.dma_start(out=sin_sb[:], in_=sin_in).then_inc(ld0.h, 16)))
    ld0g = p.swdge_sem()
    ng1 = ld0g.bump(16)
    p.gp(lambda e, w_wait=w_wait: (
        (e.wait_ge(w_wait[0].h, w_wait[1]) if w_wait else None),
        e.dma_start(out=lnt_sb[0:16, 0:D], in_=ws1_in).then_inc(ld0g.h, 16)))
    vc0 = p.sem()
    nvc0 = vc0.bump()
    p.ve(lambda e, n0=n0: (
        e.wait_ge(ld0.h, n0),
        e.memset(negc[:], -CEXP), e.memset(ones128[:], 1.0),
        e.memset(one1[:], 1.0), e.memset(qT_h[:], 1.0).then_inc(vc0.h, 1)))
    p.te(lambda e, n0=n0, nvc0=nvc0: (e.wait_ge(ld0.h, n0),
                                      e.wait_ge(vc0.h, nvc0)))
    p.sc(lambda e, nvc0=nvc0: e.wait_ge(vc0.h, nvc0))

    stt = p.sem()
    sve = p.sem()
    ssc = p.sem()
    lnld = p.sem()
    vch = Chain(p, "vch")

    n_t1 = stt.bump()
    p.te(lambda e, n_t1=n_t1, ng1=ng1: (
        e.wait_ge(ld0g.h, ng1),
        [e.transpose(tp[0].ap()[0:128, 16 * c:16 * (c + 1)],
                     lnt_sb[0:16, 128 * c:128 * (c + 1)], idn_sb[0:16, 0:16])
         for c in range(16)][-1].then_inc(stt.h, 1)))
    nws2 = ld0g.bump(32)
    p.gp(lambda e, n_t1=n_t1: (
        e.wait_ge(stt.h, n_t1),
        e.dma_start(out=wrow[0:128, 0:16], in_=ws2_in[0:128, :]).then_inc(ld0g.h, 16),
        e.dma_start(out=wrow[0:128, 16:32], in_=ws2_in[128:256, :]).then_inc(ld0g.h, 16)))
    n_v1 = sve.bump()
    p.ve(lambda e, n_t1=n_t1: (
        e.wait_ge(stt.h, n_t1),
        e.tensor_copy(ws1T[:, 0:256], tp[0].ap()[0:128, 0:256]).then_inc(sve.h, 1)))
    n_t2 = stt.bump()
    p.te(lambda e, n_v1=n_v1, n_t2=n_t2, nws2=nws2: (
        e.wait_ge(sve.h, n_v1),
        e.wait_ge(ld0g.h, nws2),
        e.transpose(tp[0].ap()[0:16, 0:128], wrow[0:128, 0:16], idn_sb[:]),
        e.transpose(tp[0].ap()[0:16, 128:256], wrow[0:128, 16:32], idn_sb[:])
        .then_inc(stt.h, 1)))
    n_v2 = sve.bump()
    p.ve(lambda e, n_t2=n_t2: (
        e.wait_ge(stt.h, n_t2),
        e.tensor_copy(ws2T[0:16, 0:256], tp[0].ap()[0:16, 0:256]).then_inc(sve.h, 1)))

    for ts in range(16):
        nll = lnld.bump(16)
        mmwar = stt.n - 1
        p.sy(lambda e, ts=ts, nll=nll, mmwar=mmwar, lnt_wait=lnt_wait,
                    nws1t=n_t1: (
            (e.wait_ge(lnt_wait[0].h, lnt_wait[1])
             if (lnt_wait and ts == 0) else None),
            (e.wait_ge(stt.h, nws1t) if ts == 0 else None),
            (e.wait_ge(stt.h, mmwar) if ts > 0 else None),
            e.dma_start(out=lnt_sb.ap().rearrange("p (c t) -> p c t", c=16),
                        in_=lnt_tile(ts)
                        .rearrange("(c p) t -> p c t", p=128)).then_inc(lnld.h, 16)))
        n_tz = stt.bump()
        p.te(lambda e, ts=ts, nll=nll, n_tz=n_tz, n_v2=n_v2, vw=sve.n,
                    scw=ssc.n: (
            e.wait_ge(lnld.h, nll),
            (e.wait_ge(sve.h, n_v2) if ts == 0 else None),
            (e.wait_ge(sve.h, vw) if ts > 0 else None),
            (e.wait_ge(ssc.h, scw) if scw > 0 else None),
            [e.matmul(f[5][0:16, 0:512], ws1T[:, 16 * c:16 * (c + 1)],
                      lnt_sb[:, 512 * c:512 * (c + 1)],
                      start=(c == 0), stop=(c == 15)) for c in range(16)][-1]
            .then_inc(stt.h, 1)))
        n_vz = sve.bump()
        p.ve(lambda e, n_tz=n_tz: (
            e.wait_ge(stt.h, n_tz),
            e.tensor_copy(z1t[0:16, :], f[5][0:16, 0:512]).then_inc(sve.h, 1)))
        for mt in range(2):
            n_ts = stt.bump()
            p.te(lambda e, mt=mt, n_vz=n_vz, n_ts=n_ts, vw=sve.n, scw=ssc.n: (
                e.wait_ge(sve.h, vw),
                (e.wait_ge(ssc.h, scw) if scw > 0 else None),
                e.matmul(f[5][0:128, 0:512], ws2T[0:16, 128 * mt:128 * (mt + 1)],
                         z1t[0:16, :], start=True, stop=True).then_inc(stt.h, 1)))
            n_vs = sve.bump()
            p.ve(lambda e, mt=mt, n_ts=n_ts, scw=ssc.n: (
                e.wait_ge(stt.h, n_ts),
                (e.wait_ge(ssc.h, scw) if scw > 0 else None),
                vch.step(e, e.tensor_scalar(f[5][0:128, 0:512], f[5][0:128, 0:512],
                                            bs2_sb[:, mt:mt + 1], 32.0, OP.add, OP.min)),
                e.tensor_scalar(f[5][0:128, 0:512], f[5][0:128, 0:512], -32.0,
                                None, OP.max).then_inc(sve.h, 1)))
            n_ss = ssc.bump()
            p.sc(lambda e, mt=mt, ts=ts, vw=sve.n: (
                e.wait_ge(sve.h, vw),
                e.activation(esT2.ap()[:, mt * S + 512 * ts:mt * S + 512 * (ts + 1)],
                             f[5][0:128, 0:512], AF.Exp).then_inc(ssc.h, 1)))
    nes = ssc.n
    nzv = 0
    for mt in range(2):
        n_v = sve.bump()
        p.ve(lambda e, mt=mt, nes=nes, vw=sve.n - 1: (
            (e.wait_ge(ssc.h, nes) if mt == 0 else None),
            (e.wait_ge(sve.h, vw) if mt == 1 else None),
            e.tensor_tensor_scan(kT_h[:], qT_h[:],
                                 esT2.ap()[:, mt * S:(mt + 1) * S], 0.0,
                                 OP.mult, OP.add).then_inc(sve.h, 1)))
        n_v2b = sve.bump()

        def _recip(e, mt=mt, n_v=n_v):
            e.wait_ge(sve.h, n_v)
            with p.nc.allow_low_precision(reason="enz = 1/Z in bf16 (multiplicative)"):
                e.reciprocal(enzT2.ap()[:, mt * S:(mt + 1) * S],
                             kT_h[:]).then_inc(sve.h, 1)

        p.ve(_recip)
        nzv = n_v2b
    if debug:
        dbst = p.sem()
        nd = dbst.bump(32)
        p.gp(lambda e, nzv=nzv: (
            e.wait_ge(sve.h, nzv),
            e.dma_start(out=dbg_es, in_=esT2[:]).then_inc(dbst.h, 16),
            e.dma_start(out=dbg_enz, in_=enzT2[:]).then_inc(dbst.h, 16)))
        p.sy(lambda e, nd=nd: e.wait_ge(dbst.h, nd))

    gld = p.swdge_sem()
    wtt = p.sem()
    wev = p.sem()
    pmm = p.sem()
    pro = p.sem()
    ctt = p.sem()
    cve = p.sem()
    csc = p.sem()
    cst = p.swdge_sem()
    hbar = p.sem()
    hbar0 = hbar.n
    SCL = 1.0 / float(np.sqrt(128.0))

    if stages <= 1:
        p.sy(lambda e, n=sve.n: e.wait_ge(sve.h, n))
        return
    ep_state = [0, 0]
    last_s2 = 0
    n_stt_sph = stt.n
    for hi in range(nheads):
        mt, half = hi % 2, hi // 2
        for wi, w_in in enumerate((wq_in, wk_in, wv_in, wg_in)):
            for hf in range(2):
                nld = gld.bump(16)
                p.gp(lambda e, w_in=w_in, hi=hi, wi=wi, hf=hf, nld=nld,
                            ttw=wtt.n, n_t2=n_t2: (
                    (e.wait_ge(stt.h, n_t2) if (hi == 0 and wi == 0 and hf == 0) else None),
                    (e.wait_ge(wtt.h, ttw) if ttw > 0 else None),
                    e.dma_start(out=wrow[:],
                                in_=w_in[128 * hi:128 * (hi + 1),
                                         1024 * hf:1024 * (hf + 1)])
                    .then_inc(gld.h, 16)))
                ntt = wtt.bump()
                p.te(lambda e, hf=hf, nld=nld, ntt=ntt, evw=wev.n - 1: (
                    e.wait_ge(gld.h, nld),
                    (e.wait_ge(wev.h, evw) if evw > 0 else None),
                    [e.transpose(tp[hf][:, 128 * t:128 * (t + 1)],
                                 wrow[:, 128 * t:128 * (t + 1)],
                                 idn_sb[:]) for t in range(8)][-1].then_inc(wtt.h, 1)))
                nev = wev.bump()
                p.ve(lambda e, wi=wi, hf=hf, ntt=ntt, hb=hi, pmw=pmm.n,
                            nzv=nzv: (
                    e.wait_ge(wtt.h, ntt),
                    (e.wait_ge(hbar.h, hbar0 + hb) if (hb > 0 and wi == 0 and hf == 0) else None),
                    (e.wait_ge(sve.h, nzv) if (hb == 0 and wi == 0 and hf == 0) else None),
                    (e.wait_ge(pmm.h, pmw) if pmw > 0 else None),
                    [e.tensor_copy(wT4[:, wi * D + 128 * (8 * hf + t):wi * D + 128 * (8 * hf + t) + 128],
                                   tp[hf][:, 128 * t:128 * (t + 1)])
                     for t in range(8)][-1].then_inc(wev.h, 1)))
        nwev_h = wev.n
        for ts in range(16):
            nll = lnld.bump(16)
            pmwar = pmm.n
            p.sy(lambda e, ts=ts, nll=nll, pmwar=pmwar, hi=hi,
                        n_stt_sph=n_stt_sph: (
                (e.wait_ge(stt.h, n_stt_sph)
                 if (hi == 0 and ts == 0 and n_stt_sph > 0) else None),
                (e.wait_ge(pmm.h, pmwar) if pmwar > 0 else None),
                e.dma_start(out=lnt_sb.ap().rearrange("p (c t) -> p c t", c=16),
                            in_=lnt_tile(ts)
                            .rearrange("(c p) t -> p c t", p=128)).then_inc(lnld.h, 16)))
            nmm = pmm.bump()
            p.te(lambda e, ts=ts, nll=nll, nmm=nmm, nwev=nwev_h, prw=pro.n,
                        nzv=nzv, hi=hi: (
                e.wait_ge(lnld.h, nll),
                (e.wait_ge(wev.h, nwev) if ts == 0 else None),
                (e.wait_ge(sve.h, nzv) if (hi == 0 and ts == 0) else None),
                (e.wait_ge(pro.h, prw) if prw > 0 else None),
                [e.matmul(f[wi][0:128, 0:512],
                          wT4[:, wi * D + 128 * i:wi * D + 128 * (i + 1)],
                          lnt_sb[:, 512 * i:512 * (i + 1)],
                          start=(i == 0), stop=(i == NDC - 1))
                 for wi in range(4) for i in range(NDC)][-1].then_inc(pmm.h, 1)))
            npr1 = pro.bump()
            p.ve(lambda e, nmm=nmm, npr1=npr1: (
                e.wait_ge(pmm.h, nmm),
                e.tensor_copy(qraw[:], f[0][0:128, 0:512]).then_inc(pro.h, 1)))
            ntsw1 = pro.bump()
            p.te(lambda e, npr1=npr1: (
                e.wait_ge(pro.h, npr1),
                e.matmul(f[4][0:128, 0:512], pmat_sb[:], qraw[:],
                         start=True, stop=True).then_inc(pro.h, 1)))
            npr2 = pro.bump()
            tsl = slice(512 * ts, 512 * (ts + 1))
            p.ve(lambda e, ntsw1=ntsw1, npr2=npr2, tsl=tsl: (
                e.wait_ge(pro.h, ntsw1),
                e.tensor_mul(rtmp1, f[0][0:128, 0:512], cos_sb[:, tsl]),
                vch.step(e, e.tensor_mul(rtmp2, f[4][0:128, 0:512], sin_sb[:, tsl])),
                e.tensor_add(qT_h.ap()[:, tsl], rtmp1, rtmp2),
                e.tensor_copy(qraw[:], f[1][0:128, 0:512]).then_inc(pro.h, 1)))
            ntsw2 = pro.bump()
            p.te(lambda e, npr2=npr2: (
                e.wait_ge(pro.h, npr2),
                e.matmul(f[4][0:128, 0:512], pmat_sb[:], qraw[:],
                         start=True, stop=True).then_inc(pro.h, 1)))
            npr3 = pro.bump()
            p.ve(lambda e, ntsw2=ntsw2, npr3=npr3, tsl=tsl: (
                e.wait_ge(pro.h, ntsw2),
                e.tensor_mul(rtmp1, f[1][0:128, 0:512], cos_sb[:, tsl]),
                vch.step(e, e.tensor_mul(rtmp2, f[4][0:128, 0:512], sin_sb[:, tsl])),
                e.tensor_add(kT_h.ap()[:, tsl], rtmp1, rtmp2),
                e.tensor_copy(vT_h.ap()[:, tsl], f[2][0:128, 0:512]),
                e.tensor_copy(gT_h.ap()[:, tsl], f[3][0:128, 0:512]).then_inc(pro.h, 1)))
        npro_h = pro.n
        if debug and hi == 0:
            dbs2 = p.sem()
            nd2 = dbs2.bump(32)
            p.gp(lambda e, npro=npro_h: (
                e.wait_ge(pro.h, npro),
                e.dma_start(out=dbg_q, in_=qT_h[:]).then_inc(dbs2.h, 16),
                e.dma_start(out=dbg_k, in_=kT_h[:]).then_inc(dbs2.h, 16)))
            p.sy(lambda e, nd2=nd2: e.wait_ge(dbs2.h, nd2))
        if stages <= 2:
            p.sy(lambda e, n=pro.n: e.wait_ge(pro.h, n))
            return
        nst0 = cve.bump()
        p.ve(lambda e, npro=npro_h, nst0=nst0: (
            e.wait_ge(pro.h, npro),
            e.memset(Mst_f[:], 0.0), e.memset(Sst_f[:], 0.0),
            e.memset(MstB2[:], 0.0), e.memset(SstB2[:], 0.0).then_inc(cve.h, 1)))
        stx = {k: {} for k in ("v1", "v2", "v4", "v5", "v7", "f3c", "s2",
                               "s1", "t4", "t6b")}

        def _reg(P):
            o = 896 * P
            return {"at": U.ap()[:, o:o + 128],
                    "esc": U.ap()[:, o + 128:o + 192],
                    "enzc": U.ap()[:, o + 192:o + 256],
                    "vc": U.ap()[:, o + 256:o + 384],
                    "kc": U.ap()[:, o + 384:o + 512],
                    "esm": U.ap()[0:64, o + 512:o + 640],
                    "p2T": U.ap()[0:64, o + 640:o + 768],
                    "w2t": U.ap()[:, o + 768:o + 896]}

        def _bank(P):
            return (f[P][0:128, 0:128], f[P][0:128, 256:320],
                    f[2 + P][0:128, 0:128], f[2 + P][0:128, 256:384])

        def stage_P(c):
            P = c % 2
            r = _reg(P)
            bA = _bank(P)[0]
            csl = slice(128 * c, 128 * (c + 1))
            msl = slice(mt * S + 128 * c, mt * S + 128 * (c + 1))
            prow = slice(64 * half, 64 * (half + 1))
            idn64 = idn_sb[64:128, 64:128] if half else idn_sb[0:64, 0:64]
            tpP = tp[P]
            n_t1 = ctt.bump()
            vw = stx["v5"].get(c - 2, 0)
            p.te(lambda e, csl=csl, msl=msl, prow=prow, vw=vw, idn64=idn64,
                        tpP=tpP: (
                (e.wait_ge(cve.h, vw) if vw > 0 else None),
                e.transpose(tpP.ap()[0:128, 0:64], esT2.ap()[prow, msl], idn64),
                e.transpose(tpP.ap()[0:128, 64:128], enzT2.ap()[prow, msl], idn64),
                e.transpose(tpP.ap()[0:128, 128:256], vT_h.ap()[:, csl], idn_sb[:]),
                e.transpose(tpP.ap()[0:128, 256:384], kT_h.ap()[:, csl], idn_sb[:])
                .then_inc(ctt.h, 1)))
            n_v1 = cve.bump()
            p.ve(lambda e, n_t1=ctt.n, r=r, tpP=tpP: (
                e.wait_ge(ctt.h, n_t1),
                e.tensor_copy(r["esc"], tpP.ap()[0:128, 0:64]),
                e.tensor_copy(r["enzc"], tpP.ap()[0:128, 64:128]),
                e.tensor_copy(r["vc"], tpP.ap()[0:128, 128:256]),
                e.tensor_copy(r["kc"], tpP.ap()[0:128, 256:384]).then_inc(cve.h, 1)))
            stx["v1"][c] = cve.n
            n_t1b = ctt.bump()
            p.te(lambda e, n_v1=n_v1, r=r, tpP=tpP: (
                e.wait_ge(cve.h, n_v1),
                e.transpose(tpP.ap()[0:64, 384:512], r["esc"], idn_sb[:])
                .then_inc(ctt.h, 1)))
            n_v1b = cve.bump()
            p.ve(lambda e, n_t1b=n_t1b, r=r, tpP=tpP: (
                e.wait_ge(ctt.h, n_t1b),
                e.tensor_copy(r["esm"], tpP.ap()[0:64, 384:512]).then_inc(cve.h, 1)))
            n_t2 = ctt.bump()
            wf = stx["f3c"].get(c - 2, 0)
            ws = stx["s2"].get(c - 2, 0)
            p.te(lambda e, csl=csl, bA=bA, wf=wf, ws=ws: (
                (e.wait_ge(cve.h, wf) if wf > 0 else None),
                (e.wait_ge(csc.h, ws) if ws > 0 else None),
                e.matmul(bA, kT_h.ap()[:, csl], qT_h.ap()[:, csl],
                         start=True, stop=True).then_inc(ctt.h, 1)))
            n_v2 = cve.bump()
            p.ve(lambda e, n_t2=n_t2, r=r, bA=bA: (
                e.wait_ge(ctt.h, n_t2),
                e.tensor_mul(r["at"], bA, cmA_sb[:]).then_inc(cve.h, 1)))
            stx["v2"][c] = cve.n
            n_t6a = ctt.bump()
            p.te(lambda e, nv1=stx["v1"][c], r=r: (
                e.wait_ge(cve.h, nv1),
                e.matmul(f[4][0:128, 0:64], r["kc"], r["esc"],
                         start=True, stop=True),
                e.matmul(f[4].ap()[0:64, 128:256], r["esc"], r["vc"],
                         start=True, stop=True).then_inc(ctt.h, 1)))
            slot = (c + 1) % 3
            n_v7 = cve.bump()
            p.ve(lambda e, n_t6a=n_t6a, slot=slot: (
                e.wait_ge(ctt.h, n_t6a),
                vch.step(e, e.tensor_add(Mst_f[:], Mst_f[:], f[4][0:128, 0:64])),
                e.tensor_scalar(MstB2[:, 64 * slot:64 * (slot + 1)], Mst_f[:],
                                SCL, None, OP.mult),
                vch.step(e, e.tensor_add(Sst_f[:], Sst_f[:],
                                         f[4].ap()[0:64, 128:256])),
                e.tensor_copy(SstB2[0:64, 128 * slot:128 * (slot + 1)],
                              Sst_f[:]).then_inc(cve.h, 1)))
            stx["v7"][c] = cve.n

        def stage_X1(c):
            P = c % 2
            r = _reg(P)
            _, bX, _, _ = _bank(P)
            csl = slice(128 * c, 128 * (c + 1))
            slot = c % 3
            okP = ok_sb[:, 64 * P:64 * (P + 1)]
            eqP = eq_sb[:, 64 * P:64 * (P + 1)]
            denP = den_sb[:, P:P + 1]
            e2P = e2_sb[:, P:P + 1]
            p2P = p2_sb[:, 64 * P:64 * (P + 1)]
            n_t3 = ctt.bump()
            p.te(lambda e, csl=csl, bX=bX, r=r, nv2=stx["v2"][c], slot=slot: (
                e.wait_ge(cve.h, nv2),
                e.matmul(bX, r["at"], r["esc"], start=True, stop=False),
                e.matmul(bX, qT_h.ap()[:, csl],
                         MstB2[:, 64 * slot:64 * (slot + 1)],
                         start=False, stop=True).then_inc(ctt.h, 1)))
            n_v3 = cve.bump()
            ws1 = stx["s1"].get(c - 2, 0)
            p.ve(lambda e, n_t3=n_t3, okP=okP, bX=bX, r=r, ws1=ws1: (
                e.wait_ge(ctt.h, n_t3),
                (e.wait_ge(csc.h, ws1) if ws1 > 0 else None),
                e.tensor_mul(okP, bX, r["enzc"]).then_inc(cve.h, 1)))
            n_s1 = csc.bump()
            p.sc(lambda e, n_v3=n_v3, okP=okP, eqP=eqP: (
                e.wait_ge(cve.h, n_v3),
                e.activation(eqP, okP, AF.Exp, bias=negc[:, 0:1])
                .then_inc(csc.h, 1)))
            stx["s1"][c] = csc.n
            n_v4a = cve.bump()
            p.ve(lambda e, n_s1=n_s1, denP=denP, eqP=eqP: (
                e.wait_ge(csc.h, n_s1),
                e.reduce_sum(denP, eqP, AX.X).then_inc(cve.h, 1)))
            n_v4 = cve.bump()
            wt4 = stx["t4"].get(c - 2, 0)
            p.ve(lambda e, n_v4a=n_v4a, denP=denP, e2P=e2P, p2P=p2P, eqP=eqP,
                        r=r, wt4=wt4: (
                e.wait_ge(cve.h, n_v4a),
                (e.wait_ge(ctt.h, wt4) if wt4 > 0 else None),
                vch.step(e, e.reciprocal(e2P, denP)),
                vch.step(e, e.tensor_scalar(p2P, eqP, e2P[0:128, 0:1],
                                            None, OP.mult)),
                e.tensor_mul(p2P, p2P, r["enzc"]).then_inc(cve.h, 1)))
            stx["v4"][c] = cve.n

        def stage_X2(c, lvl=4):
            P = c % 2
            r = _reg(P)
            _, _, bW, bO = _bank(P)
            csl = slice(128 * c, 128 * (c + 1))
            slot = c % 3
            p2P = p2_sb[:, 64 * P:64 * (P + 1)]
            tpP = tp[P]
            n_t4 = ctt.bump()
            stx["t4"][c] = n_t4
            p.te(lambda e, nv4=stx["v4"][c], p2P=p2P, tpP=tpP: (
                e.wait_ge(cve.h, nv4),
                e.transpose(tpP.ap()[0:64, 512:640], p2P, idn_sb[:])
                .then_inc(ctt.h, 1)))
            n_v5 = cve.bump()
            p.ve(lambda e, n_t4=n_t4, r=r, tpP=tpP: (
                e.wait_ge(ctt.h, n_t4),
                e.tensor_copy(r["p2T"], tpP.ap()[0:64, 512:640]).then_inc(cve.h, 1)))
            stx["v5"][c] = cve.n
            if lvl < 2:
                stx["t6b"][c] = ctt.n
                stx["f3c"][c] = cve.n
                stx["s2"][c] = csc.n
                return csc.n
            n_t5 = ctt.bump()
            p.te(lambda e, n_v5=n_v5, r=r, bW=bW: (
                e.wait_ge(cve.h, n_v5),
                e.matmul(bW, r["esm"], r["p2T"], start=True, stop=True)
                .then_inc(ctt.h, 1)))
            n_v6 = cve.bump()
            p.ve(lambda e, n_t5=n_t5, r=r, bW=bW: (
                e.wait_ge(ctt.h, n_t5),
                e.tensor_mul(r["w2t"], bW, cmW_sb[:]).then_inc(cve.h, 1)))
            if lvl < 3:
                stx["t6b"][c] = ctt.n
                stx["f3c"][c] = cve.n
                stx["s2"][c] = csc.n
                return csc.n
            n_t6b = ctt.bump()
            p.te(lambda e, n_v6=n_v6, r=r, bO=bO, slot=slot: (
                e.wait_ge(cve.h, n_v6),
                e.matmul(bO, r["vc"], r["w2t"], start=True, stop=False),
                e.matmul(bO, SstB2[0:64, 128 * slot:128 * (slot + 1)],
                         r["p2T"], start=False, stop=True).then_inc(ctt.h, 1)))
            stx["t6b"][c] = ctt.n
            if lvl < 4:
                stx["f3c"][c] = cve.n
                stx["s2"][c] = csc.n
                return csc.n
            n_f3c = cve.bump()
            p.ve(lambda e, n_t6b=n_t6b, bO=bO, par=P: (
                e.wait_ge(ctt.h, n_t6b),
                e.tensor_copy(f3c_sb[:, 128 * par:128 * (par + 1)], bO)
                .then_inc(cve.h, 1)))
            stx["f3c"][c] = cve.n
            if lvl < 5:
                stx["s2"][c] = csc.n
                return csc.n
            n_s2 = csc.bump()
            p.sc(lambda e, n_f3c=n_f3c, csl=csl, par=P: (
                e.wait_ge(cve.h, n_f3c),
                e.activation(sq_sb[:, 128 * par:128 * (par + 1)],
                             f3c_sb[:, 128 * par:128 * (par + 1)], AF.Square),
                e.activation(sg2_sb[:, 128 * par:128 * (par + 1)],
                             gT_h.ap()[:, csl], AF.Silu).then_inc(csc.h, 1)))
            stx["s2"][c] = csc.n
            return csc.n

        def emit_epi(ec, s2cnt, hi=hi):
            par = ec % 2
            n_t7 = ctt.bump()
            p.te(lambda e, s2cnt=s2cnt, par=par, w8=ep_state[0]: (
                e.wait_ge(csc.h, s2cnt),
                (e.wait_ge(cve.h, w8) if w8 > 0 else None),
                e.matmul(f[5][0:128, 0:1], sq_sb[:, 128 * par:128 * (par + 1)],
                         ones128[:], start=True, stop=True).then_inc(ctt.h, 1)))
            n_v8a = cve.bump()
            p.ve(lambda e, n_t7=n_t7: (
                e.wait_ge(ctt.h, n_t7),
                e.tensor_scalar(e1_sb[:], f[5][0:128, 0:1], 1.0 / 128.0, EPS,
                                OP.mult, OP.add).then_inc(cve.h, 1)))
            ep_state[0] = cve.n
            n_v8 = cve.bump()
            p.ve(lambda e, n_v8a=n_v8a: (
                e.wait_ge(cve.h, n_v8a),
                e.reciprocal(e3_sb[:], e1_sb[:]).then_inc(cve.h, 1)))
            n_s3 = csc.bump()
            p.sc(lambda e, n_v8=n_v8: (
                e.wait_ge(cve.h, n_v8),
                e.sqrt(rstd_sb[:], e3_sb[:]).then_inc(csc.h, 1)))
            n_t8 = ctt.bump()
            p.te(lambda e, n_s3=n_s3, w9=ep_state[1]: (
                e.wait_ge(csc.h, n_s3),
                (e.wait_ge(cve.h, w9) if w9 > 0 else None),
                e.transpose(f[5].ap()[0:1, 128:256], rstd_sb[:], idn32_sb[:])
                .then_inc(ctt.h, 1)))
            n_v9 = cve.bump()
            p.ve(lambda e, n_t8=n_t8: (
                e.wait_ge(ctt.h, n_t8),
                e.tensor_copy(erow_sb[:], f[5].ap()[0:1, 128:256]).then_inc(cve.h, 1)))
            ep_state[1] = cve.n
            n_t9 = ctt.bump()
            p.te(lambda e, n_v9=n_v9: (
                e.wait_ge(cve.h, n_v9),
                e.matmul(f[5].ap()[0:128, 256:384], one1[:], erow_sb[:],
                         start=True, stop=True).then_inc(ctt.h, 1)))
            n_v10 = cve.bump()
            cpos = ec % 2
            n_stw = cst.n
            p.ve(lambda e, n_t9=n_t9, cpos=cpos, n_stw=n_stw, par=par: (
                e.wait_ge(ctt.h, n_t9),
                ((e.wait_ge(cst.h, n_stw) if n_stw > 0 else None) if cpos == 0 else None),
                vch.step(e, e.tensor_copy(rbc_sb[:], f[5].ap()[0:128, 256:384])),
                vch.step(e, e.tensor_mul(rbc_sb[:],
                                         f3c_sb[:, 128 * par:128 * (par + 1)],
                                         rbc_sb[:])),
                vch.step(e, e.tensor_scalar(rbc_sb[:], rbc_sb[:], gnw_sb[:, 0:1],
                                            None, OP.mult)),
                e.tensor_mul(ct_sb[:, 128 * cpos:128 * (cpos + 1)],
                             rbc_sb[:], sg2_sb[:, 128 * par:128 * (par + 1)])
                .then_inc(cve.h, 1)))
            if cpos == 1:
                ncs = cst.bump(16)
                p.gp(lambda e, hi=hi, ec=ec, nv=cve.n: (
                    e.wait_ge(cve.h, nv),
                    e.dma_start(out=ct_out[128 * hi:128 * (hi + 1),
                                           128 * (ec - 1):128 * (ec + 1)],
                                in_=ct_sb[:, 0:256])
                    .then_inc(cst.h, 16)))

        s2cnt = {}
        for it in range(nchunks + 3):
            if flash_stages >= 3 and 2 <= it and it - 2 < nchunks:
                s2cnt[it - 2] = stage_X2(it - 2, lvl=flash_stages - 2)
            if flash_stages >= 2 and 1 <= it and it - 1 < nchunks:
                stage_X1(it - 1)
            if it < nchunks:
                stage_P(it)
            if flash_stages >= 4 and 3 <= it and it - 3 < nchunks:
                emit_epi(it - 3, s2cnt[it - 3])
        nhb = hbar.bump()
        p.ve(lambda e, nv=cve.n, nhb=nhb: (
            e.wait_ge(cve.h, nv), e.engine_nop().then_inc(hbar.h, 1)))
    p.sy(lambda e, n=cst.n: e.wait_ge(cst.h, n))


def _idn128():
    return np.eye(128, dtype=BF)


def _heads(c):
    j = c % 4
    return [j, j + 4, j + 8, j + 12]


def _shard_x(x):
    return [np.ascontiguousarray(x[c // 4, 2048 * (c % 4):2048 * (c % 4 + 1), :])
            for c in range(8)]


def _scales_128(s):
    return np.ascontiguousarray(np.asarray(s, np.float32).reshape(-1, 128).T)


def _bc128(s):
    s = np.asarray(s, np.float32)
    return np.ascontiguousarray(np.broadcast_to(s[None, :], (128, len(s))))


def _dmask():
    m = np.zeros((128, 4 * 512), dtype=BF)
    for r in range(4):
        for pp in range(128):
            m[pp, 512 * r + pp + 128 * r:512 * (r + 1)] = 1.0
    return m


def _selh():
    s = np.zeros((16, 16 * 128), dtype=BF)
    for h in range(16):
        s[h, 128 * h:128 * (h + 1)] = 1.0
    return s


def _l0_tables(c):
    j = c % 4
    inv = 1.0 / 10000.0 ** (np.arange(0, 2048, 2, dtype=np.float64) / 2048)
    s = np.arange(S, dtype=np.float64)
    cosp = np.empty((128, 2 * S), dtype=BF)
    sinp = np.empty((128, 2 * S), dtype=BF)
    for pi, hA in enumerate((j, j + 4)):
        ang = np.outer(inv[128 * hA:128 * (hA + 1)], s)
        cosp[:, pi * S:(pi + 1) * S] = np.cos(ang).astype(np.float32)
        sinp[:, pi * S:(pi + 1) * S] = np.sin(ang).astype(np.float32)
    return cosp, sinp


def _l0_wrows(W, c):
    j = c % 4
    order = [j, j + 8, j + 4, j + 12]
    return np.ascontiguousarray(np.concatenate(
        [np.asarray(W)[128 * h:128 * (h + 1)] for h in order], axis=0))


def _hrows(W, c):
    return np.ascontiguousarray(np.concatenate(
        [np.asarray(W)[128 * h:128 * (h + 1)] for h in _heads(c)], axis=0))


def _l2_tables():
    inv = 1.0 / 10000.0 ** (np.arange(0, 128, 2, dtype=np.float64) / 128)
    s = np.arange(S, dtype=np.float64)
    ang = np.outer(inv, s)
    cos = np.concatenate([np.cos(ang), np.cos(ang)], axis=0).astype(np.float32)
    sin = np.concatenate([-np.sin(ang), np.sin(ang)], axis=0).astype(np.float32)
    return np.ascontiguousarray(cos.astype(BF)), np.ascontiguousarray(sin.astype(BF))


def _pmat():
    m = np.zeros((128, 128), dtype=BF)
    for col in range(128):
        m[(col + 64) % 128, col] = 1.0
    return m


def _wo_perm(Wo):
    idx = np.empty(2048, np.int64)
    for jc in range(4):
        for hi in range(4):
            h = jc + 4 * hi
            idx[512 * jc + 128 * hi:512 * jc + 128 * (hi + 1)] = \
                np.arange(128 * h, 128 * (h + 1))
    return np.ascontiguousarray(np.asarray(Wo)[:, idx])


_PROGS = {}


def _prog(name, builder):
    if name not in _PROGS:
        _PROGS[name] = builder()
    return _PROGS[name]


def _run(name, builder, in_maps):
    nc = _prog(name, builder)
    return run_bass_kernel_spmd(nc, in_maps, core_ids=list(range(NCORES))).results


def _assemble_lnt(outs):
    return [np.concatenate([np.asarray(outs[4 * b + j]["lnT"]) for j in range(4)],
                           axis=1) for b in range(B)]


def _assemble_ct(outs):
    return [np.concatenate([np.asarray(outs[4 * b + j]["ct"]) for j in range(4)],
                           axis=0) for b in range(B)]


def _attn_sandwich(xs, kind, d):
    idn = _idn128()
    r1 = _run("preln", build_preln_prog,
              [{"x": xs[c], "idn": idn} for c in range(8)])
    lnt_b = _assemble_lnt(r1)
    if kind == "l0":
        in_maps = []
        for c in range(8):
            cosp, sinp = _l0_tables(c)
            gate = np.asarray(d["L0_attn_gate"], np.float32).reshape(16)
            in_maps.append({
                "lnt": np.ascontiguousarray(lnt_b[c // 4]),
                "wq": _l0_wrows(d["L0_attn_Wq"], c),
                "wk": _l0_wrows(d["L0_attn_Wk"], c),
                "wv": _l0_wrows(d["L0_attn_Wv"], c),
                "cosp": cosp, "sinp": sinp,
                "gate": np.ascontiguousarray(gate[_heads(c)][None, :]),
                "idn": idn})
        r2 = _run("attn0", build_attn0_prog, in_maps)
        wo = _wo_perm(d["L0_attn_Wo"])
    elif kind == "l2":
        SCL = 1.0 / np.sqrt(np.float32(128))
        cos_t, sin_t = _l2_tables()
        ws2 = np.asarray(d["L2_attn_Ws2"])
        bs2 = np.asarray(d["L2_attn_bs2"])
        in_maps = []
        for c in range(8):
            hl = _heads(c)
            morder = [hl[0], hl[2], hl[1], hl[3]]
            ws2_st = np.concatenate([ws2[64 * h:64 * (h + 1)] for h in morder], 0)
            bs2p = np.stack(
                [np.concatenate([bs2[64 * morder[0]:64 * morder[0] + 64],
                                 bs2[64 * morder[1]:64 * morder[1] + 64]]),
                 np.concatenate([bs2[64 * morder[2]:64 * morder[2] + 64],
                                 bs2[64 * morder[3]:64 * morder[3] + 64]])],
                axis=1).astype(np.float32)
            in_maps.append({
                "lnt": np.ascontiguousarray(lnt_b[c // 4]),
                "wq": _hrows(d["L2_attn_Wq"], c), "wk": _hrows(d["L2_attn_Wk"], c),
                "wv": _hrows(d["L2_attn_Wv"], c), "wg": _hrows(d["L2_attn_Wg"], c),
                "ws1": np.asarray(d["L2_attn_Ws1"]),
                "ws2": np.ascontiguousarray(ws2_st),
                "bs2p": np.ascontiguousarray(bs2p),
                "gnw": np.asarray(d["L2_attn_g_norm_weight"]).reshape(128, 1).astype(np.float32),
                "cosr": cos_t, "sinr": sin_t, "pmat": _pmat(),
                "cmaskA": np.ascontiguousarray((np.triu(np.ones((128, 128), np.float32)) * SCL).astype(BF)),
                "cmaskW": np.ascontiguousarray(np.triu(np.ones((128, 128), np.float32)).astype(BF)),
                "idn": idn, "idn32": np.eye(128, dtype=np.float32)})
        r2 = _run("attn2", build_attn2_prog, in_maps)
        wo = _wo_perm(d["L2_attn_Wo"])
    else:
        dm = _dmask()
        in_maps = [{"lnt": np.ascontiguousarray(lnt_b[c // 4]),
                    "wq": _hrows(d["L3_attn_Wq"], c),
                    "wk": _hrows(d["L3_attn_Wk"], c),
                    "wv": _hrows(d["L3_attn_Wv"], c),
                    "idn": idn, "dmask": dm} for c in range(8)]
        r2 = _run("attn3", build_attn3_prog, in_maps)
        wo = _wo_perm(d["L3_attn_Wo"])
    ct_b = _assemble_ct(r2)
    r3 = _run("outproj", build_outproj_prog,
              [{"ct": np.ascontiguousarray(ct_b[c // 4][:, 2048 * (c % 4):2048 * (c % 4 + 1)]),
                "wo": wo, "x": xs[c], "idn": idn} for c in range(8)])
    return [np.asarray(r3[c]["xn"]) for c in range(8)]


def _ffn_sandwich(xs, kind, w1, w3, w2, scales=None):
    idn = _idn128()
    ins = {"w1": np.asarray(w1), "w3": np.asarray(w3), "w2": np.asarray(w2),
           "idn": idn}
    if scales is not None:
        ins["gs"] = _scales_128(scales[0])
        ins["us"] = _scales_128(scales[1])
        ins["ds"] = _bc128(scales[2])
    r = _run("ffn_" + kind, lambda: build_ffn_prog(kind),
             [dict(ins, x=xs[c]) for c in range(8)])
    return [np.asarray(r[c]["xn"]) for c in range(8)]


def _l1_sandwich(xs, d):
    idn = _idn128()
    r1 = _run("l1qkv", build_l1qkv_prog,
              [{"x": xs[c], "wqkv": np.asarray(d["L1_attn_weight_qkv"]),
                "idn": idn} for c in range(8)])
    qkvT = [np.asarray(r1[c]["qkvT"]) for c in range(8)]
    sel = _selh()
    wo = np.asarray(d["L1_attn_weight_out"])
    in_maps = []
    for c in range(8):
        part = c % 4
        in_maps.append({
            "qT": qkvT[c][0:2048], "koT": qkvT[c][2048:4096],
            "voT": qkvT[c][4096:6144], "kfT": qkvT[part][2048:4096],
            "vfT": qkvT[part][4096:6144],
            "msk": np.full((128, 1), 0.0 if c < 4 else 1.0, np.float32),
            "selh": sel, "wo": wo, "x": xs[c], "idn": idn})
    r2 = _run("l1comb", build_l1comb_prog, in_maps)
    return [np.asarray(r2[c]["xn"]) for c in range(8)]


def emit_partial_outproj(p, ct_in, wos_in, idn_sb, yp_out, idn_wait):
    wt_sb = p.sb("pow", [128, 4 * D], BF16)
    potp = [p.ps(p._uname("potp"), [128, 1024], BF16) for i in range(2)]
    wdone = emit_wt_prep(p, wos_in, wt_sb, D, 512, idn_sb, p._uname("pwp"),
                         idn_wait=idn_wait, tp_pool=potp)
    nw = wdone.n
    ct_sb = p.sb("poc", [128, 2 * 512], BF16)
    y_sb = p.sb("poy", [128, 2 * D], BF16)
    y_ps = [p.ps("poyp", [128, 512], F32) for _ in range(4)]
    y_ps2 = ([p.ps(p._uname("poyq"), [128, 512], F32) for _ in range(2)]
             + [potp[0].ap()[:, 0:1024].bitcast(F32),
                potp[1].ap()[:, 0:1024].bitcast(F32)])
    clds = [p.sem(), p.sem()]
    mm = p.sem()
    yev = p.sem()
    ysts = [p.swdge_sem(), p.swdge_sem()]
    for tt in range(64):
        buf = tt % 2
        cld = clds[buf]
        yst = ysts[buf]
        ncl = cld.bump(16)
        mmwar = mm.n - 1
        p.sy(lambda e, tt=tt, buf=buf, ncl=ncl, mmwar=mmwar, cld=cld: (
            (e.wait_ge(mm.h, mmwar) if mmwar > 0 else None),
            e.dma_start(out=ct_sb[:, buf * 512:(buf + 1) * 512]
                        .rearrange("p (c t) -> p c t", c=4),
                        in_=ct_in[:, 128 * tt:128 * (tt + 1)]
                        .rearrange("(c p) t -> p c t", p=128)).then_inc(cld.h, 16)))
        nmm = mm.bump()
        psn = y_ps if tt % 2 == 0 else y_ps2
        p.te(lambda e, buf=buf, ncl=ncl, nmm=nmm, nw=nw, tt=tt, yw=yev.n - 1,
                    cld=cld, psn=psn: (
            e.wait_ge(cld.h, ncl),
            (e.wait_ge(wdone.h, nw) if tt == 0 else None),
            (e.wait_ge(yev.h, yw) if yw > 0 else None),
            [e.matmul(psn[n][0:128, 0:512],
                      ct_sb[:, buf * 512 + 128 * c:buf * 512 + 128 * (c + 1)],
                      wt_sb[:, c * D + 512 * n:c * D + 512 * (n + 1)],
                      start=(c == 0), stop=(c == 3))
             for c in range(4) for n in range(4)][-1].then_inc(mm.h, 1)))
        nye = yev.bump()
        p.ve(lambda e, buf=buf, nmm=nmm, stw=yst.n, yst=yst, psn=psn: (
            e.wait_ge(mm.h, nmm),
            (e.wait_ge(yst.h, stw) if stw > 0 else None),
            [e.tensor_copy(y_sb[:, buf * D + 512 * n:buf * D + 512 * (n + 1)],
                           psn[n][0:128, 0:512]) for n in range(4)][-1].then_inc(yev.h, 1)))
        nys = yst.bump(16)
        p.gp(lambda e, tt=tt, buf=buf, nye=nye, yst=yst: (
            e.wait_ge(yev.h, nye),
            e.dma_start(out=yp_out[128 * tt:128 * (tt + 1), :],
                        in_=y_sb[:, buf * D:(buf + 1) * D]).then_inc(yst.h, 16)))
    fin = p.sem()
    nf = fin.bump()
    p.gp(lambda e, n0=ysts[0].n, n1=ysts[1].n, nf=nf: (
        e.wait_ge(ysts[0].h, n0), e.wait_ge(ysts[1].h, n1),
        e.nop().then_inc(fin.h, 1)))
    return fin


def emit_postln_rs(p, ys_in, x_in, x_out, y_wait=None):
    pl = PostLN(p, x_in, x_out, p._uname("plr"))
    y_sb = p.sb("plry", [128, 2 * D], BF16)
    yld = p.sem()
    for j in range(NTT):
        buf = j % 2
        nld = pl.emit_ld(j, gate=(y_wait[:1] if y_wait else None))
        nyl = yld.bump(16)
        ldwar = pl.lnst.ss.n - 2
        p.sy(lambda e, j=j, buf=buf, nyl=nyl, ldwar=ldwar, y_wait=y_wait: (
            ([e.wait_ge(s.h, n) for (s, n) in y_wait]
             if (y_wait and j == 0) else None),
            (e.wait_ge(pl.lnst.ss.h, ldwar) if ldwar > 0 else None),
            e.dma_start(out=y_sb[:, buf * D:(buf + 1) * D],
                        in_=ys_in[128 * j:128 * (j + 1), :]).then_inc(yld.h, 16)))
        srcs = [y_sb[:, buf * D + 512 * n:buf * D + 512 * (n + 1)] for n in range(4)]
        acts = [(y_sb[:, buf * D:(buf + 1) * D], 0, D)]
        pl.emit(j, srcs, acts, (yld, nyl), nld=nld)
    pl.finish()
    return pl.st


def emit_cc(p, kind, op, groups, in_ap, out_ap, waits, block=True):
    cc = p._newsem(p._uname("cc"))
    p.gp(lambda e, waits=waits, block=block: (
        [e.wait_ge(s.h, n) for (s, n) in waits],
        e.collective_compute(kind, op, replica_groups=groups,
                             ins=[in_ap], outs=[out_ap]).then_inc(cc.h),
        (e.wait_ge(cc.h, 1) if block else None)))
    return cc


GROUPS4 = [[0, 1, 2, 3], [4, 5, 6, 7]]
PAIRS = [[0, 4], [1, 5], [2, 6], [3, 7]]


def build_fused_prog(stages=8, sub=6):
    p = Prog("fused", pool_sems=True)
    NHC2 = 44
    xg = p.din("x", [TPC, D], F32)
    idn = p.din("idn", [128, 128], BF16)
    idn32 = p.din("idn32", [128, 128], F32)
    wq0 = p.din("wq0", [512, D], F32)
    wk0 = p.din("wk0", [512, D], F32)
    wv0 = p.din("wv0", [512, D], F32)
    cosp = p.din("cosp", [128, 2 * S], BF16)
    sinp = p.din("sinp", [128, 2 * S], BF16)
    gate0 = p.din("gate0", [1, 4], F32)
    wos0 = p.din("wos0", [D, 512], F32)
    w1_0 = p.din("w1_0", [5461, D], F32)
    w3_0 = p.din("w3_0", [5461, D], F32)
    w2_0 = p.din("w2_0", [D, 5461], F32)
    wqkv1 = p.din("wqkv1", [3 * D, D], F32)
    wo1 = p.din("wo1", [D, D], F32)
    msk1 = p.din("msk1", [128, 1], F32)
    selh1 = p.din("selh1", [16, 16 * 128], BF16)
    w1_1 = p.din("w1_1", [5461, D], F32)
    w3_1 = p.din("w3_1", [5461, D], F32)
    w2_1 = p.din("w2_1", [D, 5461], F32)
    wq2 = p.din("wq2", [512, D], F32)
    wk2 = p.din("wk2", [512, D], F32)
    wv2 = p.din("wv2", [512, D], F32)
    wg2 = p.din("wg2", [512, D], F32)
    ws1 = p.din("ws1", [16, D], F32)
    ws2 = p.din("ws2", [256, 16], F32)
    bs2p = p.din("bs2p", [128, 2], F32)
    gnw = p.din("gnw", [128, 1], F32)
    cosr = p.din("cosr", [128, S], BF16)
    sinr = p.din("sinr", [128, S], BF16)
    pmat = p.din("pmat", [128, 128], BF16)
    cmaskA = p.din("cmaskA", [128, 128], BF16)
    cmaskW = p.din("cmaskW", [128, 128], BF16)
    wos2 = p.din("wos2", [D, 512], F32)
    gw2 = p.din("gw2", [5632, D], F32)
    uw2 = p.din("uw2", [5632, D], F32)
    dw2 = p.din("dw2", [D, 5632], F32)
    gs2 = p.din("gs2", [128, NHC2], F32)
    us2 = p.din("us2", [128, NHC2], F32)
    ds2 = p.din("ds2", [128, D], F32)
    wq3 = p.din("wq3", [512, D], F32)
    wk3 = p.din("wk3", [512, D], F32)
    wv3 = p.din("wv3", [512, D], F32)
    dmask = p.din("dmask", [128, 4 * 512], BF16)
    wos3 = p.din("wos3", [D, 512], F32)
    g3 = p.din("g3", [5632, D], I32)
    u3 = p.din("u3", [5632, D], I32)
    d3 = p.din("d3", [D, 5632], I32)
    gs3 = p.din("gs3", [128, NHC2], F32)
    us3 = p.din("us3", [128, NHC2], F32)
    ds3 = p.din("ds3", [128, D], F32)
    x_out = p.dout("xn", [TPC, D], F32)

    x_int = p.dtmp("x_int", [TPC, D], F32)
    lnt_loc = p.dtmp("lnt_loc", [D, TPC], BF16)
    lnt_ag = p.dtmp("lnt_ag", [4 * D, TPC], BF16)
    ct_loc = p.dtmp("ct_loc", [512, S], BF16)
    yp = p.dtmp("yp", [S, D], BF16)
    ysum = p.dtmp("ysum", [TPC, D], BF16)
    qkv_loc = p.dtmp("qkv_loc", [3 * D, TPC], BF16)
    kv_ag = p.dtmp("kv_ag", [2 * 2 * D, TPC], BF16)

    def lnt_tile_ag(ts):
        r, lts = ts // 4, ts % 4
        return lnt_ag[D * r:D * (r + 1), 512 * lts:512 * (lts + 1)]

    idn_sb = p.sb("idn_sb", [128, 128], BF16)
    x0 = p.sem()
    n0 = x0.bump(16)
    p.sy(lambda e: e.dma_start(out=idn_sb[:], in_=idn).then_inc(x0.h, 16))
    for st in ("vector", "tensor", "scalar", "gpsimd"):
        p.streams[st].append(lambda e, n0=n0: e.wait_ge(x0.h, n0))
    p.phase_barrier()

    def attn_sandwich(emit_ln_attn, wos, x_src=x_int, sub=6):
        def s1():
            p.push_scope()
            lnT_sb = p.sb("lnT_f", [128, NDC * TPC], BF16)
            lndone = emit_preln(p, x_src, lnT_sb, idn_sb, out_dram=lnt_loc,
                                tag=p._uname("pln"))
            s1.nld = (lndone, lndone.n)
            p.pop_scope()

        def s2():
            s2.cc = emit_cc(p, "AllGather", OP.bypass, GROUPS4, lnt_loc,
                            lnt_ag, [s1.nld], block=False)

        def s3():
            p.push_scope()
            emit_ln_attn(w_wait=s1.nld, lnt_wait=(s2.cc, 1))
            p.pop_scope()
            p.phase_barrier()

        def s4():
            p.push_scope()
            fin = emit_partial_outproj(p, ct_loc, wos, idn_sb, yp, None)
            s4.nfin = (fin, fin.n)
            p.pop_scope()

        def s5():
            s5.cc = emit_cc(p, "ReduceScatter", OP.add, GROUPS4, yp, ysum,
                            [s4.nfin], block=False)

        def s6():
            p.push_scope()
            emit_postln_rs(p, ysum, x_src, x_int,
                           y_wait=[s4.nfin, (s5.cc, 1)])
            p.pop_scope()
            p.phase_barrier()

        for f in (s1, s2, s3, s4, s5, s6)[:sub]:
            f()

    def ffn_block(kind, a, b, c, gs=None, us=None, ds=None, out=None):
        p.push_scope()
        emit_ffn(p, kind, x_int, out if out is not None else x_int,
                 a, b, c, idn, gs, us, ds)
        p.pop_scope()
        p.phase_barrier()

    def l1_attn():
        p.push_scope()
        emit_l1qkv(p, x_int, wqkv1, idn, qkv_loc)
        p.pop_scope()
        p.phase_barrier()
        l1fin = p.sem()
        nl1 = l1fin.bump()
        p.sy(lambda e, nl1=nl1: e.nop().then_inc(l1fin.h, 1))
        emit_cc(p, "AllGather", OP.bypass, PAIRS, qkv_loc[D:3 * D, :], kv_ag,
                [(l1fin, nl1)])
        p.phase_barrier()
        p.push_scope()
        emit_l1comb(p, qkv_loc[0:D, :], qkv_loc[D:2 * D, :],
                    qkv_loc[2 * D:3 * D, :], kv_ag[0:D, :], kv_ag[D:2 * D, :],
                    msk1, selh1, wo1, x_int, idn, x_int)
        p.pop_scope()
        p.phase_barrier()

    phases = [
        lambda: attn_sandwich(
            lambda **kw: emit_attn0(p, lnt_tile_ag, wq0, wk0, wv0, cosp,
                                    sinp, gate0, idn, ct_loc, **kw),
            wos0, x_src=xg, sub=sub),
        lambda: ffn_block("tern", w1_0, w3_0, w2_0),
        l1_attn,
        lambda: ffn_block("tern", w1_1, w3_1, w2_1),
        lambda: attn_sandwich(
            lambda **kw: emit_attn2(p, lnt_tile_ag, wq2, wk2, wv2, wg2, ws1,
                                    ws2, bs2p, gnw, cosr, sinr, pmat, cmaskA,
                                    cmaskW, idn, idn32, ct_loc, **kw), wos2),
        lambda: ffn_block("l2", gw2, uw2, dw2, gs2, us2, ds2),
        lambda: attn_sandwich(
            lambda **kw: emit_attn3(p, lnt_tile_ag, wq3, wk3, wv3, idn,
                                    dmask, ct_loc, **kw), wos3),
        lambda: ffn_block("l3", g3, u3, d3, gs3, us3, ds3,
                          out=x_out if stages == 8 else None),
    ]
    for k in range(stages):
        phases[k]()
    if stages < 8:
        xo = p.sem()
        nxo = xo.bump(16)
        p.sy(lambda e, nxo=nxo: (e.dma_start(out=x_out, in_=x_int)
                                 .then_inc(xo.h, 16),
                                 e.wait_ge(xo.h, nxo)))
    return p.finalize()


def _fused_in_maps(d):
    idn = _idn128()
    idn32 = np.eye(128, dtype=np.float32)
    cos_t, sin_t = _l2_tables()
    SCL = 1.0 / np.sqrt(np.float32(128))
    ws2 = np.asarray(d["L2_attn_Ws2"])
    bs2 = np.asarray(d["L2_attn_bs2"])
    x = np.asarray(d["L0_attn_hidden_states"], np.float32)
    xs = _shard_x(x)
    maps = []
    for c in range(8):
        hl = _heads(c)
        cosp, sinp = _l0_tables(c)
        gate = np.asarray(d["L0_attn_gate"], np.float32).reshape(16)
        morder = [hl[0], hl[2], hl[1], hl[3]]
        ws2_st = np.concatenate([ws2[64 * h:64 * (h + 1)] for h in morder], 0)
        bs2p = np.stack(
            [np.concatenate([bs2[64 * morder[0]:64 * morder[0] + 64],
                             bs2[64 * morder[1]:64 * morder[1] + 64]]),
             np.concatenate([bs2[64 * morder[2]:64 * morder[2] + 64],
                             bs2[64 * morder[3]:64 * morder[3] + 64]])],
            axis=1).astype(np.float32)

        def wos(W):
            return np.ascontiguousarray(np.concatenate(
                [np.asarray(W)[:, 128 * h:128 * (h + 1)] for h in hl], axis=1))

        maps.append({
            "x": xs[c], "idn": idn, "idn32": idn32,
            "wq0": _l0_wrows(d["L0_attn_Wq"], c),
            "wk0": _l0_wrows(d["L0_attn_Wk"], c),
            "wv0": _l0_wrows(d["L0_attn_Wv"], c),
            "cosp": cosp, "sinp": sinp,
            "gate0": np.ascontiguousarray(gate[hl][None, :]),
            "wos0": wos(d["L0_attn_Wo"]),
            "w1_0": np.asarray(d["L0_ffn_w1"]), "w3_0": np.asarray(d["L0_ffn_w3"]),
            "w2_0": np.asarray(d["L0_ffn_w2"]),
            "wqkv1": np.asarray(d["L1_attn_weight_qkv"]),
            "wo1": np.asarray(d["L1_attn_weight_out"]),
            "msk1": np.full((128, 1), 0.0 if c < 4 else 1.0, np.float32),
            "selh1": _selh(),
            "w1_1": np.asarray(d["L1_ffn_w1"]), "w3_1": np.asarray(d["L1_ffn_w3"]),
            "w2_1": np.asarray(d["L1_ffn_w2"]),
            "wq2": _hrows(d["L2_attn_Wq"], c), "wk2": _hrows(d["L2_attn_Wk"], c),
            "wv2": _hrows(d["L2_attn_Wv"], c), "wg2": _hrows(d["L2_attn_Wg"], c),
            "ws1": np.asarray(d["L2_attn_Ws1"]),
            "ws2": np.ascontiguousarray(ws2_st),
            "bs2p": np.ascontiguousarray(bs2p),
            "gnw": np.asarray(d["L2_attn_g_norm_weight"]).reshape(128, 1).astype(np.float32),
            "cosr": cos_t, "sinr": sin_t, "pmat": _pmat(),
            "cmaskA": np.ascontiguousarray((np.triu(np.ones((128, 128), np.float32)) * SCL).astype(BF)),
            "cmaskW": np.ascontiguousarray(np.triu(np.ones((128, 128), np.float32)).astype(BF)),
            "wos2": wos(d["L2_attn_Wo"]),
            "gw2": np.asarray(d["L2_ffn_gate_w"]), "uw2": np.asarray(d["L2_ffn_up_w"]),
            "dw2": np.asarray(d["L2_ffn_down_w"]),
            "gs2": _scales_128(d["L2_ffn_gate_s"]), "us2": _scales_128(d["L2_ffn_up_s"]),
            "ds2": _bc128(d["L2_ffn_down_s"]),
            "wq3": _hrows(d["L3_attn_Wq"], c), "wk3": _hrows(d["L3_attn_Wk"], c),
            "wv3": _hrows(d["L3_attn_Wv"], c),
            "dmask": _dmask(), "wos3": wos(d["L3_attn_Wo"]),
            "g3": np.asarray(d["L3_ffn_wq_gate"]), "u3": np.asarray(d["L3_ffn_wq_up"]),
            "d3": np.asarray(d["L3_ffn_wq_down"]),
            "gs3": _scales_128(d["L3_ffn_s_gate"]), "us3": _scales_128(d["L3_ffn_s_up"]),
            "ds3": _bc128(d["L3_ffn_s_down"])})
    return maps


def kernel(**inputs):
    nc = _prog("fused", build_fused_prog)
    in_maps = _fused_in_maps(inputs)
    res = run_bass_kernel_spmd(nc, in_maps, core_ids=list(range(NCORES)))
    out = np.empty((B, S, D), np.float32)
    for c in range(8):
        out[c // 4, 2048 * (c % 4):2048 * (c % 4 + 1), :] = np.asarray(res.results[c]["xn"])
    return out


def kernel_unfused(**inputs):
    d = inputs
    x = np.asarray(d["L0_attn_hidden_states"], np.float32)
    xs = _shard_x(x)
    xs = _attn_sandwich(xs, "l0", d)
    xs = _ffn_sandwich(xs, "tern", d["L0_ffn_w1"], d["L0_ffn_w3"], d["L0_ffn_w2"])
    xs = _l1_sandwich(xs, d)
    xs = _ffn_sandwich(xs, "tern", d["L1_ffn_w1"], d["L1_ffn_w3"], d["L1_ffn_w2"])
    xs = _attn_sandwich(xs, "l2", d)
    xs = _ffn_sandwich(xs, "l2", d["L2_ffn_gate_w"], d["L2_ffn_up_w"],
                       d["L2_ffn_down_w"],
                       (d["L2_ffn_gate_s"], d["L2_ffn_up_s"], d["L2_ffn_down_s"]))
    xs = _attn_sandwich(xs, "l3", d)
    xs = _ffn_sandwich(xs, "l3", d["L3_ffn_wq_gate"], d["L3_ffn_wq_up"],
                       d["L3_ffn_wq_down"],
                       (d["L3_ffn_s_gate"], d["L3_ffn_s_up"], d["L3_ffn_s_down"]))
    out = np.empty((B, S, D), np.float32)
    for c in range(8):
        out[c // 4, 2048 * (c % 4):2048 * (c % 4 + 1), :] = xs[c]
    return out
